# version 11
# baseline (speedup 1.0000x reference)
import numpy as np
import ml_dtypes
import concourse.bass as bass
import concourse.mybir as mybir
from concourse.bass_utils import run_bass_kernel_spmd

F32 = mybir.dt.float32
BF16 = mybir.dt.bfloat16
AF = mybir.ActivationFunctionType
OP = mybir.AluOpType
AX = mybir.AxisListType

NCORES = 8
T = 2048
D = 2048
DT = 16
TTN = 16
TCN = 4
FS = 704
DS3 = 256
SH12 = 2048 * FS
SH3 = 5632 * DS3
AGS = 2 * SH12 + SH3
NSC = float(5632 * 2048 // 64)

NBLOCKS = 4

_LV = np.array([-1.0, -0.6962, -0.5251, -0.3949, -0.2844, -0.1848, -0.0911,
                0.0, 0.0796, 0.1609, 0.2461, 0.3379, 0.4407, 0.5626, 0.723,
                1.0], dtype=np.float32).astype(ml_dtypes.bfloat16).astype(np.float64)
NF4_MIDS = [float(x) for x in (_LV[:-1] + _LV[1:]) / 2.0]
NF4_DELTAS = [float(x) for x in np.diff(_LV)]
NF4_L0 = float(_LV[0])

ENGS = ("sync", "scalar", "vector", "tensor", "gpsimd")

ARENA_OFF = {}


NDCH = 6


class _Op:
    __slots__ = ("eng", "fn", "is_async", "step", "marked", "syncval", "waits",
                 "ridx", "sem", "chval", "pre_waits")

    def __init__(self, eng, fn, is_async, step):
        self.eng = eng
        self.fn = fn
        self.is_async = is_async
        self.step = step
        self.marked = is_async
        self.syncval = None
        self.waits = []
        self.ridx = 0
        self.sem = None
        self.chval = 0
        self.pre_waits = []


class K:
    def __init__(self, nc):
        self.nc = nc
        self.ops = {e: [] for e in ENGS}
        self.lastw = {}
        self.readers = {}
        self._cms = {}
        self._sem_cms = []
        self.sems = {}
        for e in ENGS:
            cm = nc.semaphore("sem_" + e)
            self.sems[e] = cm.__enter__()
            self._sem_cms.append(cm)
        self.dsem = {}
        self.dcnt = {}
        for q in ("sync", "scalar", "gpsimd"):
            self.dsem[q] = []
            self.dcnt[q] = []
            for i in range(NDCH):
                cm = nc.semaphore(f"dma_{q}_{i}")
                self.dsem[q].append(cm.__enter__())
                self._sem_cms.append(cm)
                self.dcnt[q].append(0)
        cm = nc.semaphore("sem_cc")
        self.ccsem = cm.__enter__()
        self._sem_cms.append(cm)
        self.cccnt = 0
        self._barn = 0
        self._ridx = 0
        self._arena = None
        self._astack = []
        self._atop = 0
        self._dmaidx = {q: 0 for q in ("sync", "scalar", "gpsimd")}

    ARENA_B = 92 * 1024

    def alloc(self, name, shape, dtype, psum=False, perm=False):
        if psum or perm:
            cm = (self.nc.psum_tensor if psum else self.nc.sbuf_tensor)(name, list(shape), dtype)
            t = cm.__enter__()
            self._cms[name] = cm
            return t
        if self._arena is None:
            cm = self.nc.sbuf_tensor("ARENA", [128, self.ARENA_B // 2], BF16)
            self._arena = cm.__enter__()
            self._cms["__ARENA"] = cm
        esz = 4 if dtype == F32 else 2
        nel = 1
        for s in shape[1:]:
            nel *= s
        nbytes = nel * esz
        nbytes_al = (nbytes + 3) // 4 * 4
        off = self._atop
        self._atop = off + nbytes_al
        assert self._atop <= self.ARENA_B, (name, self._atop)
        ARENA_OFF[name] = (off, tuple(shape), dtype)
        view = self._arena[0:shape[0], off // 2:(off + nbytes) // 2]
        if dtype != BF16:
            view = view.bitcast(dtype)
        if len(shape) == 3:
            view = view.rearrange("p (a b) -> p a b", a=shape[1])
        elif len(shape) == 4:
            view = view.rearrange("p (a b c) -> p a b c", a=shape[1], b=shape[2])
        self._astack.append((name, off))
        return view

    def free(self, *names):
        for n in names:
            nm, off = self._astack.pop()
            assert nm == n, (nm, n)
            self._atop = off

    def _dep(self, a, b, force=False):
        if a is None or a is b:
            return
        if (not force) and (a.eng == b.eng) and not a.is_async:
            return
        a.marked = True
        b.waits.append(a)

    def rec(self, eng, fn, reads=(), writes=(), is_async=False, step=1, fence=False):
        op = _Op(eng, fn, is_async, step)
        op.ridx = self._ridx
        self._ridx += 1
        for key in reads:
            self._dep(self.lastw.get(key), op, force=fence)
        for key in writes:
            self._dep(self.lastw.get(key), op)
            for r in self.readers.get(key, ()):
                self._dep(r, op)
        for key in reads:
            self.readers.setdefault(key, []).append(op)
        for key in writes:
            self.lastw[key] = op
            self.readers[key] = []
        self.ops[eng].append(op)
        return op

    def _rec_dma(self, queue, fn, reads, writes):
        ch = self._dmaidx[queue] % NDCH
        self._dmaidx[queue] += 1
        op = self.rec(queue, fn, reads, writes, is_async=True, step=16)
        op.sem = self.dsem[queue][ch]
        prev = self.dcnt[queue][ch]
        if prev:
            op.pre_waits.append((op.sem, prev))
        self.dcnt[queue][ch] = prev + 16
        op.chval = prev + 16
        return op

    def barrier(self, wait_cc=True):
        n = self._barn
        self._barn += 1
        for e in ENGS:
            op = self.rec(e, lambda eng: eng.drain(), writes=[f"__bar{n}_{e}"])
            op.marked = True
            if e in self.dsem:
                for i in range(NDCH):
                    if self.dcnt[e][i]:
                        op.pre_waits.append((self.dsem[e][i], self.dcnt[e][i]))
            if e == "gpsimd" and self.cccnt and wait_cc:
                op.pre_waits.append((self.ccsem, self.cccnt))
        for e in ENGS:
            self.rec(e, lambda eng: eng.nop(),
                     reads=[f"__bar{n}_{f}" for f in ENGS if f != e])

    def dma(self, queue, out, in_, reads=(), writes=(), slow=False):
        def fn(eng):
            return eng.dma_start(out=out, in_=in_, allow_slow_non_contiguous=slow)
        return self._rec_dma(queue, fn, reads, writes)

    def dmat(self, queue, out, in_, reads=(), writes=()):
        def fn(eng):
            return eng.dma_start_transpose(out, in_)
        return self._rec_dma(queue, fn, reads, writes)

    def cc(self, fn, reads, writes):
        op = self.rec("gpsimd", fn, reads, writes, is_async=True, step=1)
        op.sem = self.ccsem
        self.cccnt += 1
        op.chval = self.cccnt
        return op

    def mm(self, out_ap, pairs, reads, writes, start0=True, stop=True):
        def fn(eng):
            last = None
            n = len(pairs)
            for i, (lhsT, rhs) in enumerate(pairs):
                last = eng.matmul(out_ap, lhsT, rhs,
                                  start=(start0 and i == 0),
                                  stop=(stop and i == n - 1))
            return last
        return self.rec("tensor", fn, reads, writes)

    def tp(self, out_ap, in_ap, ident, reads, writes):
        def fn(eng):
            return eng.transpose(out_ap, in_ap, ident)
        return self.rec("tensor", fn, reads, writes)

    def act(self, out_ap, in_ap, func, reads, writes, bias=0.0, scale=1.0, fence=False):
        def fn(eng):
            return eng.activation(out_ap, in_ap, func, bias=bias, scale=scale)
        return self.rec("scalar", fn, reads, writes, fence=fence)

    def v(self, fn, reads, writes, fence=False):
        return self.rec("vector", fn, reads, writes, fence=fence)

    def finalize(self):
        for e in ENGS:
            c = 0
            for op in self.ops[e]:
                if op.marked and not op.is_async:
                    c += op.step
                    op.syncval = c
                    op.sem = self.sems[e]
                    op.chval = c

        def mk(e):
            def body(eng):
                hi = {}
                for op in self.ops[e]:
                    tgt = {}
                    for a in op.waits:
                        key = a.sem.name
                        if a.chval > tgt.get(key, (None, 0))[1]:
                            tgt[key] = (a.sem, a.chval)
                    for key, (s, v) in tgt.items():
                        if hi.get(key, 0) < v:
                            hi[key] = v
                            eng.wait_ge(s, v)
                    for (s, v) in op.pre_waits:
                        if hi.get(s.name, 0) < v:
                            hi[s.name] = v
                            eng.wait_ge(s, v)
                    inst = op.fn(eng)
                    if op.marked:
                        inst.then_inc(op.sem, op.step)
            return body
        with self.nc.Block() as block:
            block.sync(mk("sync"))
            block.scalar(mk("scalar"))
            block.vector(mk("vector"))
            block.tensor(mk("tensor"))
            block.gpsimd(mk("gpsimd"))
        for name in list(self._cms.keys())[::-1]:
            self._cms.pop(name).__exit__(None, None, None)
        for cm in self._sem_cms[::-1]:
            cm.__exit__(None, None, None)


def build_program():
    nc = bass.Bass("TRN2", target_bir_lowering=False, debug=False)
    nc.detect_race_conditions = False
    k = K(nc)

    def din(name, shape, dtype=F32):
        return nc.dram_tensor(name, list(shape), dtype, kind="ExternalInput").ap()

    xin = din("xT_in", (D, T))
    Wqk = din("Wqk", (D, D)); bqk = din("bqk", (D,))
    Wv0 = din("Wv0", (D, D)); bv0 = din("bv0", (D,))
    Wu0 = din("Wu0", (D, D)); bu0 = din("bu0", (D,))
    Wo0 = din("Wo0", (D, D)); bo0 = din("bo0", (D,))
    ln_g0 = din("ln_g0", (D,)); ln_b0 = din("ln_b0", (D,))
    convw = din("convw", (D, 2))
    Wq1 = din("Wq1", (D, 1024)); Wk1 = din("Wk1", (D, 1024))
    Wv1 = din("Wv1", (D, D)); Wg1 = din("Wg1", (D, D))
    routw = din("routw", (16, 128, 2))
    Wout1 = din("Wout1", (D, D))
    ln_g2 = din("ln_g2", (D,)); ln_b2 = din("ln_b2", (D,))
    W_in = din("W_in", (D, 8)); b_in = din("b_in", (8,))
    dec_l = din("dec_l", (8,))
    W_out2 = din("W_out2", (8, D)); b_out2 = din("b_out2", (D,))
    D_skip = din("D_skip", (D,))
    W_down = din("W_down", (D, 32)); b_down = din("b_down", (32,))
    W_up = din("W_up", (32, D)); b_up = din("b_up", (D,))
    w1s = din("w1s", (FS, 2048), BF16)
    w2s = din("w2s", (FS, 2048), BF16)
    w3s = din("w3s", (DS3, 5632), BF16)
    ident_bf = din("ident_bf", (128, 128), BF16)
    ident_f = din("ident_f", (128, 128))
    ones_in = din("ones_in", (128,))
    ones_in_bf = din("ones_in_bf", (128,), BF16)
    g2pat = din("g2pat", (128, 2), BF16)
    nf4negmids = din("nf4negmids", (16,))
    hselin = din("hselin", (128, 16, 16), BF16)
    carry_m = din("carry_m", (8,))

    outT = nc.dram_tensor("outT", [D, T], F32, kind="ExternalOutput").ap()

    kv_in = nc.dram_tensor("kv_in", [128, 2048], F32).ap()
    kv_out = nc.dram_tensor("kv_out", [128, 2048], F32).ap()
    mean_in = nc.dram_tensor("mean_in", [8], F32).ap()
    mean_out = nc.dram_tensor("mean_out", [8], F32).ap()
    E_in = nc.dram_tensor("E_in", [8], F32).ap()
    E_out = nc.dram_tensor("E_out", [64], F32).ap()
    rowb = nc.dram_tensor("rowb", [2, 1024], F32).ap()
    hrowb = nc.dram_tensor("hrowb", [16, 1024], F32).ap()
    s1fl = nc.dram_tensor("s1fl", [3, FS * 32], F32).ap()
    accd = nc.dram_tensor("accd", [3, 128], F32).ap()
    deqfl = nc.dram_tensor("deqfl", [3, FS * 32], BF16).ap()
    ag_in = nc.dram_tensor("ag_in", [AGS], BF16).ap()
    ag_out = nc.dram_tensor("ag_out", [8 * AGS], BF16).ap()

    xT = k.alloc("xT", [128, DT, T], BF16, perm=True)
    cols = k.alloc("cols", [128, 16, 16], F32, perm=True)
    COL = {}
    for i, n in enumerate(["bqk", "bv0", "bu0", "bo0", "ln_g0", "ln_b0", "cw1",
                           "ln_g2", "ln_b2", "b_out2", "D_skip", "b_up"]):
        COL[n] = i
    smalls = k.alloc("smalls", [32, 4], F32, perm=True)
    identb = k.alloc("identb", [128, 128], BF16, perm=True)
    identf = k.alloc("identf", [128, 128], F32, perm=True)
    onesc = k.alloc("onesc", [128, 1], F32, perm=True)
    onescb = k.alloc("onescb", [128, 1], BF16, perm=True)
    g2p = k.alloc("g2p", [128, 2], BF16, perm=True)
    hsel = k.alloc("hsel", [128, 16, 16], BF16, perm=True)
    tmpa = k.alloc("tmpa", [128, 512], F32, perm=True)
    tmpb = k.alloc("tmpb", [128, 512], F32, perm=True)
    tmpc = k.alloc("tmpc", [128, 512], F32, perm=True)
    sq2 = k.alloc("sq2", [128, 2, 512], F32, perm=True)
    rowbuf = k.alloc("rowbuf", [1, 2, 1024], F32, perm=True)
    bcast = k.alloc("bcast", [128, 2, 1024], F32, perm=True)
    tbf = k.alloc("tbf", [128, 4, 512], BF16, perm=True)

    pKV = k.alloc("pKV", [128, 2048], F32, psum=True)
    pA = k.alloc("pA", [128, 512], F32, psum=True)
    pB = k.alloc("pB", [128, 512], F32, psum=True)
    pC = k.alloc("pC", [128, 512], F32, psum=True)
    pD = k.alloc("pD", [128, 512], F32, psum=True)

    k.dma("sync", identb[:, :], ident_bf, writes=["identb"])
    k.dma("sync", identf[:, :], ident_f, writes=["identf"])
    k.dma("sync", onesc[:, :], ones_in.rearrange("(p one) -> p one", one=1), writes=["onesc"])
    k.dma("sync", onescb[:, :], ones_in_bf.rearrange("(p one) -> p one", one=1), writes=["onescb"])
    k.dma("sync", g2p[:, :], g2pat, writes=["g2p"])
    k.dma("sync", hsel[:, :, :], hselin, writes=["hsel"])
    for n, ap_ in [("bqk", bqk), ("bv0", bv0), ("bu0", bu0), ("bo0", bo0),
                   ("ln_g0", ln_g0), ("ln_b0", ln_b0), ("ln_g2", ln_g2),
                   ("ln_b2", ln_b2), ("b_out2", b_out2), ("D_skip", D_skip),
                   ("b_up", b_up)]:
        k.dma("sync", cols[:, COL[n], :], ap_.rearrange("(j p) -> p j", p=128),
              writes=[f"col_{n}"], slow=True)
    k.dma("sync", cols[:, COL["cw1"], :],
          convw[:, 1:2].rearrange("(j p) one -> p (j one)", p=128), writes=["col_cw1"],
          slow=True)
    k.dma("sync", smalls[0:8, 0:1], b_in.rearrange("(r one) -> r one", one=1), writes=["smalls0"])
    k.dma("sync", smalls[0:32, 1:2], b_down.rearrange("(r one) -> r one", one=1), writes=["smalls1"])
    for j in range(DT):
        k.dma("gpsimd", xT[:, j, :], xin[j * 128:(j + 1) * 128, :], writes=[f"xT{j}"])

    rowslot = [0]

    def ln_stats(src_fn, src_keys_fn, n_d=DT, denom=2048.0, bf=False):
        ones_ap = onescb if bf else onesc
        ones_key = "onescb" if bf else "onesc"
        slot = rowslot[0] % 2
        rowslot[0] += 1
        rb = rowbuf[0:1, slot, :]
        rk = f"rowbuf{slot}"
        for dj in range(n_d):
            src = src_fn(dj)
            sqs = dj % 2
            k.v(lambda eng, src=src, sqs=sqs: eng.tensor_tensor(sq2[:, sqs, :], src, src, OP.mult),
                reads=src_keys_fn(dj), writes=[f"sq{sqs}"])
            k.mm(pA[0:1, :], [(ones_ap[:, :], src)],
                 reads=[ones_key] + src_keys_fn(dj), writes=["pA"],
                 start0=(dj == 0), stop=(dj == n_d - 1))
            k.mm(pB[0:1, :], [(onesc[:, :], sq2[:, sqs, :])],
                 reads=["onesc", f"sq{sqs}"], writes=["pB"],
                 start0=(dj == 0), stop=(dj == n_d - 1))
        k.v(lambda eng: eng.tensor_scalar(rb[:, 0:512], pA[0:1, :], 1.0 / denom, None, OP.mult),
            reads=["pA"], writes=[rk])
        k.v(lambda eng: eng.tensor_scalar(tmpa[0:1, 0:512], pB[0:1, :], 1.0 / denom, None, OP.mult),
            reads=["pB"], writes=["tmpa"])
        k.v(lambda eng: eng.tensor_tensor(tmpb[0:1, 0:512], rb[:, 0:512], rb[:, 0:512], OP.mult),
            reads=[rk], writes=["tmpb"])
        k.v(lambda eng: eng.scalar_tensor_tensor(tmpc[0:1, 0:512], tmpb[0:1, 0:512], -1.0,
                                                 tmpa[0:1, 0:512], OP.mult, OP.add),
            reads=["tmpa", "tmpb"], writes=["tmpc"])
        k.v(lambda eng: eng.tensor_scalar(tmpc[0:1, 0:512], tmpc[0:1, 0:512], 1e-5, None, OP.add),
            reads=["tmpc"], writes=["tmpc"])
        k.v(lambda eng: eng.reciprocal(tmpa[0:1, 0:512], tmpc[0:1, 0:512]),
            reads=["tmpc"], writes=["tmpa"])
        k.act(rb[:, 512:1024], tmpa[0:1, 0:512], AF.Sqrt, reads=["tmpa"], writes=[rk])
        k.v(lambda eng: eng.tensor_tensor(tmpa[0:1, 0:512], rb[:, 512:1024], rb[:, 512:1024], OP.mult),
            reads=[rk], writes=["tmpa"])
        k.v(lambda eng: eng.tensor_tensor(tmpa[0:1, 0:512], tmpa[0:1, 0:512], tmpc[0:1, 0:512], OP.mult),
            reads=["tmpa", "tmpc"], writes=["tmpa"])
        k.v(lambda eng: eng.tensor_scalar(tmpa[0:1, 0:512], tmpa[0:1, 0:512], -0.5, 1.5, OP.mult, OP.add),
            reads=["tmpa"], writes=["tmpa"])
        k.v(lambda eng: eng.tensor_tensor(rb[:, 512:1024], rb[:, 512:1024], tmpa[0:1, 0:512], OP.mult),
            reads=["tmpa", rk], writes=[rk])
        k.dma("sync", rowb[slot, :], rb[:, :], reads=[rk], writes=[f"rbd{slot}"])
        bc = bcast[:, slot, :]
        bkey = f"bcast{slot}"
        k.dma("sync", bc, rowb[slot:slot + 1, :].partition_broadcast(128),
              reads=[f"rbd{slot}"], writes=[bkey])
        return bc, bkey

    def early_tap():
        k.barrier()
        for dj in range(DT):
            k.dma("gpsimd", outT[dj * 128:(dj + 1) * 128, :], xT[:, dj, :],
                  reads=[f"xT{dj}"], writes=["outd"])
        k.rec("gpsimd", lambda eng: eng.nop(), reads=["outd"])
        k.finalize()
        return nc


    P4TAP = []
    _p4 = {"it": None}
    meand = nc.dram_tensor("meand", [8], F32).ap()

    def setup_p4():
        wnat = k.alloc("wnat", [128, 4096], BF16)
        s1w = k.alloc("s1w", [128, 6, 32], F32)
        s3w = k.alloc("s3w", [128, 2, 88], F32)
        meansb = k.alloc("meansb", [1, 8], F32)
        acc = k.alloc("acc", [128, 4], F32)
        sblk = k.alloc("sblk", [88, 256], F32)
        scbf = k.alloc("scbf", [88, 2, 256], BF16)
        scq = k.alloc("scq", [88, 256], F32)
        sccol = k.alloc("sccol", [88, 4], F32)
        meanc = k.alloc("meanc", [88, 1], F32)
        recn = k.alloc("recn", [128, 6, 32], F32)
        deqn = k.alloc("deqn", [128, 6, 32], BF16)
        rec3 = k.alloc("rec3", [128, 2, 88], F32)
        deq3 = k.alloc("deq3", [128, 2, 88], BF16)
        scb2 = k.alloc("scb2", [128, 2048], BF16)
        stair = k.alloc("stair", [128, 2048], F32)
        wqT = k.alloc("wqT", [128, 2, 128], BF16)
        scb2f = k.alloc("scb2f", [128, 128], F32)
        p4a = k.alloc("p4a", [128, 512], F32)
        p4s = k.alloc("p4s", [128, 2, 2048], BF16)
        p4m = k.alloc("p4m", [128, 16], F32)
        k.dma("sync", p4m[:, :], nf4negmids.unsqueeze(0).partition_broadcast(128), writes=["p4m"])

        def staircase(src_ap, rows, cols_, rkey):
            C0 = NF4_L0 + sum(NF4_DELTAS) / 2.0
            for i, (bmid, dlt) in enumerate(zip(NF4_MIDS, NF4_DELTAS)):
                sb = i % 2
                k.act(p4s[0:rows, sb, 0:cols_], src_ap, AF.Sign, bias=p4m[0:rows, i:i + 1],
                      reads=[rkey, "p4m"], writes=[f"p4s{sb}"])
                if i == 0:
                    k.v(lambda eng, rows=rows, cols_=cols_, dlt=dlt, sb=sb, C0=C0: eng.tensor_scalar(
                        stair[0:rows, 0:cols_], p4s[0:rows, sb, 0:cols_], dlt / 2.0, C0,
                        OP.mult, OP.add),
                        reads=[f"p4s{sb}"], writes=["stair"])
                else:
                    k.v(lambda eng, rows=rows, cols_=cols_, dlt=dlt, sb=sb: eng.scalar_tensor_tensor(
                        stair[0:rows, 0:cols_], p4s[0:rows, sb, 0:cols_], dlt / 2.0,
                        stair[0:rows, 0:cols_], OP.mult, OP.add),
                        reads=[f"p4s{sb}", "stair"], writes=["stair"])

        def gen():
            for wi, wap in [(0, w1s), (1, w2s)]:
                for tj in range(6):
                    rows = min(128, FS - tj * 128)
                    k.dma("sync", wnat[0:rows, 0:2048], wap[tj * 128:tj * 128 + rows, :], writes=["wnat"])
                    k.v(lambda eng, tj=tj, rows=rows: eng.tensor_reduce(
                        s1w[0:rows, tj, :], wnat[0:rows, 0:2048].rearrange("p (g e) -> p g e", e=64),
                        AX.X, OP.max, apply_absolute_value=True),
                        reads=["wnat"], writes=[f"s1w{tj}"])
                    k.v(lambda eng, tj=tj, rows=rows: eng.tensor_reduce(
                        p4a[0:rows, 0:1], s1w[0:rows, tj, :], AX.X, OP.add),
                        reads=[f"s1w{tj}"], writes=["p4a"], fence=True)
                    if rows < 128:
                        k.v(lambda eng, rows=rows: eng.tensor_scalar(
                            p4a[rows:128, 0:1], onesc[rows:128, :], 0.0, None, OP.mult),
                            reads=["onesc"], writes=["p4a"])
                    if tj == 0:
                        k.v(lambda eng, wi=wi: eng.tensor_copy(acc[:, wi:wi + 1], p4a[:, 0:1]),
                            reads=["p4a"], writes=[f"acc{wi}"], fence=True)
                    else:
                        k.v(lambda eng, wi=wi: eng.tensor_tensor(acc[:, wi:wi + 1], acc[:, wi:wi + 1],
                                                                 p4a[:, 0:1], OP.add),
                            reads=[f"acc{wi}", "p4a"], writes=[f"acc{wi}"], fence=True)
                    k.dma("sync", s1fl[wi, tj * 128 * 32:(tj * 128 + rows) * 32].rearrange("(p g) -> p g", g=32),
                          s1w[0:rows, tj, :], reads=[f"s1w{tj}"], writes=[f"s1fl{wi}"])
                    yield
            for tj in range(2):
                for h2 in range(2):
                    k.dma("sync", wnat[:, 0:2816],
                          w3s[tj * 128:(tj + 1) * 128, h2 * 2816:(h2 + 1) * 2816], writes=["wnat"])
                    k.v(lambda eng, tj=tj, h2=h2: eng.tensor_reduce(
                        s3w[:, tj, h2 * 44:(h2 + 1) * 44],
                        wnat[:, 0:2816].rearrange("p (g e) -> p g e", e=64),
                        AX.X, OP.max, apply_absolute_value=True),
                        reads=["wnat"], writes=[f"s3w{tj}"])
                k.v(lambda eng, tj=tj: eng.tensor_reduce(p4a[:, 0:1], s3w[:, tj, :], AX.X, OP.add),
                    reads=[f"s3w{tj}"], writes=["p4a"], fence=True)
                if tj == 0:
                    k.v(lambda eng: eng.tensor_copy(acc[:, 2:3], p4a[:, 0:1]),
                        reads=["p4a"], writes=["acc2"], fence=True)
                else:
                    k.v(lambda eng: eng.tensor_tensor(acc[:, 2:3], acc[:, 2:3], p4a[:, 0:1], OP.add),
                        reads=["acc2", "p4a"], writes=["acc2"], fence=True)
                k.dma("sync", s1fl[2, tj * 128 * 88:(tj + 1) * 128 * 88].rearrange("(p g) -> p g", g=88),
                      s3w[:, tj, :], reads=[f"s3w{tj}"], writes=["s1fl2"])
                yield
            for w in range(3):
                k.dma("sync", accd[w, :], acc[:, w:w + 1], reads=[f"acc{w}"], writes=["accd"], slow=True)
            k.dma("sync", p4a[0:1, 0:384], accd.rearrange("w p -> (w p)").rearrange("(one n) -> one n", one=1),
                  reads=["accd"], writes=["p4a"])
            k.v(lambda eng: eng.tensor_reduce(meansb[0:1, 0:3],
                                              p4a[0:1, 0:384].rearrange("p (w n) -> p w n", w=3),
                                              AX.X, OP.add),
                reads=["p4a"], writes=["meansb"])
            k.v(lambda eng: eng.tensor_scalar(meansb[0:1, 3:8], p4a[0:1, 0:5], 0.0, None, OP.mult),
                reads=["p4a"], writes=["meansb"])
            k.dma("sync", mean_in, meansb[0:1, :], reads=["meansb"], writes=["mean_in_d"])
            k.cc(lambda eng: eng.collective_compute(
                "AllReduce", OP.add, replica_groups=[list(range(NCORES))],
                ins=[mean_in.opt()], outs=[mean_out.opt()]),
                reads=["mean_in_d"], writes=["mean_out_d"])
            k.dma("sync", meansb[0:1, :], mean_out, reads=["mean_out_d"], writes=["meansb"])
            k.v(lambda eng: eng.tensor_scalar(p4s[0:1, 0, 0:8], meansb[0:1, :], 1.0 / NSC, None, OP.mult),
                reads=["meansb"], writes=["p4s0"])
            k.v(lambda eng: eng.tensor_copy(meansb[0:1, :], p4s[0:1, 0, 0:8]), reads=["p4s0"], writes=["meansb"],
                fence=True)
            k.dma("sync", meand, meansb[0:1, :], reads=["meansb"], writes=["meand"])
            yield
            for wi in range(3):
                k.dma("sync", sblk[:, :], s1fl[wi, 0:22528].rearrange("(b j) -> b j", j=256),
                      reads=[f"s1fl{wi}"], writes=["sblk"])
                k.dma("sync", meanc[:, :], meand[wi:wi + 1].unsqueeze(0).partition_broadcast(88),
                      reads=["meand"], writes=["meanc"])
                k.v(lambda eng: eng.tensor_scalar(scbf[:, 0, :], sblk[:, :], meanc[:, 0:1], None, OP.subtract),
                    reads=["sblk", "meanc"], writes=["scbf"])
                k.v(lambda eng: eng.tensor_reduce(sccol[:, 0:1], scbf[:, 0, :], AX.X, OP.max,
                                                  apply_absolute_value=True),
                    reads=["scbf"], writes=["sccol"], fence=True)
                k.v(lambda eng: eng.reciprocal(sccol[:, 3:4], sccol[:, 0:1]), reads=["sccol"], writes=["sccol"],
                    fence=True)
                k.v(lambda eng: eng.tensor_scalar(scbf[:, 1, 0:1], sccol[:, 3:4], 128.0, None, OP.mult),
                    reads=["sccol"], writes=["scbf"], fence=True)
                k.v(lambda eng: eng.tensor_copy(sccol[:, 1:2], scbf[:, 1, 0:1]), reads=["scbf"], writes=["sccol"],
                    fence=True)
                k.v(lambda eng: eng.reciprocal(sccol[:, 2:3], sccol[:, 1:2]), reads=["sccol"], writes=["sccol"],
                    fence=True)
                k.v(lambda eng: eng.tensor_scalar(scbf[:, 0, :], scbf[:, 0, :], sccol[:, 1:2], None, OP.mult),
                    reads=["scbf", "sccol"], writes=["scbf"], fence=True)
                k.v(lambda eng: eng.tensor_copy(scq[:, :], scbf[:, 0, :]), reads=["scbf"], writes=["scq"],
                    fence=True)
                k.v(lambda eng: eng.tensor_scalar(scq[:, :], scq[:, :], 1.5 * 2.0 ** 23, 1.5 * 2.0 ** 23,
                                                  OP.add, OP.subtract),
                    reads=["scq"], writes=["scq"], fence=True)
                k.v(lambda eng: eng.tensor_scalar(scq[:, :], scq[:, :], 127.0, -128.0, OP.min, OP.max),
                    reads=["scq"], writes=["scq"], fence=True)
                k.v(lambda eng: eng.tensor_scalar(scbf[:, 1, :], scq[:, :], sccol[:, 2:3], None, OP.mult),
                    reads=["scq", "sccol"], writes=["scbf"], fence=True)
                k.v(lambda eng: eng.tensor_scalar(scbf[:, 1, :], scbf[:, 1, :], meanc[:, 0:1], None, OP.add),
                    reads=["scbf", "meanc"], writes=["scbf"], fence=True)
                k.dma("sync", deqfl[wi, 0:22528].rearrange("(b j) -> b j", j=256), scbf[:, 1, :],
                      reads=["scbf"], writes=[f"deqfl{wi}"])
                if NBLOCKS == 3.15 and wi == 0:
                    k.barrier()
                    k.dma("sync", outT[100:188, 0:256], sblk[:, :], reads=["sblk"], writes=["outd"])
                    k.dma("gpsimd", outT[200:288, 0:256], scbf[:, 0, :], reads=["scbf"], writes=["outd"])
                    k.dma("sync", outT[300:388, 0:256], scq[:, :], reads=["scq"], writes=["outd"])
                    k.dma("sync", outT[400:488, 0:4], sccol[:, :], reads=["sccol"], writes=["outd"], slow=True)
                    k.dma("sync", outT[500:588, 0:1], meanc[:, :], reads=["meanc"], writes=["outd"], slow=True)
                    k.dma("gpsimd", outT[600:688, 0:256], scbf[:, 1, :], reads=["scbf"], writes=["outd"])
                    k.rec("sync", lambda eng: eng.nop(), reads=["outd"])
                    k.finalize()
                    P4TAP.append(nc)
                    return
                yield
            if NBLOCKS == 3.2:
                k.barrier()
                for w in range(3):
                    k.dma("sync", outT[11 * w:11 * w + 11, :].rearrange("a b -> (a b)")[0:22528],
                          s1fl[w, 0:22528], reads=[f"s1fl{w}"], writes=["outd"])
                    k.dma("gpsimd", outT[33 + 11 * w:44 + 11 * w, :].rearrange("a b -> (a b)")[0:22528],
                          deqfl[w, 0:22528], reads=[f"deqfl{w}"], writes=["outd"])
                k.rec("sync", lambda eng: eng.nop(), reads=["outd"])
                k.finalize()
                P4TAP.append(nc)
                return
            for wi, wap in [(0, w1s), (1, w2s)]:
                for tj in range(6):
                    rows = min(128, FS - tj * 128)
                    k.dma("sync", s1w[0:rows, tj, :],
                          s1fl[wi, tj * 128 * 32:(tj * 128 + rows) * 32].rearrange("(p g) -> p g", g=32),
                          reads=[f"s1fl{wi}"], writes=[f"s1w{tj}"])
                k.v(lambda eng: eng.tensor_scalar(s1w[64:128, 5, :], identb[64:128, 0:32], 0.0, 1.0,
                                                  OP.mult, OP.add),
                    reads=["identb"], writes=["s1w5"])
                k.v(lambda eng: eng.reciprocal(recn[:, :, :], s1w[:, :, :]),
                    reads=[f"s1w{t}" for t in range(6)], writes=["recn"], fence=True)
                for tj in range(6):
                    rows = min(128, FS - tj * 128)
                    k.dma("sync", deqn[0:rows, tj, :],
                          deqfl[wi, tj * 128 * 32:(tj * 128 + rows) * 32].rearrange("(p g) -> p g", g=32),
                          reads=[f"deqfl{wi}"], writes=["deqn"])
                yield
                for tj in range(6):
                    rows = min(128, FS - tj * 128)
                    k.dma("sync", wnat[0:rows, 0:2048], wap[tj * 128:tj * 128 + rows, :], writes=["wnat"])
                    k.v(lambda eng, tj=tj, rows=rows: eng.tensor_tensor(
                        scb2[0:rows, :].rearrange("p (g e) -> p g e", e=64),
                        wnat[0:rows, 0:2048].rearrange("p (g e) -> p g e", e=64),
                        recn[0:rows, tj, :].unsqueeze(2).broadcast_to((rows, 32, 64)), OP.mult),
                        reads=["wnat", "recn"], writes=["scb2"], fence=True)
                    staircase(scb2[0:rows, 0:2048], rows, 2048, "scb2")
                    k.v(lambda eng, tj=tj, rows=rows: eng.tensor_tensor(
                        scb2[0:rows, :].rearrange("p (g e) -> p g e", e=64),
                        stair[0:rows, 0:2048].rearrange("p (g e) -> p g e", e=64),
                        deqn[0:rows, tj, :].unsqueeze(2).broadcast_to((rows, 32, 64)), OP.mult),
                        reads=["stair", "deqn"], writes=["scb2"])
                    yield
                    for dj in range(16):
                        ws = dj % 2
                        k.v(lambda eng, dj=dj, rows=rows: eng.tensor_copy(
                            scb2f[0:rows, :], scb2[0:rows, dj * 128:(dj + 1) * 128]),
                            reads=["scb2"], writes=["scb2f"])
                        k.tp(pC[:, 0:rows], scb2f[0:rows, :],
                             identf[0:rows, 0:rows], reads=["scb2f", "identf"], writes=["pC"])
                        k.v(lambda eng, ws=ws, rows=rows: eng.tensor_copy(wqT[:, ws, 0:rows], pC[:, 0:rows]),
                            reads=["pC"], writes=[f"wqT{ws}"])
                        k.dma("sync",
                              ag_in[wi * SH12:(wi + 1) * SH12].rearrange("(d f) -> d f", f=FS)
                              [dj * 128:(dj + 1) * 128, tj * 128:tj * 128 + rows],
                              wqT[:, ws, 0:rows], reads=[f"wqT{ws}"], writes=["ag_in_d"])
                    yield
            k.v(lambda eng: eng.reciprocal(rec3[:, :, :], s3w[:, :, :]),
                reads=["s3w0", "s3w1"], writes=["rec3"], fence=True)
            for tj in range(2):
                k.dma("sync", deq3[:, tj, :],
                      deqfl[2, tj * 128 * 88:(tj + 1) * 128 * 88].rearrange("(p g) -> p g", g=88),
                      reads=["deqfl2"], writes=["deq3"])
            yield
            for tj in range(2):
                for cc3i in range(3):
                    c0 = cc3i * 2048
                    cw = min(2048, 5632 - c0)
                    g0 = c0 // 64
                    gw = cw // 64
                    k.dma("sync", wnat[:, 0:cw], w3s[tj * 128:(tj + 1) * 128, c0:c0 + cw],
                          writes=["wnat"])
                    k.v(lambda eng, tj=tj, c0=c0, cw=cw, g0=g0, gw=gw: eng.tensor_tensor(
                        scb2[:, 0:cw].rearrange("p (g e) -> p g e", e=64),
                        wnat[:, 0:cw].rearrange("p (g e) -> p g e", e=64),
                        rec3[:, tj, g0:g0 + gw].unsqueeze(2).broadcast_to((128, gw, 64)), OP.mult),
                        reads=["wnat", "rec3"], writes=["scb2"], fence=True)
                    staircase(scb2[0:128, 0:cw], 128, cw, "scb2")
                    k.v(lambda eng, tj=tj, c0=c0, cw=cw, g0=g0, gw=gw: eng.tensor_tensor(
                        scb2[:, 0:cw].rearrange("p (g e) -> p g e", e=64),
                        stair[:, 0:cw].rearrange("p (g e) -> p g e", e=64),
                        deq3[:, tj, g0:g0 + gw].unsqueeze(2).broadcast_to((128, gw, 64)), OP.mult),
                        reads=["stair", "deq3"], writes=["scb2"])
                    yield
                    for fj in range(cw // 128):
                        fg = (c0 // 128) + fj
                        ws = fg % 2
                        k.v(lambda eng, fj=fj: eng.tensor_copy(scb2f[:, :], scb2[:, fj * 128:(fj + 1) * 128]),
                            reads=["scb2"], writes=["scb2f"])
                        k.tp(pC[:, 0:128], scb2f[:, :],
                             identf[:, :], reads=["scb2f", "identf"], writes=["pC"])
                        k.v(lambda eng, ws=ws: eng.tensor_copy(wqT[:, ws, :], pC[:, 0:128]),
                            reads=["pC"], writes=[f"wqT{ws}"])
                        k.dma("sync",
                              ag_in[2 * SH12:].rearrange("(f d) -> f d", d=DS3)
                              [fg * 128:(fg + 1) * 128, tj * 128:(tj + 1) * 128],
                              wqT[:, ws, :], reads=[f"wqT{ws}"], writes=["ag_in_d"])
                    yield
            if NBLOCKS == 3.5:
                k.barrier()
                k.dma("gpsimd", outT[:, 0:704], ag_in[0:SH12].rearrange("(d f) -> d f", f=FS),
                      reads=["ag_in_d"], writes=["outd"])
                k.dma("gpsimd", outT[:, 704:1408], ag_in[SH12:2 * SH12].rearrange("(d f) -> d f", f=FS),
                      reads=["ag_in_d"], writes=["outd"])
                k.dma("gpsimd", outT[:, 1408:2048],
                      ag_in[2 * SH12:2 * SH12 + 2048 * 640].rearrange("(a b) -> a b", b=640),
                      reads=["ag_in_d"], writes=["outd"])
                k.rec("sync", lambda eng: eng.nop(), reads=["outd"])
                k.finalize()
                P4TAP.append(nc)
                return
            k.cc(lambda eng: eng.collective_compute(
                "AllGather", OP.bypass, replica_groups=[list(range(NCORES))],
                ins=[ag_in.opt()], outs=[ag_out.opt()]),
                reads=["ag_in_d"], writes=["ag_out_d"])
        _p4["it"] = gen()

    def drain_p4():
        it = _p4["it"]
        if it is None:
            return False
        try:
            next(it)
            return True
        except StopIteration:
            _p4["it"] = None
            return False

    P4_FREE = ("p4m", "p4s", "p4a", "scb2f", "wqT", "stair", "scb2", "deq3", "rec3",
               "deqn", "recn", "meanc", "sccol", "scq", "scbf", "sblk", "acc",
               "meansb", "s3w", "s1w", "wnat")
    if NBLOCKS == 4:
        setup_p4()

    if NBLOCKS == 0:
        return early_tap()

    wbufA = k.alloc("wbufA", [128, 2, 16, 256], BF16)
    kvbf = k.alloc("kvbf", [128, 2048], BF16)
    wbufB = k.alloc("wbufB", [128, 2, 16, 256], BF16)
    bias_bc = k.alloc("bias_bc", [128, 2, 512], F32)
    kvsb = k.alloc("kvsb", [128, 2048], F32)

    WqkR = Wqk.rearrange("(ki p) c -> p ki c", p=128)
    Wv0R = Wv0.rearrange("(ki p) c -> p ki c", p=128)
    Wu0R = Wu0.rearrange("(ki p) c -> p ki c", p=128)
    Wo0R = Wo0.rearrange("(ki p) c -> p ki c", p=128)
    for hc in range(8):
        ws = hc % 2
        if NBLOCKS == 4:
            drain_p4()
        k.dma("gpsimd", wbufA[:, ws, :, :], WqkR[:, :, hc * 256:(hc + 1) * 256],
              writes=[f"wA{ws}"])
        k.dma("gpsimd", wbufB[:, ws, :, :], Wv0R[:, :, hc * 256:(hc + 1) * 256],
              writes=[f"wB{ws}"])
        k.dma("sync", bias_bc[:, ws, 0:256],
              bqk[hc * 256:(hc + 1) * 256].unsqueeze(0).partition_broadcast(128), writes=[f"bbc{ws}"])
        k.dma("sync", bias_bc[:, ws, 256:512],
              bv0[hc * 256:(hc + 1) * 256].unsqueeze(0).partition_broadcast(128), writes=[f"bbc{ws}"])
        for tt in range(TTN):
            k.mm(pA[:, 0:256],
                 [(xT[:, ki, tt * 128:(tt + 1) * 128], wbufA[:, ws, ki, :]) for ki in range(16)],
                 reads=[f"xT{ki}" for ki in range(16)] + [f"wA{ws}"], writes=["pA"])
            k.mm(pB[:, 0:256],
                 [(xT[:, ki, tt * 128:(tt + 1) * 128], wbufB[:, ws, ki, :]) for ki in range(16)],
                 reads=[f"xT{ki}" for ki in range(16)] + [f"wB{ws}"], writes=["pB"])
            k.v(lambda eng, ws=ws: eng.tensor_tensor(tmpa[:, 0:256], pA[:, 0:256], bias_bc[:, ws, 0:256], OP.add),
                reads=["pA", f"bbc{ws}"], writes=["tmpa"])
            k.v(lambda eng: eng.tensor_scalar(tmpb[:, 0:256], tmpa[:, 0:256], 0.0, None, OP.min),
                reads=["tmpa"], writes=["tmpb"])
            k.act(tmpc[:, 0:256], tmpb[:, 0:256], AF.Exp, reads=["tmpb"], writes=["tmpc"])
            k.v(lambda eng: eng.tensor_scalar(tmpa[:, 0:256], tmpa[:, 0:256], 0.0, None, OP.max),
                reads=["tmpa"], writes=["tmpa"])
            k.v(lambda eng: eng.scalar_tensor_tensor(tbf[:, 0, 0:256], tmpc[:, 0:256], -1.0,
                                                     tmpa[:, 0:256], OP.add, OP.add),
                reads=["tmpa", "tmpc"], writes=["tbf0"])
            k.v(lambda eng, ws=ws: eng.tensor_tensor(tmpb[:, 0:256], pB[:, 0:256], bias_bc[:, ws, 256:512], OP.add),
                reads=["pB", f"bbc{ws}"], writes=["tmpb"])
            k.act(tbf[:, 1, 0:256], tmpb[:, 0:256], AF.Silu, reads=["tmpb"], writes=["tbf1"])
            for hh in range(2):
                h = hc * 2 + hh
                col = (h % 4) * 512 + (h // 4) * 128
                k.mm(pKV[:, col:col + 128],
                     [(tbf[:, 0, hh * 128:(hh + 1) * 128], tbf[:, 1, hh * 128:(hh + 1) * 128])],
                     reads=["tbf0", "tbf1"], writes=[f"pKV{h % 4}"],
                     start0=(tt == 0), stop=(tt == TTN - 1))
    for b4 in range(4):
        k.v(lambda eng, b4=b4: eng.tensor_copy(kvsb[:, b4 * 512:(b4 + 1) * 512], pKV[:, b4 * 512:(b4 + 1) * 512]),
            reads=[f"pKV{b4}"], writes=["kvsb"])
    k.dma("sync", kv_in, kvsb[:, :], reads=["kvsb"], writes=["kv_in_d"])
    if NBLOCKS == 0.5:
        return early_tap()
    k.cc(lambda eng: eng.collective_compute(
        "AllReduce", OP.add, replica_groups=[list(range(NCORES))],
        ins=[kv_in.opt()], outs=[kv_out.opt()]),
        reads=["kv_in_d"], writes=["kv_out_d"])
    k.dma("gpsimd", kvbf[:, :], kv_out, reads=["kv_out_d"], writes=["kvbf"])
    if NBLOCKS == 0.75:
        return early_tap()

    k.barrier()
    k.free("kvsb", "bias_bc", "wbufB")
    ych1 = k.alloc("ych1", [128, DT, 512], BF16)
    zbuf = k.alloc("zbuf", [128, 16, 512], BF16)
    ubuf = k.alloc("ubuf", [128, 512], BF16)

    for tc in range(TCN):
        for h in range(16):
            ws = h % 2
            col = (h % 4) * 512 + (h // 4) * 128
            if NBLOCKS == 4:
                drain_p4()
            k.dma("gpsimd", wbufA[:, ws, :, 0:128],
                  WqkR[:, :, h * 128:(h + 1) * 128], writes=[f"wA{ws}"])
            k.mm(pA[:, :],
                 [(wbufA[:, ws, ki, 0:128], xT[:, ki, tc * 512:(tc + 1) * 512]) for ki in range(16)],
                 reads=[f"wA{ws}"] + [f"xT{ki}" for ki in range(16)], writes=["pA"])
            k.v(lambda eng, h=h: eng.tensor_scalar(tmpa[:, :], pA[:, :],
                                                   cols[:, COL["bqk"], h:h + 1], None, OP.add),
                reads=["pA", "col_bqk"], writes=["tmpa"])
            k.v(lambda eng: eng.tensor_scalar(tmpb[:, :], tmpa[:, :], 0.0, None, OP.min),
                reads=["tmpa"], writes=["tmpb"])
            k.act(tmpc[:, :], tmpb[:, :], AF.Exp, reads=["tmpb"], writes=["tmpc"])
            k.v(lambda eng: eng.tensor_scalar(tmpa[:, :], tmpa[:, :], 0.0, None, OP.max),
                reads=["tmpa"], writes=["tmpa"])
            k.v(lambda eng: eng.scalar_tensor_tensor(tbf[:, 0, :], tmpc[:, :], -1.0,
                                                     tmpa[:, :], OP.add, OP.add),
                reads=["tmpa", "tmpc"], writes=["tbf0"])
            if NBLOCKS == 0.78:
                continue
            k.mm(pD[:, :], [(kvbf[:, col:col + 128], tbf[:, 0, :])],
                 reads=["kvbf", "tbf0"], writes=["pD"])
            if NBLOCKS == 0.79:
                continue
            k.v(lambda eng, h=h: eng.tensor_copy(ych1[:, h, :], pD[:, :]),
                reads=["pD"], writes=[f"y{h}"])
        if NBLOCKS in (0.78, 0.79, 0.8):
            return early_tap()
        bc, bkey = ln_stats(lambda dj: ych1[:, dj, :], lambda dj: [f"y{dj}"], bf=True)
        if NBLOCKS == 0.85:
            return early_tap()
        for dj in range(DT):
            ws = dj % 2
            if NBLOCKS == 4:
                drain_p4()
            k.dma("gpsimd", wbufA[:, ws, :, 0:128],
                  Wu0R[:, :, dj * 128:(dj + 1) * 128], writes=[f"wA{ws}"])
            k.mm(pA[:, :],
                 [(wbufA[:, ws, ki, 0:128], xT[:, ki, tc * 512:(tc + 1) * 512]) for ki in range(16)],
                 reads=[f"wA{ws}"] + [f"xT{ki}" for ki in range(16)], writes=["pA"])
            k.act(ubuf[:, :], pA[:, :], AF.Silu, bias=cols[:, COL["bu0"], dj:dj + 1],
                  reads=["pA", "col_bu0"], writes=["ubuf"])
            k.v(lambda eng, dj=dj, bc=bc: eng.tensor_tensor(tmpa[:, :], ych1[:, dj, :], bc[:, 0:512], OP.subtract),
                reads=[f"y{dj}", bkey], writes=["tmpa"])
            k.v(lambda eng, dj=dj, bc=bc: eng.scalar_tensor_tensor(
                tmpa[:, :], tmpa[:, :], cols[:, COL["ln_g0"], dj:dj + 1], bc[:, 512:1024], OP.mult, OP.mult),
                reads=["tmpa", bkey, "col_ln_g0"], writes=["tmpa"])
            k.v(lambda eng, dj=dj: eng.scalar_tensor_tensor(
                zbuf[:, dj, :], tmpa[:, :], cols[:, COL["ln_b0"], dj:dj + 1], ubuf[:, :], OP.add, OP.mult),
                reads=["tmpa", "ubuf", "col_ln_b0"], writes=[f"z{dj}"])
        for dj in range(DT):
            ws = dj % 2
            if NBLOCKS == 4:
                drain_p4()
            k.dma("gpsimd", wbufA[:, ws, :, 0:128],
                  Wo0R[:, :, dj * 128:(dj + 1) * 128], writes=[f"wA{ws}"])
            k.mm(pB[:, :],
                 [(wbufA[:, ws, ki, 0:128], zbuf[:, ki, :]) for ki in range(16)],
                 reads=[f"wA{ws}"] + [f"z{ki}" for ki in range(16)], writes=["pB"])
            k.v(lambda eng, dj=dj: eng.tensor_scalar(tmpb[:, :], pB[:, :],
                                                     cols[:, COL["bo0"], dj:dj + 1], None, OP.add),
                reads=["pB", "col_bo0"], writes=["tmpb"])
            k.v(lambda eng, dj=dj, tc=tc: eng.tensor_tensor(
                ych1[:, dj, :], tmpb[:, :], xT[:, dj, tc * 512:(tc + 1) * 512], OP.add),
                reads=["tmpb", f"xT{dj}"], writes=[f"y{dj}"])
        bc2, bk2 = ln_stats(lambda dj: ych1[:, dj, :], lambda dj: [f"y{dj}"], bf=True)
        for dj in range(DT):
            k.v(lambda eng, dj=dj, bc2=bc2: eng.tensor_tensor(tmpa[:, :], ych1[:, dj, :], bc2[:, 0:512], OP.subtract),
                reads=[f"y{dj}", bk2], writes=["tmpa"])
            k.v(lambda eng, dj=dj, tc=tc, bc2=bc2: eng.tensor_tensor(
                xT[:, dj, tc * 512:(tc + 1) * 512], tmpa[:, :], bc2[:, 512:1024], OP.mult),
                reads=["tmpa", bk2], writes=[f"xT{dj}"])

    def finish_tap():
        k.barrier()
        for dj in range(DT):
            k.dma("gpsimd", outT[dj * 128:(dj + 1) * 128, :], xT[:, dj, :],
                  reads=[f"xT{dj}"], writes=["outd"])
        k.rec("gpsimd", lambda eng: eng.nop(), reads=["outd"])
        k.finalize()
        return nc

    if NBLOCKS == 1:
        return finish_tap()

    if NBLOCKS == 4:
        while drain_p4():
            pass
    k.barrier(wait_cc=False)
    k.free("ubuf", "zbuf", "ych1", "kvbf")
    if NBLOCKS == 4:
        k.free("wbufA")
        k.free(*P4_FREE)
        wbufA = k.alloc("wbufA", [128, 2, 16, 256], BF16)

    xcb = k.alloc("xcb", [128, 16, 512], BF16)
    qkz = k.alloc("qkz", [128, 16, 512], BF16)
    vb4 = k.alloc("vb4", [128, 16, 512], BF16)
    gb4 = k.alloc("gb4", [128, 16, 512], BF16)
    rwsb = k.alloc("rwsb", [128, 16, 2], BF16)
    dotsb = k.alloc("dotsb", [128, 4, 16], F32)
    logsb = k.alloc("logsb", [128, 4, 32], F32)
    csb = k.alloc("csb", [128, 4, 16], F32)
    cTsb = k.alloc("cTsb", [16, 512], F32)
    statro = k.alloc("statro", [16, 1024], F32)
    hbc = k.alloc("hbc", [128, 1024], F32)
    for h in range(16):
        k.dma("gpsimd", rwsb[:, h, :], routw[h, :, :], writes=["rwsb"])

    for tc in range(TCN):
        for j in range(DT):
            k.act(xcb[:, j, :], xT[:, j, tc * 512:(tc + 1) * 512], AF.Silu,
                  scale=cols[:, COL["cw1"], j:j + 1], reads=[f"xT{j}", "col_cw1"], writes=[f"xcb{j}"])
        for Wap, ntile, nm in [(Wq1, 8, "q"), (Wk1, 8, "k"), (Wv1, 16, "v"), (Wg1, 16, "g")]:
            WapR = Wap.rearrange("(ki p) c -> p ki c", p=128)
            for dj in range(ntile):
                ws = dj % 2
                k.dma("gpsimd", wbufA[:, ws, :, 0:128],
                      WapR[:, :, dj * 128:(dj + 1) * 128], writes=[f"wA{ws}"])
                k.mm(pA[:, :],
                     [(wbufA[:, ws, ki, 0:128], xcb[:, ki, :]) for ki in range(16)],
                     reads=[f"wA{ws}"] + [f"xcb{ki}" for ki in range(16)], writes=["pA"])
                if nm == "q":
                    k.act(qkz[:, dj, :], pA[:, :], AF.Silu, reads=["pA"], writes=[f"z{dj}"])
                elif nm == "k":
                    k.v(lambda eng, dj=dj: eng.tensor_copy(qkz[:, 8 + dj, :], pA[:, :]),
                        reads=["pA"], writes=[f"z{8 + dj}"])
                elif nm == "v":
                    k.v(lambda eng, dj=dj: eng.tensor_copy(vb4[:, dj, :], pA[:, :]),
                        reads=["pA"], writes=[f"fv{dj}"])
                else:
                    k.act(gb4[:, dj, :], pA[:, :], AF.Silu, reads=["pA"], writes=[f"fg{dj}"])
        pCv = pC[:, :].rearrange("p (s c) -> p s c", s=4)
        pDv = pD[:, :].rearrange("p (s c) -> p s c", s=4)
        for j in range(8):
            k.v(lambda eng, j=j: eng.tensor_tensor(tbf[:, 0, :], qkz[:, j, :], qkz[:, 8 + j, :], OP.mult),
                reads=[f"z{j}", f"z{8 + j}"], writes=["tbf0"])
            for s4 in range(4):
                k.mm(pCv[:, s4, 2 * j:2 * j + 2],
                     [(tbf[:, 0, s4 * 128:(s4 + 1) * 128], g2p[:, :])],
                     reads=["tbf0", "g2p"], writes=["pC"])
        k.v(lambda eng: eng.tensor_copy(dotsb[:, :, :], pCv[:, :, 0:16]), reads=["pC"], writes=["dotsb"])
        for h in range(16):
            for s4 in range(4):
                k.mm(pDv[:, s4, 2 * h:2 * h + 2],
                     [(vb4[:, h, s4 * 128:(s4 + 1) * 128], rwsb[:, h, :])],
                     reads=[f"fv{h}", "rwsb"], writes=["pD"])
        k.v(lambda eng: eng.tensor_copy(logsb[:, :, :], pDv[:, :, 0:32]), reads=["pD"], writes=["logsb"])
        lgv = logsb[:, :, :].rearrange("p s (h two) -> p s h two", two=2)
        k.v(lambda eng: eng.tensor_tensor(csb[:, :, :], lgv[:, :, :, 1], lgv[:, :, :, 0], OP.subtract),
            reads=["logsb"], writes=["csb"])
        k.v(lambda eng: eng.tensor_scalar(tmpa[:, 0:64].rearrange("p (s h) -> p s h", s=4),
                                          csb[:, :, :], -1.0, None, OP.mult),
            reads=["csb"], writes=["tmpa"])
        k.v(lambda eng: eng.tensor_tensor(csb[:, :, :], csb[:, :, :],
                                          tmpa[:, 0:64].rearrange("p (s h) -> p s h", s=4), OP.max),
            reads=["csb", "tmpa"], writes=["csb"])
        k.act(csb[:, :, :], csb[:, :, :], AF.Sigmoid, reads=["csb"], writes=["csb"])
        k.v(lambda eng: eng.scalar_tensor_tensor(csb[:, :, :], csb[:, :, :], 0.125,
                                                 dotsb[:, :, :], OP.mult, OP.mult),
            reads=["csb", "dotsb"], writes=["csb"])
        for s4 in range(4):
            k.tp(pC[0:16, 0:128], csb[:, s4, :], identf[:, :], reads=["csb", "identf"], writes=["pC"])
            k.v(lambda eng, s4=s4: eng.tensor_copy(cTsb[:, s4 * 128:(s4 + 1) * 128], pC[0:16, 0:128]),
                reads=["pC"], writes=["cTsb"])
        for h in range(16):
            k.mm(pA[0:16, :], [(hsel[:, h, :], vb4[:, h, :])],
                 reads=["hsel", f"fv{h}"], writes=["pA"],
                 start0=(h == 0), stop=(h == 15))
            k.v(lambda eng, h=h: eng.tensor_tensor(tbf[:, 0, :], vb4[:, h, :], vb4[:, h, :], OP.mult),
                reads=[f"fv{h}"], writes=["tbf0"])
            k.mm(pB[0:16, :], [(hsel[:, h, :], tbf[:, 0, :])],
                 reads=["hsel", "tbf0"], writes=["pB"],
                 start0=(h == 0), stop=(h == 15))
        k.v(lambda eng: eng.tensor_scalar(statro[:, 0:512], pA[0:16, :], 1.0 / 128.0, None, OP.mult),
            reads=["pA"], writes=["statro"])
        k.v(lambda eng: eng.tensor_scalar(statro[:, 512:1024], pB[0:16, :], 1.0 / 128.0, None, OP.mult),
            reads=["pB"], writes=["statro"])
        k.v(lambda eng: eng.tensor_tensor(sq2[0:16, 0, :], statro[:, 0:512], statro[:, 0:512], OP.mult),
            reads=["statro"], writes=["sq0"])
        k.v(lambda eng: eng.scalar_tensor_tensor(sq2[0:16, 0, :], sq2[0:16, 0, :], -1.0,
                                                 statro[:, 512:1024], OP.mult, OP.add),
            reads=["sq0", "statro"], writes=["sq0"])
        k.v(lambda eng: eng.tensor_tensor(tmpa[0:16, :], cTsb[:, :], cTsb[:, :], OP.mult),
            reads=["cTsb"], writes=["tmpa"])
        k.v(lambda eng: eng.tensor_tensor(tmpa[0:16, :], tmpa[0:16, :], sq2[0:16, 0, :], OP.mult),
            reads=["tmpa", "sq0"], writes=["tmpa"])
        k.v(lambda eng: eng.tensor_scalar(tmpa[0:16, :], tmpa[0:16, :], 1e-5, None, OP.add),
            reads=["tmpa"], writes=["tmpa"])
        k.v(lambda eng: eng.reciprocal(tmpb[0:16, :], tmpa[0:16, :]), reads=["tmpa"], writes=["tmpb"])
        k.act(tmpc[0:16, :], tmpb[0:16, :], AF.Sqrt, reads=["tmpb"], writes=["tmpc"])
        k.v(lambda eng: eng.tensor_tensor(tmpb[0:16, :], tmpc[0:16, :], tmpc[0:16, :], OP.mult),
            reads=["tmpc"], writes=["tmpb"])
        k.v(lambda eng: eng.tensor_tensor(tmpb[0:16, :], tmpb[0:16, :], tmpa[0:16, :], OP.mult),
            reads=["tmpb", "tmpa"], writes=["tmpb"])
        k.v(lambda eng: eng.tensor_scalar(tmpb[0:16, :], tmpb[0:16, :], -0.5, 1.5, OP.mult, OP.add),
            reads=["tmpb"], writes=["tmpb"])
        k.v(lambda eng: eng.tensor_tensor(tmpc[0:16, :], tmpc[0:16, :], tmpb[0:16, :], OP.mult),
            reads=["tmpb", "tmpc"], writes=["tmpc"])
        k.v(lambda eng: eng.tensor_tensor(statro[:, 512:1024], cTsb[:, :], tmpc[0:16, :], OP.mult),
            reads=["cTsb", "tmpc"], writes=["statro"])
        k.dma("sync", hrowb, statro[:, :], reads=["statro"], writes=["hrb"])
        for h in range(16):
            k.dma("sync", hbc[:, :], hrowb[h:h + 1, :].partition_broadcast(128),
                  reads=["hrb"], writes=["hbc"])
            k.v(lambda eng, h=h: eng.tensor_tensor(tmpa[:, :], vb4[:, h, :], hbc[:, 0:512], OP.subtract),
                reads=[f"fv{h}", "hbc"], writes=["tmpa"])
            k.v(lambda eng, h=h: eng.tensor_tensor(tmpa[:, :], tmpa[:, :], hbc[:, 512:1024], OP.mult),
                reads=["tmpa", "hbc"], writes=["tmpa"])
            k.v(lambda eng, h=h: eng.tensor_tensor(qkz[:, h, :], tmpa[:, :], gb4[:, h, :], OP.mult),
                reads=["tmpa", f"fg{h}"], writes=[f"z{h}"])
        for dj in range(DT):
            ws = dj % 2
            k.dma("gpsimd", wbufA[:, ws, :, 0:128],
                  Wout1.rearrange("(ki p) c -> p ki c", p=128)[:, :, dj * 128:(dj + 1) * 128],
                  writes=[f"wA{ws}"])
            k.mm(pB[:, :],
                 [(wbufA[:, ws, ki, 0:128], qkz[:, ki, :]) for ki in range(16)],
                 reads=[f"wA{ws}"] + [f"z{ki}" for ki in range(16)], writes=["pB"])
            k.v(lambda eng, dj=dj, tc=tc: eng.tensor_tensor(
                vb4[:, dj, :], pB[:, :], xT[:, dj, tc * 512:(tc + 1) * 512], OP.add),
                reads=["pB", f"xT{dj}"], writes=[f"fv{dj}"])
        bc3, bk3 = ln_stats(lambda dj: vb4[:, dj, :], lambda dj: [f"fv{dj}"], bf=True)
        for dj in range(DT):
            k.v(lambda eng, dj=dj, bc3=bc3: eng.tensor_tensor(tmpa[:, :], vb4[:, dj, :], bc3[:, 0:512], OP.subtract),
                reads=[f"fv{dj}", bk3], writes=["tmpa"])
            k.v(lambda eng, dj=dj, tc=tc, bc3=bc3: eng.tensor_tensor(
                xT[:, dj, tc * 512:(tc + 1) * 512], tmpa[:, :], bc3[:, 512:1024], OP.mult),
                reads=["tmpa", bk3], writes=[f"xT{dj}"])

    if NBLOCKS == 2:
        return finish_tap()

    k.barrier()
    k.free("hbc", "statro", "cTsb", "csb", "logsb", "dotsb", "rwsb",
           "gb4", "vb4", "qkz", "xcb", "wbufA")

    xnc = k.alloc("xnc", [128, 16, 512], BF16)
    u2 = k.alloc("u2", [8, T], F32)
    win8 = k.alloc("win8", [128, 16, 8], BF16)
    dec8 = k.alloc("dec8", [8, 6], F32)
    k.dma("sync", dec8[0:8, 5:6], dec_l.rearrange("(r one) -> r one", one=1), writes=["dec_lg"])
    for ki in range(16):
        k.dma("gpsimd", win8[:, ki, :], W_in[ki * 128:(ki + 1) * 128, :], writes=["win8"])
    k.act(dec8[0:8, 0:1], dec8[0:8, 5:6], AF.Sigmoid, reads=["dec_lg"], writes=["dec8a"])
    k.act(dec8[0:8, 1:2], dec8[0:8, 0:1], AF.Ln, reads=["dec8a"], writes=["dec8b"], fence=True)
    k.v(lambda eng: eng.tensor_scalar(dec8[0:8, 2:3], dec8[0:8, 1:2], 2048.0, None, OP.mult),
        reads=["dec8b"], writes=["dec8c"])

    def emit_xn(tc):
        bc, bkey = ln_stats(lambda dj, tc=tc: xT[:, dj, tc * 512:(tc + 1) * 512],
                            lambda dj: [f"xT{dj}"], bf=True)
        for dj in range(DT):
            k.v(lambda eng, dj=dj, tc=tc, bc=bc: eng.tensor_tensor(
                tmpa[:, :], xT[:, dj, tc * 512:(tc + 1) * 512], bc[:, 0:512], OP.subtract),
                reads=[f"xT{dj}", bkey], writes=["tmpa"])
            k.v(lambda eng, dj=dj, bc=bc: eng.scalar_tensor_tensor(
                tmpa[:, :], tmpa[:, :], cols[:, COL["ln_g2"], dj:dj + 1], bc[:, 512:1024], OP.mult, OP.mult),
                reads=["tmpa", bkey, "col_ln_g2"], writes=["tmpa"])
            k.v(lambda eng, dj=dj: eng.tensor_scalar(
                xnc[:, dj, :], tmpa[:, :], cols[:, COL["ln_b2"], dj:dj + 1], None, OP.add),
                reads=["tmpa", "col_ln_b2"], writes=[f"xn{dj}"])

    for tc in range(TCN):
        emit_xn(tc)
        k.mm(pA[0:8, :],
             [(win8[:, ki, :], xnc[:, ki, :]) for ki in range(16)],
             reads=["win8"] + [f"xn{ki}" for ki in range(16)], writes=["pA"])
        k.v(lambda eng, tc=tc: eng.tensor_scalar(u2[:, tc * 512:(tc + 1) * 512], pA[0:8, :],
                                                 smalls[0:8, 0:1], None, OP.add),
            reads=["pA", "smalls0"], writes=["u2"])
    if NBLOCKS == 2.3:
        return finish_tap()
    hbf = k.alloc("hbf", [8, T], BF16)
    scanw = k.alloc("scanw", [8, T], F32)
    dectl = k.alloc("dectl", [8, T], F32)
    aux8 = k.alloc("aux8", [8, T], F32)
    eall = k.alloc("eall", [8, 8], F32)
    carr = k.alloc("carr", [8, 8], F32)
    k.dma("sync", carr[:, :], carry_m.unsqueeze(0).partition_broadcast(8), writes=["carr"])
    k.v(lambda eng: eng.tensor_scalar(aux8[:, :], u2[:, :], 0.0, 1.0, OP.mult, OP.add),
        reads=["u2"], writes=["aux8"], fence=True)
    k.v(lambda eng: eng.tensor_scalar(dectl[:, :], aux8[:, :], dec8[0:8, 0:1], None, OP.mult),
        reads=["aux8", "dec8a"], writes=["dectl"], fence=True)
    k.v(lambda eng: eng.tensor_tensor_scan(scanw[:, :], dectl[:, :], u2[:, :], 0.0, OP.mult, OP.add),
        reads=["dectl", "u2"], writes=["scanw"], fence=True)
    k.dma("sync", E_in, scanw[:, 2047:2048], reads=["scanw"], writes=["E_in_d"])
    k.cc(lambda eng: eng.collective_compute(
        "AllGather", OP.bypass, replica_groups=[list(range(NCORES))],
        ins=[E_in.opt()], outs=[E_out.opt()]),
        reads=["E_in_d"], writes=["E_out_d"])
    k.dma("sync", eall[:, :], E_out.rearrange("(c r) -> r c", r=8), reads=["E_out_d"],
          writes=["eall"], slow=True)
    k.act(carr[:, :], carr[:, :], AF.Exp, scale=dec8[0:8, 2:3], reads=["carr", "dec8c"], writes=["carr"])
    k.v(lambda eng: eng.tensor_tensor(carr[:, :], carr[:, :], eall[:, :], OP.mult),
        reads=["carr", "eall"], writes=["carr"])
    k.v(lambda eng: eng.reduce_sum(dec8[0:8, 3:4], carr[:, :], axis=AX.X),
        reads=["carr"], writes=["dec8d"], fence=True)
    k.v(lambda eng: eng.tensor_tensor_scan(dectl[:, :], aux8[:, :], aux8[:, :], 0.0, OP.mult, OP.add),
        reads=["aux8"], writes=["dectl"], fence=True)
    k.act(dectl[:, :], dectl[:, :], AF.Exp, scale=dec8[0:8, 1:2], reads=["dectl", "dec8b"], writes=["dectl"])
    k.v(lambda eng: eng.scalar_tensor_tensor(scanw[:, :], dectl[:, :], dec8[0:8, 3:4], scanw[:, :],
                                             OP.mult, OP.add),
        reads=["dectl", "dec8d", "scanw"], writes=["scanw"], fence=True)
    k.v(lambda eng: eng.tensor_copy(hbf[:, :], scanw[:, :]), reads=["scanw"], writes=["hbf"], fence=True)

    if NBLOCKS == 2.5:
        return finish_tap()

    k.barrier()
    k.free("carr", "eall", "aux8", "dectl", "scanw")
    ych3 = k.alloc("ychunkB", [128, DT, 512], F32)
    hf2 = k.alloc("hf2", [32, T], BF16)
    wsm = k.alloc("wsm", [32, 2, D], BF16)
    wdn = k.alloc("wdn", [128, 16, 32], BF16)
    k.dma("gpsimd", wsm[0:8, 0, :], W_out2, writes=["wsm0"])
    k.dma("gpsimd", wsm[0:32, 1, :], W_up, writes=["wsm1"])
    for ki in range(16):
        k.dma("gpsimd", wdn[:, ki, :], W_down[ki * 128:(ki + 1) * 128, :], writes=["wdn"])

    for tc in range(TCN):
        emit_xn(tc)
        for dj in range(DT):
            k.mm(pA[:, :], [(wsm[0:8, 0, dj * 128:(dj + 1) * 128], hbf[:, tc * 512:(tc + 1) * 512])],
                 reads=["wsm0", "hbf"], writes=["pA"])
            k.v(lambda eng, dj=dj: eng.tensor_scalar(tmpb[:, :], pA[:, :],
                                                     cols[:, COL["b_out2"], dj:dj + 1], None, OP.add),
                reads=["pA", "col_b_out2"], writes=["tmpb"])
            k.v(lambda eng, dj=dj: eng.scalar_tensor_tensor(
                tmpb[:, :], xnc[:, dj, :], cols[:, COL["D_skip"], dj:dj + 1], tmpb[:, :], OP.mult, OP.add),
                reads=[f"xn{dj}", "col_D_skip", "tmpb"], writes=["tmpb"])
            k.v(lambda eng, dj=dj, tc=tc: eng.tensor_tensor(
                ych3[:, dj, :], tmpb[:, :], xT[:, dj, tc * 512:(tc + 1) * 512], OP.add),
                reads=["tmpb", f"xT{dj}"], writes=[f"y{dj}"])
        if NBLOCKS == 2.6:
            return finish_tap()
        bc, bkey = ln_stats(lambda dj: ych3[:, dj, :], lambda dj: [f"y{dj}"])
        for dj in range(DT):
            k.v(lambda eng, dj=dj, bc=bc: eng.tensor_tensor(tmpa[:, :], ych3[:, dj, :], bc[:, 0:512], OP.subtract),
                reads=[f"y{dj}", bkey], writes=["tmpa"])
            k.v(lambda eng, dj=dj, bc=bc: eng.scalar_tensor_tensor(
                tmpa[:, :], tmpa[:, :], cols[:, COL["ln_g2"], dj:dj + 1], bc[:, 512:1024], OP.mult, OP.mult),
                reads=["tmpa", bkey, "col_ln_g2"], writes=["tmpa"])
            k.v(lambda eng, dj=dj: eng.tensor_scalar(
                xnc[:, dj, :], tmpa[:, :], cols[:, COL["ln_b2"], dj:dj + 1], None, OP.add),
                reads=["tmpa", "col_ln_b2"], writes=[f"xn{dj}"])
        if NBLOCKS == 2.65:
            return finish_tap()
        k.mm(pB[0:32, :],
             [(wdn[:, ki, :], xnc[:, ki, :]) for ki in range(16)],
             reads=["wdn"] + [f"xn{ki}" for ki in range(16)], writes=["pB"])
        k.act(hf2[:, tc * 512:(tc + 1) * 512], pB[0:32, :], AF.Relu, bias=smalls[0:32, 1:2],
              reads=["pB", "smalls1"], writes=["hf2"])
        for dj in range(DT):
            k.mm(pC[:, :], [(wsm[0:32, 1, dj * 128:(dj + 1) * 128], hf2[:, tc * 512:(tc + 1) * 512])],
                 reads=["wsm1", "hf2"], writes=["pC"])
            k.v(lambda eng, dj=dj: eng.tensor_scalar(tmpb[:, :], pC[:, :],
                                                     cols[:, COL["b_up"], dj:dj + 1], None, OP.add),
                reads=["pC", "col_b_up"], writes=["tmpb"])
            k.v(lambda eng, dj=dj: eng.tensor_tensor(tmpb[:, :], tmpb[:, :], ych3[:, dj, :], OP.add),
                reads=["tmpb", f"y{dj}"], writes=["tmpb"])
            k.v(lambda eng, dj=dj, tc=tc: eng.tensor_tensor(
                ych3[:, dj, :], tmpb[:, :], xT[:, dj, tc * 512:(tc + 1) * 512], OP.add),
                reads=["tmpb", f"xT{dj}"], writes=[f"y{dj}"])
        if NBLOCKS == 2.7:
            return finish_tap()
        bc2, bk2 = ln_stats(lambda dj: ych3[:, dj, :], lambda dj: [f"y{dj}"])
        for dj in range(DT):
            k.v(lambda eng, dj=dj, bc2=bc2: eng.tensor_tensor(tmpa[:, :], ych3[:, dj, :], bc2[:, 0:512], OP.subtract),
                reads=[f"y{dj}", bk2], writes=["tmpa"])
            k.v(lambda eng, dj=dj, tc=tc, bc2=bc2: eng.tensor_tensor(
                xT[:, dj, tc * 512:(tc + 1) * 512], tmpa[:, :], bc2[:, 512:1024], OP.mult),
                reads=["tmpa", bk2], writes=[f"xT{dj}"])
        if NBLOCKS == 2.75:
            return finish_tap()

    if NBLOCKS == 3:
        return finish_tap()

    k.barrier()
    k.free("wdn", "wsm", "hf2", "ychunkB", "hbf", "dec8", "win8", "u2", "xnc")

    if NBLOCKS != 4:
        setup_p4()
        while drain_p4():
            pass
        if P4TAP:
            return P4TAP[0]
        k.barrier()
        k.free(*P4_FREE)

    hch = k.alloc("hch", [128, 22, 512], BF16)
    wbufA = k.alloc("wbufA_f", [128, 2, 16, 128], BF16)
    wbufB = k.alloc("wbufB_f", [128, 2, 16, 128], BF16)
    w3st = k.alloc("w3st", [128, 2, 22, 128], BF16)
    ych4 = k.alloc("ychunkC", [128, DT, 512], F32)

    def w12_stripe(wi, fi, ws, wbuf, wkey):
        f0g = fi * 128
        segs = []
        f = f0g
        while f < f0g + 128:
            cc2 = f // FS
            take = min((cc2 + 1) * FS - f, f0g + 128 - f)
            segs.append((cc2, f - cc2 * FS, f - f0g, take))
            f += take
        for (cc2, floc, foff, take) in segs:
            base = cc2 * AGS + wi * SH12
            k.dma("sync" if wi == 0 else "gpsimd", wbuf[:, ws, :, foff:foff + take],
                  ag_out[base:base + SH12].rearrange("(ki p f) -> p ki f", p=128, f=FS)
                  [:, :, floc:floc + take],
                  reads=["ag_out_d"], writes=[wkey])

    for tc in range(TCN):
        for half in range(2):
            for fo in range(22):
                fi = half * 22 + fo
                ws = fi % 2
                w12_stripe(0, fi, ws, wbufA, f"wSa{ws}")
                w12_stripe(1, fi, ws, wbufB, f"wSb{ws}")
                k.mm(pA[:, :],
                     [(wbufA[:, ws, ki, :], xT[:, ki, tc * 512:(tc + 1) * 512]) for ki in range(16)],
                     reads=[f"wSa{ws}"] + [f"xT{ki}" for ki in range(16)], writes=["pA"])
                k.mm(pB[:, :],
                     [(wbufB[:, ws, ki, :], xT[:, ki, tc * 512:(tc + 1) * 512]) for ki in range(16)],
                     reads=[f"wSb{ws}"] + [f"xT{ki}" for ki in range(16)], writes=["pB"])
                k.act(tmpa[:, :], pA[:, :], AF.Silu, reads=["pA"], writes=["tmpa"])
                k.v(lambda eng, fo=fo: eng.tensor_tensor(hch[:, fo, :], tmpa[:, :], pB[:, :], OP.mult),
                    reads=["tmpa", "pB"], writes=[f"h{fo}"])
            for dj in range(DT):
                cc3 = dj // 2
                dloc = (dj % 2) * 128
                ws = dj % 2
                base = cc3 * AGS + 2 * SH12
                k.dma("sync" if ws == 0 else "gpsimd", w3st[:, ws, :, :],
                      ag_out[base:base + SH3].rearrange("(fi p d) -> p fi d", p=128, d=DS3)
                      [:, half * 22:half * 22 + 22, dloc:dloc + 128],
                      reads=["ag_out_d"], writes=[f"w3st{ws}"])
                k.mm(pC[:, :],
                     [(w3st[:, ws, fo, :], hch[:, fo, :]) for fo in range(22)],
                     reads=[f"w3st{ws}"] + [f"h{fo}" for fo in range(22)], writes=["pC"])
                if half == 0:
                    k.v(lambda eng, dj=dj: eng.tensor_copy(ych4[:, dj, :], pC[:, :]),
                        reads=["pC"], writes=[f"y{dj}"])
                else:
                    k.v(lambda eng, dj=dj, tc=tc: eng.scalar_tensor_tensor(
                        tmpb[:, :], ych4[:, dj, :], 1.0, pC[:, :], OP.mult, OP.add),
                        reads=["pC", f"y{dj}"], writes=["tmpb"])
                    k.v(lambda eng, dj=dj, tc=tc: eng.tensor_tensor(
                        ych4[:, dj, :], tmpb[:, :], xT[:, dj, tc * 512:(tc + 1) * 512], OP.add),
                        reads=["tmpb", f"xT{dj}"], writes=[f"y{dj}"])
        bc, bkey = ln_stats(lambda dj: ych4[:, dj, :], lambda dj: [f"y{dj}"])
        for dj in range(DT):
            k.v(lambda eng, dj=dj, bc=bc: eng.tensor_tensor(tmpa[:, :], ych4[:, dj, :], bc[:, 0:512], OP.subtract),
                reads=[f"y{dj}", bkey], writes=["tmpa"])
            k.v(lambda eng, dj=dj, bc=bc: eng.tensor_tensor(tmpc[:, :], tmpa[:, :], bc[:, 512:1024], OP.mult),
                reads=["tmpa", bkey], writes=["tmpc"])
            k.dma("sync", outT[dj * 128:(dj + 1) * 128, tc * 512:(tc + 1) * 512], tmpc[:, :],
                  reads=["tmpc"], writes=["outd"])
    k.rec("sync", lambda eng: eng.nop(), reads=["outd"])
    k.finalize()
    return nc


_CACHE = {}


def kernel(L0_attn_x, L0_attn_Wqk, L0_attn_bqk, L0_attn_Wv, L0_attn_bv,
           L0_attn_Wu, L0_attn_bu, L0_attn_Wo, L0_attn_bo, L0_attn_ln_g,
           L0_attn_ln_b, L0_ffn_conv_w, L0_ffn_Wq, L0_ffn_Wk, L0_ffn_Wv,
           L0_ffn_Wg, L0_ffn_router_w, L0_ffn_Wout, L1_attn_ln_gamma,
           L1_attn_ln_beta, L1_attn_W_in, L1_attn_b_in, L1_attn_decay_logit,
           L1_attn_W_out, L1_attn_b_out, L1_attn_D_skip, L1_attn_W_down,
           L1_attn_b_down, L1_attn_W_up, L1_attn_b_up, L1_ffn_weight1,
           L1_ffn_weight2, L1_ffn_weight3):
    bf = ml_dtypes.bfloat16
    f32 = np.float32
    if "nc" not in _CACHE:
        _CACHE["nc"] = build_program()
    nc = _CACHE["nc"]

    x = np.asarray(L0_attn_x, dtype=f32).reshape(16384, 2048)
    w1 = np.asarray(L1_ffn_weight1).astype(bf, copy=False)
    w2 = np.asarray(L1_ffn_weight2).astype(bf, copy=False)
    w3 = np.asarray(L1_ffn_weight3).astype(bf, copy=False)

    shared = {
        "Wqk": np.asarray(L0_attn_Wqk, f32), "bqk": np.asarray(L0_attn_bqk, f32),
        "Wv0": np.asarray(L0_attn_Wv, f32), "bv0": np.asarray(L0_attn_bv, f32),
        "Wu0": np.asarray(L0_attn_Wu, f32), "bu0": np.asarray(L0_attn_bu, f32),
        "Wo0": np.asarray(L0_attn_Wo, f32), "bo0": np.asarray(L0_attn_bo, f32),
        "ln_g0": np.asarray(L0_attn_ln_g, f32), "ln_b0": np.asarray(L0_attn_ln_b, f32),
        "convw": np.asarray(L0_ffn_conv_w, f32),
        "Wq1": np.asarray(L0_ffn_Wq, f32), "Wk1": np.asarray(L0_ffn_Wk, f32),
        "Wv1": np.asarray(L0_ffn_Wv, f32), "Wg1": np.asarray(L0_ffn_Wg, f32),
        "routw": np.asarray(L0_ffn_router_w, f32),
        "Wout1": np.asarray(L0_ffn_Wout, f32),
        "ln_g2": np.asarray(L1_attn_ln_gamma, f32), "ln_b2": np.asarray(L1_attn_ln_beta, f32),
        "W_in": np.asarray(L1_attn_W_in, f32), "b_in": np.asarray(L1_attn_b_in, f32),
        "dec_l": np.asarray(L1_attn_decay_logit, f32),
        "W_out2": np.asarray(L1_attn_W_out, f32), "b_out2": np.asarray(L1_attn_b_out, f32),
        "D_skip": np.asarray(L1_attn_D_skip, f32),
        "W_down": np.asarray(L1_attn_W_down, f32), "b_down": np.asarray(L1_attn_b_down, f32),
        "W_up": np.asarray(L1_attn_W_up, f32), "b_up": np.asarray(L1_attn_b_up, f32),
        "ident_bf": np.eye(128, dtype=bf),
        "ident_f": np.eye(128, dtype=f32),
        "ones_in": np.ones(128, f32),
        "ones_in_bf": np.ones(128, bf),
        "g2pat": np.concatenate([np.repeat(np.float32([1, 0])[None], 64, 0),
                                 np.repeat(np.float32([0, 1])[None], 64, 0)], 0).astype(bf),
        "hselin": np.broadcast_to(np.eye(16, dtype=bf)[None], (128, 16, 16)).copy(),
        "nf4negmids": np.array([-m for m in NF4_MIDS] + [0.0], f32),
    }
    in_maps = []
    for c in range(NCORES):
        m = dict(shared)
        m["xT_in"] = np.ascontiguousarray(x[c * 2048:(c + 1) * 2048, :].T)
        m["w1s"] = np.ascontiguousarray(w1[c * FS:(c + 1) * FS, :])
        m["w2s"] = np.ascontiguousarray(w2[c * FS:(c + 1) * FS, :])
        m["w3s"] = np.ascontiguousarray(w3[c * DS3:(c + 1) * DS3, :])
        cm = np.full(8, 1e9, f32)
        for cc in range(c):
            cm[cc] = float(c - 1 - cc)
        m["carry_m"] = cm
        in_maps.append(m)
    res = run_bass_kernel_spmd(nc, in_maps, core_ids=list(range(NCORES)))
    out = np.empty((16384, 2048), f32)
    for c in range(NCORES):
        out[c * 2048:(c + 1) * 2048, :] = res.results[c]["outT"].T
    return out.reshape(1, 16384, 2048)


# version 12 (final)
# speedup vs baseline: 1.0448x; 1.0448x over previous
import numpy as np
import ml_dtypes
import concourse.bass as bass
import concourse.mybir as mybir
from concourse.bass_utils import run_bass_kernel_spmd

F32 = mybir.dt.float32
BF16 = mybir.dt.bfloat16
AF = mybir.ActivationFunctionType
OP = mybir.AluOpType
AX = mybir.AxisListType

NCORES = 8
T = 2048
D = 2048
DT = 16
TTN = 16
TCN = 4
FS = 704
DS3 = 256
SH12 = 2048 * FS
SH3 = 5632 * DS3
AGS = 2 * SH12 + SH3
NSC = float(5632 * 2048 // 64)

NBLOCKS = 4

_LV = np.array([-1.0, -0.6962, -0.5251, -0.3949, -0.2844, -0.1848, -0.0911,
                0.0, 0.0796, 0.1609, 0.2461, 0.3379, 0.4407, 0.5626, 0.723,
                1.0], dtype=np.float32).astype(ml_dtypes.bfloat16).astype(np.float64)
NF4_MIDS = [float(x) for x in (_LV[:-1] + _LV[1:]) / 2.0]
NF4_DELTAS = [float(x) for x in np.diff(_LV)]
NF4_L0 = float(_LV[0])

ENGS = ("sync", "scalar", "vector", "tensor", "gpsimd")

ARENA_OFF = {}


NDCH = 6


class _Op:
    __slots__ = ("eng", "fn", "is_async", "step", "marked", "syncval", "waits",
                 "ridx", "sem", "chval", "pre_waits")

    def __init__(self, eng, fn, is_async, step):
        self.eng = eng
        self.fn = fn
        self.is_async = is_async
        self.step = step
        self.marked = is_async
        self.syncval = None
        self.waits = []
        self.ridx = 0
        self.sem = None
        self.chval = 0
        self.pre_waits = []


class K:
    def __init__(self, nc):
        self.nc = nc
        self.ops = {e: [] for e in ENGS}
        self.lastw = {}
        self.readers = {}
        self._cms = {}
        self._sem_cms = []
        self.sems = {}
        for e in ENGS:
            cm = nc.semaphore("sem_" + e)
            self.sems[e] = cm.__enter__()
            self._sem_cms.append(cm)
        self.dsem = {}
        self.dcnt = {}
        for q in ("sync", "scalar", "gpsimd"):
            self.dsem[q] = []
            self.dcnt[q] = []
            for i in range(NDCH):
                cm = nc.semaphore(f"dma_{q}_{i}")
                self.dsem[q].append(cm.__enter__())
                self._sem_cms.append(cm)
                self.dcnt[q].append(0)
        cm = nc.semaphore("sem_cc")
        self.ccsem = cm.__enter__()
        self._sem_cms.append(cm)
        self.cccnt = 0
        self._barn = 0
        self._ridx = 0
        self._arena = None
        self._astack = []
        self._atop = 0
        self._dmaidx = {q: 0 for q in ("sync", "scalar", "gpsimd")}

    ARENA_B = 92 * 1024

    def alloc(self, name, shape, dtype, psum=False, perm=False):
        if psum or perm:
            cm = (self.nc.psum_tensor if psum else self.nc.sbuf_tensor)(name, list(shape), dtype)
            t = cm.__enter__()
            self._cms[name] = cm
            return t
        if self._arena is None:
            cm = self.nc.sbuf_tensor("ARENA", [128, self.ARENA_B // 2], BF16)
            self._arena = cm.__enter__()
            self._cms["__ARENA"] = cm
        esz = 4 if dtype == F32 else 2
        nel = 1
        for s in shape[1:]:
            nel *= s
        nbytes = nel * esz
        nbytes_al = (nbytes + 3) // 4 * 4
        off = self._atop
        self._atop = off + nbytes_al
        assert self._atop <= self.ARENA_B, (name, self._atop)
        ARENA_OFF[name] = (off, tuple(shape), dtype)
        view = self._arena[0:shape[0], off // 2:(off + nbytes) // 2]
        if dtype != BF16:
            view = view.bitcast(dtype)
        if len(shape) == 3:
            view = view.rearrange("p (a b) -> p a b", a=shape[1])
        elif len(shape) == 4:
            view = view.rearrange("p (a b c) -> p a b c", a=shape[1], b=shape[2])
        self._astack.append((name, off))
        return view

    def free(self, *names):
        for n in names:
            nm, off = self._astack.pop()
            assert nm == n, (nm, n)
            self._atop = off

    def _dep(self, a, b, force=False):
        if a is None or a is b:
            return
        if (not force) and (a.eng == b.eng) and not a.is_async:
            return
        a.marked = True
        b.waits.append(a)

    def rec(self, eng, fn, reads=(), writes=(), is_async=False, step=1, fence=False):
        op = _Op(eng, fn, is_async, step)
        op.ridx = self._ridx
        self._ridx += 1
        for key in reads:
            self._dep(self.lastw.get(key), op, force=fence)
        for key in writes:
            self._dep(self.lastw.get(key), op)
            for r in self.readers.get(key, ()):
                self._dep(r, op)
        for key in reads:
            self.readers.setdefault(key, []).append(op)
        for key in writes:
            self.lastw[key] = op
            self.readers[key] = []
        self.ops[eng].append(op)
        return op

    def _rec_dma(self, queue, fn, reads, writes):
        ch = self._dmaidx[queue] % NDCH
        self._dmaidx[queue] += 1
        op = self.rec(queue, fn, reads, writes, is_async=True, step=16)
        op.sem = self.dsem[queue][ch]
        prev = self.dcnt[queue][ch]
        if prev:
            op.pre_waits.append((op.sem, prev))
        self.dcnt[queue][ch] = prev + 16
        op.chval = prev + 16
        return op

    def barrier(self, wait_cc=True):
        n = self._barn
        self._barn += 1
        for e in ENGS:
            op = self.rec(e, lambda eng: eng.drain(), writes=[f"__bar{n}_{e}"])
            op.marked = True
            if e in self.dsem:
                for i in range(NDCH):
                    if self.dcnt[e][i]:
                        op.pre_waits.append((self.dsem[e][i], self.dcnt[e][i]))
            if e == "gpsimd" and self.cccnt and wait_cc:
                op.pre_waits.append((self.ccsem, self.cccnt))
        for e in ENGS:
            self.rec(e, lambda eng: eng.nop(),
                     reads=[f"__bar{n}_{f}" for f in ENGS if f != e])

    def dma(self, queue, out, in_, reads=(), writes=(), slow=False):
        def fn(eng):
            return eng.dma_start(out=out, in_=in_, allow_slow_non_contiguous=slow)
        return self._rec_dma(queue, fn, reads, writes)

    def dmat(self, queue, out, in_, reads=(), writes=()):
        def fn(eng):
            return eng.dma_start_transpose(out, in_)
        return self._rec_dma(queue, fn, reads, writes)

    def cc(self, fn, reads, writes):
        op = self.rec("gpsimd", fn, reads, writes, is_async=True, step=1)
        op.sem = self.ccsem
        self.cccnt += 1
        op.chval = self.cccnt
        return op

    def mm(self, out_ap, pairs, reads, writes, start0=True, stop=True):
        def fn(eng):
            last = None
            n = len(pairs)
            for i, (lhsT, rhs) in enumerate(pairs):
                last = eng.matmul(out_ap, lhsT, rhs,
                                  start=(start0 and i == 0),
                                  stop=(stop and i == n - 1))
            return last
        return self.rec("tensor", fn, reads, writes)

    def tp(self, out_ap, in_ap, ident, reads, writes):
        def fn(eng):
            return eng.transpose(out_ap, in_ap, ident)
        return self.rec("tensor", fn, reads, writes)

    def act(self, out_ap, in_ap, func, reads, writes, bias=0.0, scale=1.0, fence=False):
        def fn(eng):
            return eng.activation(out_ap, in_ap, func, bias=bias, scale=scale)
        return self.rec("scalar", fn, reads, writes, fence=fence)

    def v(self, fn, reads, writes, fence=False):
        return self.rec("vector", fn, reads, writes, fence=fence)

    def finalize(self):
        for e in ENGS:
            c = 0
            for op in self.ops[e]:
                if op.marked and not op.is_async:
                    c += op.step
                    op.syncval = c
                    op.sem = self.sems[e]
                    op.chval = c

        def mk(e):
            def body(eng):
                hi = {}
                for op in self.ops[e]:
                    tgt = {}
                    for a in op.waits:
                        key = a.sem.name
                        if a.chval > tgt.get(key, (None, 0))[1]:
                            tgt[key] = (a.sem, a.chval)
                    for key, (s, v) in tgt.items():
                        if hi.get(key, 0) < v:
                            hi[key] = v
                            eng.wait_ge(s, v)
                    for (s, v) in op.pre_waits:
                        if hi.get(s.name, 0) < v:
                            hi[s.name] = v
                            eng.wait_ge(s, v)
                    inst = op.fn(eng)
                    if op.marked:
                        inst.then_inc(op.sem, op.step)
            return body
        with self.nc.Block() as block:
            block.sync(mk("sync"))
            block.scalar(mk("scalar"))
            block.vector(mk("vector"))
            block.tensor(mk("tensor"))
            block.gpsimd(mk("gpsimd"))
        for name in list(self._cms.keys())[::-1]:
            self._cms.pop(name).__exit__(None, None, None)
        for cm in self._sem_cms[::-1]:
            cm.__exit__(None, None, None)


def build_program():
    nc = bass.Bass("TRN2", target_bir_lowering=False, debug=False)
    nc.detect_race_conditions = False
    k = K(nc)

    def din(name, shape, dtype=F32):
        return nc.dram_tensor(name, list(shape), dtype, kind="ExternalInput").ap()

    xin = din("xT_in", (D, T))
    Wqk = din("Wqk", (D, D)); bqk = din("bqk", (D,))
    Wv0 = din("Wv0", (D, D)); bv0 = din("bv0", (D,))
    Wu0 = din("Wu0", (D, D)); bu0 = din("bu0", (D,))
    Wo0 = din("Wo0", (D, D)); bo0 = din("bo0", (D,))
    ln_g0 = din("ln_g0", (D,)); ln_b0 = din("ln_b0", (D,))
    convw = din("convw", (D, 2))
    Wq1 = din("Wq1", (D, 1024)); Wk1 = din("Wk1", (D, 1024))
    Wv1 = din("Wv1", (D, D)); Wg1 = din("Wg1", (D, D))
    routw = din("routw", (16, 128, 2))
    Wout1 = din("Wout1", (D, D))
    ln_g2 = din("ln_g2", (D,)); ln_b2 = din("ln_b2", (D,))
    W_in = din("W_in", (D, 8)); b_in = din("b_in", (8,))
    dec_l = din("dec_l", (8,))
    W_out2 = din("W_out2", (8, D)); b_out2 = din("b_out2", (D,))
    D_skip = din("D_skip", (D,))
    W_down = din("W_down", (D, 32)); b_down = din("b_down", (32,))
    W_up = din("W_up", (32, D)); b_up = din("b_up", (D,))
    w1s = din("w1s", (FS, 2048), BF16)
    w2s = din("w2s", (FS, 2048), BF16)
    w3s = din("w3s", (DS3, 5632), BF16)
    ident_bf = din("ident_bf", (128, 128), BF16)
    ident_f = din("ident_f", (128, 128))
    ones_in = din("ones_in", (128,))
    ones_in_bf = din("ones_in_bf", (128,), BF16)
    g2pat = din("g2pat", (128, 2), BF16)
    hselin = din("hselin", (128, 16, 16), BF16)
    carry_m = din("carry_m", (8,))

    outT = nc.dram_tensor("outT", [D, T], F32, kind="ExternalOutput").ap()

    wqk_scr = nc.dram_tensor("wscr_qk", [16, 128, 16, 128], BF16).ap()
    wu0_scr = nc.dram_tensor("wscr_u0", [16, 128, 16, 128], BF16).ap()
    wo0_scr = nc.dram_tensor("wscr_o0", [16, 128, 16, 128], BF16).ap()
    ffn_scr = {nm: nc.dram_tensor(f"wscr_f{nm}", [nt, 128, 16, 128], BF16).ap()
               for nm, nt in [("q", 8), ("k", 8), ("v", 16), ("g", 16), ("o", 16)]}
    kv_in = nc.dram_tensor("kv_in", [128, 2048], F32).ap()
    kv_out = nc.dram_tensor("kv_out", [128, 2048], F32).ap()
    mean_in = nc.dram_tensor("mean_in", [8], F32).ap()
    mean_out = nc.dram_tensor("mean_out", [8], F32).ap()
    E_in = nc.dram_tensor("E_in", [8], F32).ap()
    E_out = nc.dram_tensor("E_out", [64], F32).ap()
    rowb = nc.dram_tensor("rowb", [2, 1024], F32).ap()
    hrowb = nc.dram_tensor("hrowb", [16, 1024], F32).ap()
    s1fl = nc.dram_tensor("s1fl", [3, FS * 32], F32).ap()
    accd = nc.dram_tensor("accd", [3, 128], F32).ap()
    deqfl = nc.dram_tensor("deqfl", [3, FS * 32], BF16).ap()
    ag_in = nc.dram_tensor("ag_in", [AGS], BF16).ap()
    ag_out = nc.dram_tensor("ag_out", [8 * AGS], BF16).ap()

    xT = k.alloc("xT", [128, DT, T], BF16, perm=True)
    cols = k.alloc("cols", [128, 16, 16], F32, perm=True)
    COL = {}
    for i, n in enumerate(["bqk", "bv0", "bu0", "bo0", "ln_g0", "ln_b0", "cw1",
                           "ln_g2", "ln_b2", "b_out2", "D_skip", "b_up"]):
        COL[n] = i
    smalls = k.alloc("smalls", [32, 4], F32, perm=True)
    identb = k.alloc("identb", [128, 128], BF16, perm=True)
    identf = k.alloc("identf", [128, 128], F32, perm=True)
    onesc = k.alloc("onesc", [128, 1], F32, perm=True)
    onescb = k.alloc("onescb", [128, 1], BF16, perm=True)
    g2p = k.alloc("g2p", [128, 2], BF16, perm=True)
    hsel = k.alloc("hsel", [128, 16, 16], BF16, perm=True)
    tmpa = k.alloc("tmpa", [128, 512], F32, perm=True)
    tmpb = k.alloc("tmpb", [128, 512], F32, perm=True)
    tmpc = k.alloc("tmpc", [128, 512], F32, perm=True)
    sq2 = k.alloc("sq2", [128, 2, 512], F32, perm=True)
    rowbuf = k.alloc("rowbuf", [1, 2, 1024], F32, perm=True)
    bcast = k.alloc("bcast", [128, 2, 1024], F32, perm=True)
    tbf = k.alloc("tbf", [128, 4, 512], BF16, perm=True)

    pKV = k.alloc("pKV", [128, 2048], F32, psum=True)
    pA = k.alloc("pA", [128, 512], F32, psum=True)
    pB = k.alloc("pB", [128, 512], F32, psum=True)
    pC = k.alloc("pC", [128, 512], F32, psum=True)
    pD = k.alloc("pD", [128, 512], F32, psum=True)

    k.dma("sync", identb[:, :], ident_bf, writes=["identb"])
    k.dma("sync", identf[:, :], ident_f, writes=["identf"])
    k.dma("sync", onesc[:, :], ones_in.rearrange("(p one) -> p one", one=1), writes=["onesc"])
    k.dma("sync", onescb[:, :], ones_in_bf.rearrange("(p one) -> p one", one=1), writes=["onescb"])
    k.dma("sync", g2p[:, :], g2pat, writes=["g2p"])
    k.dma("sync", hsel[:, :, :], hselin, writes=["hsel"])
    for n, ap_ in [("bqk", bqk), ("bv0", bv0), ("bu0", bu0), ("bo0", bo0),
                   ("ln_g0", ln_g0), ("ln_b0", ln_b0), ("ln_g2", ln_g2),
                   ("ln_b2", ln_b2), ("b_out2", b_out2), ("D_skip", D_skip),
                   ("b_up", b_up)]:
        k.dma("sync", cols[:, COL[n], :], ap_.rearrange("(j p) -> p j", p=128),
              writes=[f"col_{n}"], slow=True)
    k.dma("sync", cols[:, COL["cw1"], :],
          convw[:, 1:2].rearrange("(j p) one -> p (j one)", p=128), writes=["col_cw1"],
          slow=True)
    k.dma("sync", smalls[0:8, 0:1], b_in.rearrange("(r one) -> r one", one=1), writes=["smalls0"])
    k.dma("sync", smalls[0:32, 1:2], b_down.rearrange("(r one) -> r one", one=1), writes=["smalls1"])
    for j in range(DT):
        k.dma("gpsimd", xT[:, j, :], xin[j * 128:(j + 1) * 128, :], writes=[f"xT{j}"])

    rowslot = [0]

    def ln_stats(src_fn, src_keys_fn, n_d=DT, denom=2048.0, bf=False):
        ones_ap = onescb if bf else onesc
        ones_key = "onescb" if bf else "onesc"
        slot = rowslot[0] % 2
        rowslot[0] += 1
        rb = rowbuf[0:1, slot, :]
        rk = f"rowbuf{slot}"
        for dj in range(n_d):
            src = src_fn(dj)
            sqs = dj % 2
            k.v(lambda eng, src=src, sqs=sqs: eng.tensor_tensor(sq2[:, sqs, :], src, src, OP.mult),
                reads=src_keys_fn(dj), writes=[f"sq{sqs}"])
            k.mm(pA[0:1, :], [(ones_ap[:, :], src)],
                 reads=[ones_key] + src_keys_fn(dj), writes=["pA"],
                 start0=(dj == 0), stop=(dj == n_d - 1))
            k.mm(pB[0:1, :], [(onesc[:, :], sq2[:, sqs, :])],
                 reads=["onesc", f"sq{sqs}"], writes=["pB"],
                 start0=(dj == 0), stop=(dj == n_d - 1))
        k.v(lambda eng: eng.tensor_scalar(rb[:, 0:512], pA[0:1, :], 1.0 / denom, None, OP.mult),
            reads=["pA"], writes=[rk])
        k.v(lambda eng: eng.tensor_scalar(tmpa[0:1, 0:512], pB[0:1, :], 1.0 / denom, None, OP.mult),
            reads=["pB"], writes=["tmpa"])
        k.v(lambda eng: eng.tensor_tensor(tmpb[0:1, 0:512], rb[:, 0:512], rb[:, 0:512], OP.mult),
            reads=[rk], writes=["tmpb"])
        k.v(lambda eng: eng.scalar_tensor_tensor(tmpc[0:1, 0:512], tmpb[0:1, 0:512], -1.0,
                                                 tmpa[0:1, 0:512], OP.mult, OP.add),
            reads=["tmpa", "tmpb"], writes=["tmpc"])
        k.v(lambda eng: eng.tensor_scalar(tmpc[0:1, 0:512], tmpc[0:1, 0:512], 1e-5, None, OP.add),
            reads=["tmpc"], writes=["tmpc"])
        k.v(lambda eng: eng.reciprocal(tmpa[0:1, 0:512], tmpc[0:1, 0:512]),
            reads=["tmpc"], writes=["tmpa"])
        k.act(rb[:, 512:1024], tmpa[0:1, 0:512], AF.Sqrt, reads=["tmpa"], writes=[rk])
        k.v(lambda eng: eng.tensor_tensor(tmpa[0:1, 0:512], rb[:, 512:1024], rb[:, 512:1024], OP.mult),
            reads=[rk], writes=["tmpa"])
        k.v(lambda eng: eng.tensor_tensor(tmpa[0:1, 0:512], tmpa[0:1, 0:512], tmpc[0:1, 0:512], OP.mult),
            reads=["tmpa", "tmpc"], writes=["tmpa"])
        k.v(lambda eng: eng.tensor_scalar(tmpa[0:1, 0:512], tmpa[0:1, 0:512], -0.5, 1.5, OP.mult, OP.add),
            reads=["tmpa"], writes=["tmpa"])
        k.v(lambda eng: eng.tensor_tensor(rb[:, 512:1024], rb[:, 512:1024], tmpa[0:1, 0:512], OP.mult),
            reads=["tmpa", rk], writes=[rk])
        k.dma("sync", rowb[slot, :], rb[:, :], reads=[rk], writes=[f"rbd{slot}"])
        bc = bcast[:, slot, :]
        bkey = f"bcast{slot}"
        k.dma("sync", bc, rowb[slot:slot + 1, :].partition_broadcast(128),
              reads=[f"rbd{slot}"], writes=[bkey])
        return bc, bkey

    def early_tap():
        k.barrier()
        for dj in range(DT):
            k.dma("gpsimd", outT[dj * 128:(dj + 1) * 128, :], xT[:, dj, :],
                  reads=[f"xT{dj}"], writes=["outd"])
        k.rec("gpsimd", lambda eng: eng.nop(), reads=["outd"])
        k.finalize()
        return nc


    P4TAP = []
    _p4 = {"it": None}
    meand = nc.dram_tensor("meand", [8], F32).ap()

    def setup_p4():
        wnat = k.alloc("wnat", [128, 5632], BF16)
        s1w = k.alloc("s1w", [128, 6, 32], F32)
        s3w = k.alloc("s3w", [128, 2, 88], F32)
        meansb = k.alloc("meansb", [1, 8], F32)
        acc = k.alloc("acc", [128, 4], F32)
        sblk = k.alloc("sblk", [88, 256], F32)
        scbf = k.alloc("scbf", [88, 2, 256], BF16)
        scq = k.alloc("scq", [88, 256], F32)
        sccol = k.alloc("sccol", [88, 4], F32)
        meanc = k.alloc("meanc", [88, 1], F32)
        recn = k.alloc("recn", [128, 6, 32], F32)
        deqn = k.alloc("deqn", [128, 6, 32], BF16)
        rec3 = k.alloc("rec3", [128, 2, 88], F32)
        deq3 = k.alloc("deq3", [128, 2, 88], BF16)
        scb2 = k.alloc("scb2", [128, 2048], BF16)
        stair = k.alloc("stair", [128, 2048], BF16)
        wqT = k.alloc("wqT", [128, 2, 128], BF16)
        scb2f = k.alloc("scb2f", [128, 128], F32)
        p4a = k.alloc("p4a", [128, 512], F32)
        p4s = k.alloc("p4s", [128, 2048], BF16)

        def staircase(src_ap, rows, cols_, rkey):
            first = True
            for bmid, dlt in zip(NF4_MIDS, NF4_DELTAS):
                k.v(lambda eng, bmid=bmid, dlt=dlt: eng.tensor_scalar(
                    p4s[0:rows, 0:cols_], src_ap, bmid, dlt, OP.is_gt, OP.mult),
                    reads=[rkey], writes=["p4s"])
                if first:
                    k.v(lambda eng, rows=rows, cols_=cols_: eng.tensor_scalar(
                        stair[0:rows, 0:cols_], p4s[0:rows, 0:cols_], NF4_L0, None, OP.add),
                        reads=["p4s"], writes=["stair"])
                    first = False
                else:
                    k.v(lambda eng, rows=rows, cols_=cols_: eng.tensor_tensor(
                        stair[0:rows, 0:cols_], stair[0:rows, 0:cols_],
                        p4s[0:rows, 0:cols_], OP.add),
                        reads=["stair", "p4s"], writes=["stair"])

        def gen():
            for wi, wap in [(0, w1s), (1, w2s)]:
                for tj in range(6):
                    rows = min(128, FS - tj * 128)
                    k.dma("sync", wnat[0:rows, 0:2048], wap[tj * 128:tj * 128 + rows, :], writes=["wnat"])
                    k.v(lambda eng, tj=tj, rows=rows: eng.tensor_reduce(
                        s1w[0:rows, tj, :], wnat[0:rows, 0:2048].rearrange("p (g e) -> p g e", e=64),
                        AX.X, OP.max, apply_absolute_value=True),
                        reads=["wnat"], writes=[f"s1w{tj}"])
                    k.v(lambda eng, tj=tj, rows=rows: eng.tensor_reduce(
                        p4a[0:rows, 0:1], s1w[0:rows, tj, :], AX.X, OP.add),
                        reads=[f"s1w{tj}"], writes=["p4a"], fence=True)
                    if rows < 128:
                        k.v(lambda eng, rows=rows: eng.tensor_scalar(
                            p4a[rows:128, 0:1], onesc[rows:128, :], 0.0, None, OP.mult),
                            reads=["onesc"], writes=["p4a"])
                    if tj == 0:
                        k.v(lambda eng, wi=wi: eng.tensor_copy(acc[:, wi:wi + 1], p4a[:, 0:1]),
                            reads=["p4a"], writes=[f"acc{wi}"], fence=True)
                    else:
                        k.v(lambda eng, wi=wi: eng.tensor_tensor(acc[:, wi:wi + 1], acc[:, wi:wi + 1],
                                                                 p4a[:, 0:1], OP.add),
                            reads=[f"acc{wi}", "p4a"], writes=[f"acc{wi}"], fence=True)
                    k.dma("sync", s1fl[wi, tj * 128 * 32:(tj * 128 + rows) * 32].rearrange("(p g) -> p g", g=32),
                          s1w[0:rows, tj, :], reads=[f"s1w{tj}"], writes=[f"s1fl{wi}"])
                    yield
            for tj in range(2):
                k.dma("sync", wnat[:, :], w3s[tj * 128:(tj + 1) * 128, :], writes=["wnat"])
                k.v(lambda eng, tj=tj: eng.tensor_reduce(
                    s3w[:, tj, :], wnat[:, :].rearrange("p (g e) -> p g e", e=64),
                    AX.X, OP.max, apply_absolute_value=True),
                    reads=["wnat"], writes=[f"s3w{tj}"])
                k.v(lambda eng, tj=tj: eng.tensor_reduce(p4a[:, 0:1], s3w[:, tj, :], AX.X, OP.add),
                    reads=[f"s3w{tj}"], writes=["p4a"], fence=True)
                if tj == 0:
                    k.v(lambda eng: eng.tensor_copy(acc[:, 2:3], p4a[:, 0:1]),
                        reads=["p4a"], writes=["acc2"], fence=True)
                else:
                    k.v(lambda eng: eng.tensor_tensor(acc[:, 2:3], acc[:, 2:3], p4a[:, 0:1], OP.add),
                        reads=["acc2", "p4a"], writes=["acc2"], fence=True)
                k.dma("sync", s1fl[2, tj * 128 * 88:(tj + 1) * 128 * 88].rearrange("(p g) -> p g", g=88),
                      s3w[:, tj, :], reads=[f"s3w{tj}"], writes=["s1fl2"])
                yield
            for w in range(3):
                k.dma("sync", accd[w, :], acc[:, w:w + 1], reads=[f"acc{w}"], writes=["accd"], slow=True)
            k.dma("sync", p4a[0:1, 0:384], accd.rearrange("w p -> (w p)").rearrange("(one n) -> one n", one=1),
                  reads=["accd"], writes=["p4a"])
            k.v(lambda eng: eng.tensor_reduce(meansb[0:1, 0:3],
                                              p4a[0:1, 0:384].rearrange("p (w n) -> p w n", w=3),
                                              AX.X, OP.add),
                reads=["p4a"], writes=["meansb"])
            k.v(lambda eng: eng.tensor_scalar(meansb[0:1, 3:8], p4a[0:1, 0:5], 0.0, None, OP.mult),
                reads=["p4a"], writes=["meansb"])
            k.dma("sync", mean_in, meansb[0:1, :], reads=["meansb"], writes=["mean_in_d"])
            k.cc(lambda eng: eng.collective_compute(
                "AllReduce", OP.add, replica_groups=[list(range(NCORES))],
                ins=[mean_in.opt()], outs=[mean_out.opt()]),
                reads=["mean_in_d"], writes=["mean_out_d"])
            k.dma("sync", meansb[0:1, :], mean_out, reads=["mean_out_d"], writes=["meansb"])
            k.v(lambda eng: eng.tensor_scalar(p4s[0:1, 0:8], meansb[0:1, :], 1.0 / NSC, None, OP.mult),
                reads=["meansb"], writes=["p4s"])
            k.v(lambda eng: eng.tensor_copy(meansb[0:1, :], p4s[0:1, 0:8]), reads=["p4s"], writes=["meansb"],
                fence=True)
            k.dma("sync", meand, meansb[0:1, :], reads=["meansb"], writes=["meand"])
            yield
            for wi in range(3):
                k.dma("sync", sblk[:, :], s1fl[wi, 0:22528].rearrange("(b j) -> b j", j=256),
                      reads=[f"s1fl{wi}"], writes=["sblk"])
                k.dma("sync", meanc[:, :], meand[wi:wi + 1].unsqueeze(0).partition_broadcast(88),
                      reads=["meand"], writes=["meanc"])
                k.v(lambda eng: eng.tensor_scalar(scbf[:, 0, :], sblk[:, :], meanc[:, 0:1], None, OP.subtract),
                    reads=["sblk", "meanc"], writes=["scbf"])
                k.v(lambda eng: eng.tensor_reduce(sccol[:, 0:1], scbf[:, 0, :], AX.X, OP.max,
                                                  apply_absolute_value=True),
                    reads=["scbf"], writes=["sccol"], fence=True)
                k.v(lambda eng: eng.reciprocal(sccol[:, 3:4], sccol[:, 0:1]), reads=["sccol"], writes=["sccol"],
                    fence=True)
                k.v(lambda eng: eng.tensor_scalar(scbf[:, 1, 0:1], sccol[:, 3:4], 128.0, None, OP.mult),
                    reads=["sccol"], writes=["scbf"], fence=True)
                k.v(lambda eng: eng.tensor_copy(sccol[:, 1:2], scbf[:, 1, 0:1]), reads=["scbf"], writes=["sccol"],
                    fence=True)
                k.v(lambda eng: eng.reciprocal(sccol[:, 2:3], sccol[:, 1:2]), reads=["sccol"], writes=["sccol"],
                    fence=True)
                k.v(lambda eng: eng.tensor_scalar(scbf[:, 0, :], scbf[:, 0, :], sccol[:, 1:2], None, OP.mult),
                    reads=["scbf", "sccol"], writes=["scbf"], fence=True)
                k.v(lambda eng: eng.tensor_copy(scq[:, :], scbf[:, 0, :]), reads=["scbf"], writes=["scq"],
                    fence=True)
                k.v(lambda eng: eng.tensor_scalar(scq[:, :], scq[:, :], 1.5 * 2.0 ** 23, 1.5 * 2.0 ** 23,
                                                  OP.add, OP.subtract),
                    reads=["scq"], writes=["scq"], fence=True)
                k.v(lambda eng: eng.tensor_scalar(scq[:, :], scq[:, :], 127.0, -128.0, OP.min, OP.max),
                    reads=["scq"], writes=["scq"], fence=True)
                k.v(lambda eng: eng.tensor_scalar(scbf[:, 1, :], scq[:, :], sccol[:, 2:3], None, OP.mult),
                    reads=["scq", "sccol"], writes=["scbf"], fence=True)
                k.v(lambda eng: eng.tensor_scalar(scbf[:, 1, :], scbf[:, 1, :], meanc[:, 0:1], None, OP.add),
                    reads=["scbf", "meanc"], writes=["scbf"], fence=True)
                k.dma("sync", deqfl[wi, 0:22528].rearrange("(b j) -> b j", j=256), scbf[:, 1, :],
                      reads=["scbf"], writes=[f"deqfl{wi}"])
                if NBLOCKS == 3.15 and wi == 0:
                    k.barrier()
                    k.dma("sync", outT[100:188, 0:256], sblk[:, :], reads=["sblk"], writes=["outd"])
                    k.dma("gpsimd", outT[200:288, 0:256], scbf[:, 0, :], reads=["scbf"], writes=["outd"])
                    k.dma("sync", outT[300:388, 0:256], scq[:, :], reads=["scq"], writes=["outd"])
                    k.dma("sync", outT[400:488, 0:4], sccol[:, :], reads=["sccol"], writes=["outd"], slow=True)
                    k.dma("sync", outT[500:588, 0:1], meanc[:, :], reads=["meanc"], writes=["outd"], slow=True)
                    k.dma("gpsimd", outT[600:688, 0:256], scbf[:, 1, :], reads=["scbf"], writes=["outd"])
                    k.rec("sync", lambda eng: eng.nop(), reads=["outd"])
                    k.finalize()
                    P4TAP.append(nc)
                    return
                yield
            if NBLOCKS == 3.2:
                k.barrier()
                for w in range(3):
                    k.dma("sync", outT[11 * w:11 * w + 11, :].rearrange("a b -> (a b)")[0:22528],
                          s1fl[w, 0:22528], reads=[f"s1fl{w}"], writes=["outd"])
                    k.dma("gpsimd", outT[33 + 11 * w:44 + 11 * w, :].rearrange("a b -> (a b)")[0:22528],
                          deqfl[w, 0:22528], reads=[f"deqfl{w}"], writes=["outd"])
                k.rec("sync", lambda eng: eng.nop(), reads=["outd"])
                k.finalize()
                P4TAP.append(nc)
                return
            for wi, wap in [(0, w1s), (1, w2s)]:
                for tj in range(6):
                    rows = min(128, FS - tj * 128)
                    k.dma("sync", s1w[0:rows, tj, :],
                          s1fl[wi, tj * 128 * 32:(tj * 128 + rows) * 32].rearrange("(p g) -> p g", g=32),
                          reads=[f"s1fl{wi}"], writes=[f"s1w{tj}"])
                k.v(lambda eng: eng.tensor_scalar(s1w[64:128, 5, :], identb[64:128, 0:32], 0.0, 1.0,
                                                  OP.mult, OP.add),
                    reads=["identb"], writes=["s1w5"])
                k.v(lambda eng: eng.reciprocal(recn[:, :, :], s1w[:, :, :]),
                    reads=[f"s1w{t}" for t in range(6)], writes=["recn"], fence=True)
                for tj in range(6):
                    rows = min(128, FS - tj * 128)
                    k.dma("sync", deqn[0:rows, tj, :],
                          deqfl[wi, tj * 128 * 32:(tj * 128 + rows) * 32].rearrange("(p g) -> p g", g=32),
                          reads=[f"deqfl{wi}"], writes=["deqn"])
                yield
                for tj in range(6):
                    rows = min(128, FS - tj * 128)
                    k.dma("sync", wnat[0:rows, 0:2048], wap[tj * 128:tj * 128 + rows, :], writes=["wnat"])
                    k.v(lambda eng, tj=tj, rows=rows: eng.tensor_tensor(
                        scb2[0:rows, :].rearrange("p (g e) -> p g e", e=64),
                        wnat[0:rows, 0:2048].rearrange("p (g e) -> p g e", e=64),
                        recn[0:rows, tj, :].unsqueeze(2).broadcast_to((rows, 32, 64)), OP.mult),
                        reads=["wnat", "recn"], writes=["scb2"], fence=True)
                    staircase(scb2[0:rows, 0:2048], rows, 2048, "scb2")
                    k.v(lambda eng, tj=tj, rows=rows: eng.tensor_tensor(
                        scb2[0:rows, :].rearrange("p (g e) -> p g e", e=64),
                        stair[0:rows, 0:2048].rearrange("p (g e) -> p g e", e=64),
                        deqn[0:rows, tj, :].unsqueeze(2).broadcast_to((rows, 32, 64)), OP.mult),
                        reads=["stair", "deqn"], writes=["scb2"])
                    yield
                    for dj in range(16):
                        ws = dj % 2
                        k.v(lambda eng, dj=dj, rows=rows: eng.tensor_copy(
                            scb2f[0:rows, :], scb2[0:rows, dj * 128:(dj + 1) * 128]),
                            reads=["scb2"], writes=["scb2f"])
                        k.tp(pC[:, 0:rows], scb2f[0:rows, :],
                             identf[0:rows, 0:rows], reads=["scb2f", "identf"], writes=["pC"])
                        k.v(lambda eng, ws=ws, rows=rows: eng.tensor_copy(wqT[:, ws, 0:rows], pC[:, 0:rows]),
                            reads=["pC"], writes=[f"wqT{ws}"])
                        k.dma("sync",
                              ag_in[wi * SH12:(wi + 1) * SH12].rearrange("(d f) -> d f", f=FS)
                              [dj * 128:(dj + 1) * 128, tj * 128:tj * 128 + rows],
                              wqT[:, ws, 0:rows], reads=[f"wqT{ws}"], writes=["ag_in_d"])
                    yield
            k.v(lambda eng: eng.reciprocal(rec3[:, :, :], s3w[:, :, :]),
                reads=["s3w0", "s3w1"], writes=["rec3"], fence=True)
            for tj in range(2):
                k.dma("sync", deq3[:, tj, :],
                      deqfl[2, tj * 128 * 88:(tj + 1) * 128 * 88].rearrange("(p g) -> p g", g=88),
                      reads=["deqfl2"], writes=["deq3"])
            yield
            for tj in range(2):
                k.dma("sync", wnat[:, :], w3s[tj * 128:(tj + 1) * 128, :], writes=["wnat"])
                for cc3i in range(3):
                    c0 = cc3i * 2048
                    cw = min(2048, 5632 - c0)
                    g0 = c0 // 64
                    gw = cw // 64
                    k.v(lambda eng, tj=tj, c0=c0, cw=cw, g0=g0, gw=gw: eng.tensor_tensor(
                        scb2[:, 0:cw].rearrange("p (g e) -> p g e", e=64),
                        wnat[:, c0:c0 + cw].rearrange("p (g e) -> p g e", e=64),
                        rec3[:, tj, g0:g0 + gw].unsqueeze(2).broadcast_to((128, gw, 64)), OP.mult),
                        reads=["wnat", "rec3"], writes=["scb2"], fence=True)
                    staircase(scb2[0:128, 0:cw], 128, cw, "scb2")
                    k.v(lambda eng, tj=tj, c0=c0, cw=cw, g0=g0, gw=gw: eng.tensor_tensor(
                        scb2[:, 0:cw].rearrange("p (g e) -> p g e", e=64),
                        stair[:, 0:cw].rearrange("p (g e) -> p g e", e=64),
                        deq3[:, tj, g0:g0 + gw].unsqueeze(2).broadcast_to((128, gw, 64)), OP.mult),
                        reads=["stair", "deq3"], writes=["scb2"])
                    yield
                    for fj in range(cw // 128):
                        fg = (c0 // 128) + fj
                        ws = fg % 2
                        k.v(lambda eng, fj=fj: eng.tensor_copy(scb2f[:, :], scb2[:, fj * 128:(fj + 1) * 128]),
                            reads=["scb2"], writes=["scb2f"])
                        k.tp(pC[:, 0:128], scb2f[:, :],
                             identf[:, :], reads=["scb2f", "identf"], writes=["pC"])
                        k.v(lambda eng, ws=ws: eng.tensor_copy(wqT[:, ws, :], pC[:, 0:128]),
                            reads=["pC"], writes=[f"wqT{ws}"])
                        k.dma("sync",
                              ag_in[2 * SH12:].rearrange("(f d) -> f d", d=DS3)
                              [fg * 128:(fg + 1) * 128, tj * 128:(tj + 1) * 128],
                              wqT[:, ws, :], reads=[f"wqT{ws}"], writes=["ag_in_d"])
                    yield
            if NBLOCKS == 3.5:
                k.barrier()
                k.dma("gpsimd", outT[:, 0:704], ag_in[0:SH12].rearrange("(d f) -> d f", f=FS),
                      reads=["ag_in_d"], writes=["outd"])
                k.dma("gpsimd", outT[:, 704:1408], ag_in[SH12:2 * SH12].rearrange("(d f) -> d f", f=FS),
                      reads=["ag_in_d"], writes=["outd"])
                k.dma("gpsimd", outT[:, 1408:2048],
                      ag_in[2 * SH12:2 * SH12 + 2048 * 640].rearrange("(a b) -> a b", b=640),
                      reads=["ag_in_d"], writes=["outd"])
                k.rec("sync", lambda eng: eng.nop(), reads=["outd"])
                k.finalize()
                P4TAP.append(nc)
                return
            k.cc(lambda eng: eng.collective_compute(
                "AllGather", OP.bypass, replica_groups=[list(range(NCORES))],
                ins=[ag_in.opt()], outs=[ag_out.opt()]),
                reads=["ag_in_d"], writes=["ag_out_d"])
        _p4["it"] = gen()

    def drain_p4():
        it = _p4["it"]
        if it is None:
            return False
        try:
            next(it)
            return True
        except StopIteration:
            _p4["it"] = None
            return False

    P4_FREE = ("p4s", "p4a", "scb2f", "wqT", "stair", "scb2", "deq3", "rec3",
               "deqn", "recn", "meanc", "sccol", "scq", "scbf", "sblk", "acc",
               "meansb", "s3w", "s1w", "wnat")
    if NBLOCKS == 4:
        setup_p4()

    if NBLOCKS == 0:
        return early_tap()

    wbufA = k.alloc("wbufA", [128, 2, 16, 256], BF16)
    kvbf = k.alloc("kvbf", [128, 2048], BF16)
    wbufB = k.alloc("wbufB", [128, 2, 16, 256], BF16)
    bias_bc = k.alloc("bias_bc", [128, 2, 512], F32)
    kvsb = k.alloc("kvsb", [128, 2048], F32)

    WqkR = Wqk.rearrange("(ki p) c -> p ki c", p=128)
    Wv0R = Wv0.rearrange("(ki p) c -> p ki c", p=128)
    Wu0R = Wu0.rearrange("(ki p) c -> p ki c", p=128)
    Wo0R = Wo0.rearrange("(ki p) c -> p ki c", p=128)
    for hc in range(8):
        ws = hc % 2
        if NBLOCKS == 4:
            drain_p4()
        k.dma("gpsimd", wbufA[:, ws, :, :], WqkR[:, :, hc * 256:(hc + 1) * 256],
              writes=[f"wA{ws}"])
        k.dma("sync", wqk_scr[2 * hc, :, :, :], wbufA[:, ws, :, 0:128],
              reads=[f"wA{ws}"], writes=["wqk_scr"])
        k.dma("sync", wqk_scr[2 * hc + 1, :, :, :], wbufA[:, ws, :, 128:256],
              reads=[f"wA{ws}"], writes=["wqk_scr"])
        k.dma("gpsimd", wbufB[:, ws, :, :], Wv0R[:, :, hc * 256:(hc + 1) * 256],
              writes=[f"wB{ws}"])
        k.dma("sync", bias_bc[:, ws, 0:256],
              bqk[hc * 256:(hc + 1) * 256].unsqueeze(0).partition_broadcast(128), writes=[f"bbc{ws}"])
        k.dma("sync", bias_bc[:, ws, 256:512],
              bv0[hc * 256:(hc + 1) * 256].unsqueeze(0).partition_broadcast(128), writes=[f"bbc{ws}"])
        for tt in range(TTN):
            k.mm(pA[:, 0:256],
                 [(xT[:, ki, tt * 128:(tt + 1) * 128], wbufA[:, ws, ki, :]) for ki in range(16)],
                 reads=[f"xT{ki}" for ki in range(16)] + [f"wA{ws}"], writes=["pA"])
            k.mm(pB[:, 0:256],
                 [(xT[:, ki, tt * 128:(tt + 1) * 128], wbufB[:, ws, ki, :]) for ki in range(16)],
                 reads=[f"xT{ki}" for ki in range(16)] + [f"wB{ws}"], writes=["pB"])
            k.v(lambda eng, ws=ws: eng.tensor_tensor(tmpa[:, 0:256], pA[:, 0:256], bias_bc[:, ws, 0:256], OP.add),
                reads=["pA", f"bbc{ws}"], writes=["tmpa"])
            k.v(lambda eng: eng.tensor_scalar(tmpb[:, 0:256], tmpa[:, 0:256], 0.0, None, OP.min),
                reads=["tmpa"], writes=["tmpb"])
            k.act(tmpc[:, 0:256], tmpb[:, 0:256], AF.Exp, reads=["tmpb"], writes=["tmpc"])
            k.v(lambda eng: eng.tensor_scalar(tmpa[:, 0:256], tmpa[:, 0:256], 0.0, None, OP.max),
                reads=["tmpa"], writes=["tmpa"])
            k.v(lambda eng: eng.scalar_tensor_tensor(tbf[:, 0, 0:256], tmpc[:, 0:256], -1.0,
                                                     tmpa[:, 0:256], OP.add, OP.add),
                reads=["tmpa", "tmpc"], writes=["tbf0"])
            k.v(lambda eng, ws=ws: eng.tensor_tensor(tmpb[:, 0:256], pB[:, 0:256], bias_bc[:, ws, 256:512], OP.add),
                reads=["pB", f"bbc{ws}"], writes=["tmpb"])
            k.act(tbf[:, 1, 0:256], tmpb[:, 0:256], AF.Silu, reads=["tmpb"], writes=["tbf1"])
            for hh in range(2):
                h = hc * 2 + hh
                col = (h % 4) * 512 + (h // 4) * 128
                k.mm(pKV[:, col:col + 128],
                     [(tbf[:, 0, hh * 128:(hh + 1) * 128], tbf[:, 1, hh * 128:(hh + 1) * 128])],
                     reads=["tbf0", "tbf1"], writes=[f"pKV{h % 4}"],
                     start0=(tt == 0), stop=(tt == TTN - 1))
    for b4 in range(4):
        k.v(lambda eng, b4=b4: eng.tensor_copy(kvsb[:, b4 * 512:(b4 + 1) * 512], pKV[:, b4 * 512:(b4 + 1) * 512]),
            reads=[f"pKV{b4}"], writes=["kvsb"])
    k.dma("sync", kv_in, kvsb[:, :], reads=["kvsb"], writes=["kv_in_d"])
    if NBLOCKS == 0.5:
        return early_tap()
    k.cc(lambda eng: eng.collective_compute(
        "AllReduce", OP.add, replica_groups=[list(range(NCORES))],
        ins=[kv_in.opt()], outs=[kv_out.opt()]),
        reads=["kv_in_d"], writes=["kv_out_d"])
    k.dma("gpsimd", kvbf[:, :], kv_out, reads=["kv_out_d"], writes=["kvbf"])
    if NBLOCKS == 0.75:
        return early_tap()

    k.barrier()
    k.free("kvsb", "bias_bc", "wbufB")
    ych1 = k.alloc("ych1", [128, DT, 512], BF16)
    zbuf = k.alloc("zbuf", [128, 16, 512], BF16)
    ubuf = k.alloc("ubuf", [128, 512], BF16)

    for tc in range(TCN):
        for h in range(16):
            ws = h % 2
            col = (h % 4) * 512 + (h // 4) * 128
            if NBLOCKS == 4:
                drain_p4()
            k.dma("sync" if ws == 0 else "gpsimd", wbufA[:, ws, :, 0:128],
                  wqk_scr[h, :, :, :], reads=["wqk_scr"], writes=[f"wA{ws}"])
            k.mm(pA[:, :],
                 [(wbufA[:, ws, ki, 0:128], xT[:, ki, tc * 512:(tc + 1) * 512]) for ki in range(16)],
                 reads=[f"wA{ws}"] + [f"xT{ki}" for ki in range(16)], writes=["pA"])
            k.v(lambda eng, h=h: eng.tensor_scalar(tmpa[:, :], pA[:, :],
                                                   cols[:, COL["bqk"], h:h + 1], None, OP.add),
                reads=["pA", "col_bqk"], writes=["tmpa"])
            k.v(lambda eng: eng.tensor_scalar(tmpb[:, :], tmpa[:, :], 0.0, None, OP.min),
                reads=["tmpa"], writes=["tmpb"])
            k.act(tmpc[:, :], tmpb[:, :], AF.Exp, reads=["tmpb"], writes=["tmpc"])
            k.v(lambda eng: eng.tensor_scalar(tmpa[:, :], tmpa[:, :], 0.0, None, OP.max),
                reads=["tmpa"], writes=["tmpa"])
            k.v(lambda eng: eng.scalar_tensor_tensor(tbf[:, 0, :], tmpc[:, :], -1.0,
                                                     tmpa[:, :], OP.add, OP.add),
                reads=["tmpa", "tmpc"], writes=["tbf0"])
            if NBLOCKS == 0.78:
                continue
            k.mm(pD[:, :], [(kvbf[:, col:col + 128], tbf[:, 0, :])],
                 reads=["kvbf", "tbf0"], writes=["pD"])
            if NBLOCKS == 0.79:
                continue
            k.v(lambda eng, h=h: eng.tensor_copy(ych1[:, h, :], pD[:, :]),
                reads=["pD"], writes=[f"y{h}"])
        if NBLOCKS in (0.78, 0.79, 0.8):
            return early_tap()
        bc, bkey = ln_stats(lambda dj: ych1[:, dj, :], lambda dj: [f"y{dj}"], bf=True)
        if NBLOCKS == 0.85:
            return early_tap()
        for dj in range(DT):
            ws = dj % 2
            if NBLOCKS == 4:
                drain_p4()
            if tc == 0:
                k.dma("gpsimd", wbufA[:, ws, :, 0:128],
                      Wu0R[:, :, dj * 128:(dj + 1) * 128], writes=[f"wA{ws}"])
                k.dma("sync", wu0_scr[dj, :, :, :], wbufA[:, ws, :, 0:128],
                      reads=[f"wA{ws}"], writes=["wu0_scr"])
            else:
                k.dma("sync" if ws == 0 else "gpsimd", wbufA[:, ws, :, 0:128],
                      wu0_scr[dj, :, :, :], reads=["wu0_scr"], writes=[f"wA{ws}"])
            k.mm(pA[:, :],
                 [(wbufA[:, ws, ki, 0:128], xT[:, ki, tc * 512:(tc + 1) * 512]) for ki in range(16)],
                 reads=[f"wA{ws}"] + [f"xT{ki}" for ki in range(16)], writes=["pA"])
            k.act(ubuf[:, :], pA[:, :], AF.Silu, bias=cols[:, COL["bu0"], dj:dj + 1],
                  reads=["pA", "col_bu0"], writes=["ubuf"])
            k.v(lambda eng, dj=dj, bc=bc: eng.tensor_tensor(tmpa[:, :], ych1[:, dj, :], bc[:, 0:512], OP.subtract),
                reads=[f"y{dj}", bkey], writes=["tmpa"])
            k.v(lambda eng, dj=dj, bc=bc: eng.scalar_tensor_tensor(
                tmpa[:, :], tmpa[:, :], cols[:, COL["ln_g0"], dj:dj + 1], bc[:, 512:1024], OP.mult, OP.mult),
                reads=["tmpa", bkey, "col_ln_g0"], writes=["tmpa"])
            k.v(lambda eng, dj=dj: eng.scalar_tensor_tensor(
                zbuf[:, dj, :], tmpa[:, :], cols[:, COL["ln_b0"], dj:dj + 1], ubuf[:, :], OP.add, OP.mult),
                reads=["tmpa", "ubuf", "col_ln_b0"], writes=[f"z{dj}"])
        for dj in range(DT):
            ws = dj % 2
            if NBLOCKS == 4:
                drain_p4()
            if tc == 0:
                k.dma("gpsimd", wbufA[:, ws, :, 0:128],
                      Wo0R[:, :, dj * 128:(dj + 1) * 128], writes=[f"wA{ws}"])
                k.dma("sync", wo0_scr[dj, :, :, :], wbufA[:, ws, :, 0:128],
                      reads=[f"wA{ws}"], writes=["wo0_scr"])
            else:
                k.dma("sync" if ws == 0 else "gpsimd", wbufA[:, ws, :, 0:128],
                      wo0_scr[dj, :, :, :], reads=["wo0_scr"], writes=[f"wA{ws}"])
            k.mm(pB[:, :],
                 [(wbufA[:, ws, ki, 0:128], zbuf[:, ki, :]) for ki in range(16)],
                 reads=[f"wA{ws}"] + [f"z{ki}" for ki in range(16)], writes=["pB"])
            k.v(lambda eng, dj=dj: eng.tensor_scalar(tmpb[:, :], pB[:, :],
                                                     cols[:, COL["bo0"], dj:dj + 1], None, OP.add),
                reads=["pB", "col_bo0"], writes=["tmpb"])
            k.v(lambda eng, dj=dj, tc=tc: eng.tensor_tensor(
                ych1[:, dj, :], tmpb[:, :], xT[:, dj, tc * 512:(tc + 1) * 512], OP.add),
                reads=["tmpb", f"xT{dj}"], writes=[f"y{dj}"])
        bc2, bk2 = ln_stats(lambda dj: ych1[:, dj, :], lambda dj: [f"y{dj}"], bf=True)
        for dj in range(DT):
            k.v(lambda eng, dj=dj, bc2=bc2: eng.tensor_tensor(tmpa[:, :], ych1[:, dj, :], bc2[:, 0:512], OP.subtract),
                reads=[f"y{dj}", bk2], writes=["tmpa"])
            k.v(lambda eng, dj=dj, tc=tc, bc2=bc2: eng.tensor_tensor(
                xT[:, dj, tc * 512:(tc + 1) * 512], tmpa[:, :], bc2[:, 512:1024], OP.mult),
                reads=["tmpa", bk2], writes=[f"xT{dj}"])

    def finish_tap():
        k.barrier()
        for dj in range(DT):
            k.dma("gpsimd", outT[dj * 128:(dj + 1) * 128, :], xT[:, dj, :],
                  reads=[f"xT{dj}"], writes=["outd"])
        k.rec("gpsimd", lambda eng: eng.nop(), reads=["outd"])
        k.finalize()
        return nc

    if NBLOCKS == 1:
        return finish_tap()

    if NBLOCKS == 4:
        while drain_p4():
            pass
    k.barrier(wait_cc=False)
    k.free("ubuf", "zbuf", "ych1", "kvbf")
    if NBLOCKS == 4:
        k.free("wbufA")
        k.free(*P4_FREE)
        wbufA = k.alloc("wbufA", [128, 2, 16, 256], BF16)

    xcb = k.alloc("xcb", [128, 16, 512], BF16)
    qkz = k.alloc("qkz", [128, 16, 512], BF16)
    vb4 = k.alloc("vb4", [128, 16, 512], BF16)
    gb4 = k.alloc("gb4", [128, 16, 512], BF16)
    rwsb = k.alloc("rwsb", [128, 16, 2], BF16)
    dotsb = k.alloc("dotsb", [128, 4, 16], F32)
    logsb = k.alloc("logsb", [128, 4, 32], F32)
    csb = k.alloc("csb", [128, 4, 16], F32)
    cTsb = k.alloc("cTsb", [16, 512], F32)
    statro = k.alloc("statro", [16, 1024], F32)
    hbc = k.alloc("hbc", [128, 1024], F32)
    for h in range(16):
        k.dma("gpsimd", rwsb[:, h, :], routw[h, :, :], writes=["rwsb"])

    for tc in range(TCN):
        for j in range(DT):
            k.act(xcb[:, j, :], xT[:, j, tc * 512:(tc + 1) * 512], AF.Silu,
                  scale=cols[:, COL["cw1"], j:j + 1], reads=[f"xT{j}", "col_cw1"], writes=[f"xcb{j}"])
        for Wap, ntile, nm in [(Wq1, 8, "q"), (Wk1, 8, "k"), (Wv1, 16, "v"), (Wg1, 16, "g")]:
            WapR = Wap.rearrange("(ki p) c -> p ki c", p=128)
            for dj in range(ntile):
                ws = dj % 2
                if tc == 0:
                    k.dma("gpsimd", wbufA[:, ws, :, 0:128],
                          WapR[:, :, dj * 128:(dj + 1) * 128], writes=[f"wA{ws}"])
                    k.dma("sync", ffn_scr[nm][dj, :, :, :], wbufA[:, ws, :, 0:128],
                          reads=[f"wA{ws}"], writes=[f"scr_{nm}"])
                else:
                    k.dma("sync" if ws == 0 else "gpsimd", wbufA[:, ws, :, 0:128],
                          ffn_scr[nm][dj, :, :, :], reads=[f"scr_{nm}"], writes=[f"wA{ws}"])
                k.mm(pA[:, :],
                     [(wbufA[:, ws, ki, 0:128], xcb[:, ki, :]) for ki in range(16)],
                     reads=[f"wA{ws}"] + [f"xcb{ki}" for ki in range(16)], writes=["pA"])
                if nm == "q":
                    k.act(qkz[:, dj, :], pA[:, :], AF.Silu, reads=["pA"], writes=[f"z{dj}"])
                elif nm == "k":
                    k.v(lambda eng, dj=dj: eng.tensor_copy(qkz[:, 8 + dj, :], pA[:, :]),
                        reads=["pA"], writes=[f"z{8 + dj}"])
                elif nm == "v":
                    k.v(lambda eng, dj=dj: eng.tensor_copy(vb4[:, dj, :], pA[:, :]),
                        reads=["pA"], writes=[f"fv{dj}"])
                else:
                    k.act(gb4[:, dj, :], pA[:, :], AF.Silu, reads=["pA"], writes=[f"fg{dj}"])
        pCv = pC[:, :].rearrange("p (s c) -> p s c", s=4)
        pDv = pD[:, :].rearrange("p (s c) -> p s c", s=4)
        for j in range(8):
            k.v(lambda eng, j=j: eng.tensor_tensor(tbf[:, 0, :], qkz[:, j, :], qkz[:, 8 + j, :], OP.mult),
                reads=[f"z{j}", f"z{8 + j}"], writes=["tbf0"])
            for s4 in range(4):
                k.mm(pCv[:, s4, 2 * j:2 * j + 2],
                     [(tbf[:, 0, s4 * 128:(s4 + 1) * 128], g2p[:, :])],
                     reads=["tbf0", "g2p"], writes=["pC"])
        k.v(lambda eng: eng.tensor_copy(dotsb[:, :, :], pCv[:, :, 0:16]), reads=["pC"], writes=["dotsb"])
        for h in range(16):
            for s4 in range(4):
                k.mm(pDv[:, s4, 2 * h:2 * h + 2],
                     [(vb4[:, h, s4 * 128:(s4 + 1) * 128], rwsb[:, h, :])],
                     reads=[f"fv{h}", "rwsb"], writes=["pD"])
        k.v(lambda eng: eng.tensor_copy(logsb[:, :, :], pDv[:, :, 0:32]), reads=["pD"], writes=["logsb"])
        lgv = logsb[:, :, :].rearrange("p s (h two) -> p s h two", two=2)
        k.v(lambda eng: eng.tensor_tensor(csb[:, :, :], lgv[:, :, :, 1], lgv[:, :, :, 0], OP.subtract),
            reads=["logsb"], writes=["csb"])
        k.v(lambda eng: eng.tensor_scalar(tmpa[:, 0:64].rearrange("p (s h) -> p s h", s=4),
                                          csb[:, :, :], -1.0, None, OP.mult),
            reads=["csb"], writes=["tmpa"])
        k.v(lambda eng: eng.tensor_tensor(csb[:, :, :], csb[:, :, :],
                                          tmpa[:, 0:64].rearrange("p (s h) -> p s h", s=4), OP.max),
            reads=["csb", "tmpa"], writes=["csb"])
        k.act(csb[:, :, :], csb[:, :, :], AF.Sigmoid, reads=["csb"], writes=["csb"])
        k.v(lambda eng: eng.scalar_tensor_tensor(csb[:, :, :], csb[:, :, :], 0.125,
                                                 dotsb[:, :, :], OP.mult, OP.mult),
            reads=["csb", "dotsb"], writes=["csb"])
        for s4 in range(4):
            k.tp(pC[0:16, 0:128], csb[:, s4, :], identf[:, :], reads=["csb", "identf"], writes=["pC"])
            k.v(lambda eng, s4=s4: eng.tensor_copy(cTsb[:, s4 * 128:(s4 + 1) * 128], pC[0:16, 0:128]),
                reads=["pC"], writes=["cTsb"])
        for h in range(16):
            k.mm(pA[0:16, :], [(hsel[:, h, :], vb4[:, h, :])],
                 reads=["hsel", f"fv{h}"], writes=["pA"],
                 start0=(h == 0), stop=(h == 15))
            k.v(lambda eng, h=h: eng.tensor_tensor(tbf[:, 0, :], vb4[:, h, :], vb4[:, h, :], OP.mult),
                reads=[f"fv{h}"], writes=["tbf0"])
            k.mm(pB[0:16, :], [(hsel[:, h, :], tbf[:, 0, :])],
                 reads=["hsel", "tbf0"], writes=["pB"],
                 start0=(h == 0), stop=(h == 15))
        k.v(lambda eng: eng.tensor_scalar(statro[:, 0:512], pA[0:16, :], 1.0 / 128.0, None, OP.mult),
            reads=["pA"], writes=["statro"])
        k.v(lambda eng: eng.tensor_scalar(statro[:, 512:1024], pB[0:16, :], 1.0 / 128.0, None, OP.mult),
            reads=["pB"], writes=["statro"])
        k.v(lambda eng: eng.tensor_tensor(sq2[0:16, 0, :], statro[:, 0:512], statro[:, 0:512], OP.mult),
            reads=["statro"], writes=["sq0"])
        k.v(lambda eng: eng.scalar_tensor_tensor(sq2[0:16, 0, :], sq2[0:16, 0, :], -1.0,
                                                 statro[:, 512:1024], OP.mult, OP.add),
            reads=["sq0", "statro"], writes=["sq0"])
        k.v(lambda eng: eng.tensor_tensor(tmpa[0:16, :], cTsb[:, :], cTsb[:, :], OP.mult),
            reads=["cTsb"], writes=["tmpa"])
        k.v(lambda eng: eng.tensor_tensor(tmpa[0:16, :], tmpa[0:16, :], sq2[0:16, 0, :], OP.mult),
            reads=["tmpa", "sq0"], writes=["tmpa"])
        k.v(lambda eng: eng.tensor_scalar(tmpa[0:16, :], tmpa[0:16, :], 1e-5, None, OP.add),
            reads=["tmpa"], writes=["tmpa"])
        k.v(lambda eng: eng.reciprocal(tmpb[0:16, :], tmpa[0:16, :]), reads=["tmpa"], writes=["tmpb"])
        k.act(tmpc[0:16, :], tmpb[0:16, :], AF.Sqrt, reads=["tmpb"], writes=["tmpc"])
        k.v(lambda eng: eng.tensor_tensor(tmpb[0:16, :], tmpc[0:16, :], tmpc[0:16, :], OP.mult),
            reads=["tmpc"], writes=["tmpb"])
        k.v(lambda eng: eng.tensor_tensor(tmpb[0:16, :], tmpb[0:16, :], tmpa[0:16, :], OP.mult),
            reads=["tmpb", "tmpa"], writes=["tmpb"])
        k.v(lambda eng: eng.tensor_scalar(tmpb[0:16, :], tmpb[0:16, :], -0.5, 1.5, OP.mult, OP.add),
            reads=["tmpb"], writes=["tmpb"])
        k.v(lambda eng: eng.tensor_tensor(tmpc[0:16, :], tmpc[0:16, :], tmpb[0:16, :], OP.mult),
            reads=["tmpb", "tmpc"], writes=["tmpc"])
        k.v(lambda eng: eng.tensor_tensor(statro[:, 512:1024], cTsb[:, :], tmpc[0:16, :], OP.mult),
            reads=["cTsb", "tmpc"], writes=["statro"])
        k.dma("sync", hrowb, statro[:, :], reads=["statro"], writes=["hrb"])
        for h in range(16):
            k.dma("sync", hbc[:, :], hrowb[h:h + 1, :].partition_broadcast(128),
                  reads=["hrb"], writes=["hbc"])
            k.v(lambda eng, h=h: eng.tensor_tensor(tmpa[:, :], vb4[:, h, :], hbc[:, 0:512], OP.subtract),
                reads=[f"fv{h}", "hbc"], writes=["tmpa"])
            k.v(lambda eng, h=h: eng.tensor_tensor(tmpa[:, :], tmpa[:, :], hbc[:, 512:1024], OP.mult),
                reads=["tmpa", "hbc"], writes=["tmpa"])
            k.v(lambda eng, h=h: eng.tensor_tensor(qkz[:, h, :], tmpa[:, :], gb4[:, h, :], OP.mult),
                reads=["tmpa", f"fg{h}"], writes=[f"z{h}"])
        for dj in range(DT):
            ws = dj % 2
            if tc == 0:
                k.dma("gpsimd", wbufA[:, ws, :, 0:128],
                      Wout1.rearrange("(ki p) c -> p ki c", p=128)[:, :, dj * 128:(dj + 1) * 128],
                      writes=[f"wA{ws}"])
                k.dma("sync", ffn_scr["o"][dj, :, :, :], wbufA[:, ws, :, 0:128],
                      reads=[f"wA{ws}"], writes=["scr_o"])
            else:
                k.dma("sync" if ws == 0 else "gpsimd", wbufA[:, ws, :, 0:128],
                      ffn_scr["o"][dj, :, :, :], reads=["scr_o"], writes=[f"wA{ws}"])
            k.mm(pB[:, :],
                 [(wbufA[:, ws, ki, 0:128], qkz[:, ki, :]) for ki in range(16)],
                 reads=[f"wA{ws}"] + [f"z{ki}" for ki in range(16)], writes=["pB"])
            k.v(lambda eng, dj=dj, tc=tc: eng.tensor_tensor(
                vb4[:, dj, :], pB[:, :], xT[:, dj, tc * 512:(tc + 1) * 512], OP.add),
                reads=["pB", f"xT{dj}"], writes=[f"fv{dj}"])
        bc3, bk3 = ln_stats(lambda dj: vb4[:, dj, :], lambda dj: [f"fv{dj}"], bf=True)
        for dj in range(DT):
            k.v(lambda eng, dj=dj, bc3=bc3: eng.tensor_tensor(tmpa[:, :], vb4[:, dj, :], bc3[:, 0:512], OP.subtract),
                reads=[f"fv{dj}", bk3], writes=["tmpa"])
            k.v(lambda eng, dj=dj, tc=tc, bc3=bc3: eng.tensor_tensor(
                xT[:, dj, tc * 512:(tc + 1) * 512], tmpa[:, :], bc3[:, 512:1024], OP.mult),
                reads=["tmpa", bk3], writes=[f"xT{dj}"])

    if NBLOCKS == 2:
        return finish_tap()

    k.barrier()
    k.free("hbc", "statro", "cTsb", "csb", "logsb", "dotsb", "rwsb",
           "gb4", "vb4", "qkz", "xcb", "wbufA")

    xnc = k.alloc("xnc", [128, 16, 512], BF16)
    u2 = k.alloc("u2", [8, T], F32)
    win8 = k.alloc("win8", [128, 16, 8], BF16)
    dec8 = k.alloc("dec8", [8, 6], F32)
    k.dma("sync", dec8[0:8, 5:6], dec_l.rearrange("(r one) -> r one", one=1), writes=["dec_lg"])
    for ki in range(16):
        k.dma("gpsimd", win8[:, ki, :], W_in[ki * 128:(ki + 1) * 128, :], writes=["win8"])
    k.act(dec8[0:8, 0:1], dec8[0:8, 5:6], AF.Sigmoid, reads=["dec_lg"], writes=["dec8a"])
    k.act(dec8[0:8, 1:2], dec8[0:8, 0:1], AF.Ln, reads=["dec8a"], writes=["dec8b"], fence=True)
    k.v(lambda eng: eng.tensor_scalar(dec8[0:8, 2:3], dec8[0:8, 1:2], 2048.0, None, OP.mult),
        reads=["dec8b"], writes=["dec8c"])

    def emit_xn(tc):
        bc, bkey = ln_stats(lambda dj, tc=tc: xT[:, dj, tc * 512:(tc + 1) * 512],
                            lambda dj: [f"xT{dj}"], bf=True)
        for dj in range(DT):
            k.v(lambda eng, dj=dj, tc=tc, bc=bc: eng.tensor_tensor(
                tmpa[:, :], xT[:, dj, tc * 512:(tc + 1) * 512], bc[:, 0:512], OP.subtract),
                reads=[f"xT{dj}", bkey], writes=["tmpa"])
            k.v(lambda eng, dj=dj, bc=bc: eng.scalar_tensor_tensor(
                tmpa[:, :], tmpa[:, :], cols[:, COL["ln_g2"], dj:dj + 1], bc[:, 512:1024], OP.mult, OP.mult),
                reads=["tmpa", bkey, "col_ln_g2"], writes=["tmpa"])
            k.v(lambda eng, dj=dj: eng.tensor_scalar(
                xnc[:, dj, :], tmpa[:, :], cols[:, COL["ln_b2"], dj:dj + 1], None, OP.add),
                reads=["tmpa", "col_ln_b2"], writes=[f"xn{dj}"])

    for tc in range(TCN):
        emit_xn(tc)
        k.mm(pA[0:8, :],
             [(win8[:, ki, :], xnc[:, ki, :]) for ki in range(16)],
             reads=["win8"] + [f"xn{ki}" for ki in range(16)], writes=["pA"])
        k.v(lambda eng, tc=tc: eng.tensor_scalar(u2[:, tc * 512:(tc + 1) * 512], pA[0:8, :],
                                                 smalls[0:8, 0:1], None, OP.add),
            reads=["pA", "smalls0"], writes=["u2"])
    if NBLOCKS == 2.3:
        return finish_tap()
    hbf = k.alloc("hbf", [8, T], BF16)
    scanw = k.alloc("scanw", [8, T], F32)
    dectl = k.alloc("dectl", [8, T], F32)
    aux8 = k.alloc("aux8", [8, T], F32)
    eall = k.alloc("eall", [8, 8], F32)
    carr = k.alloc("carr", [8, 8], F32)
    k.dma("sync", carr[:, :], carry_m.unsqueeze(0).partition_broadcast(8), writes=["carr"])
    k.v(lambda eng: eng.tensor_scalar(aux8[:, :], u2[:, :], 0.0, 1.0, OP.mult, OP.add),
        reads=["u2"], writes=["aux8"], fence=True)
    k.v(lambda eng: eng.tensor_scalar(dectl[:, :], aux8[:, :], dec8[0:8, 0:1], None, OP.mult),
        reads=["aux8", "dec8a"], writes=["dectl"], fence=True)
    k.v(lambda eng: eng.tensor_tensor_scan(scanw[:, :], dectl[:, :], u2[:, :], 0.0, OP.mult, OP.add),
        reads=["dectl", "u2"], writes=["scanw"], fence=True)
    k.dma("sync", E_in, scanw[:, 2047:2048], reads=["scanw"], writes=["E_in_d"])
    k.cc(lambda eng: eng.collective_compute(
        "AllGather", OP.bypass, replica_groups=[list(range(NCORES))],
        ins=[E_in.opt()], outs=[E_out.opt()]),
        reads=["E_in_d"], writes=["E_out_d"])
    k.dma("sync", eall[:, :], E_out.rearrange("(c r) -> r c", r=8), reads=["E_out_d"],
          writes=["eall"], slow=True)
    k.act(carr[:, :], carr[:, :], AF.Exp, scale=dec8[0:8, 2:3], reads=["carr", "dec8c"], writes=["carr"])
    k.v(lambda eng: eng.tensor_tensor(carr[:, :], carr[:, :], eall[:, :], OP.mult),
        reads=["carr", "eall"], writes=["carr"])
    k.v(lambda eng: eng.reduce_sum(dec8[0:8, 3:4], carr[:, :], axis=AX.X),
        reads=["carr"], writes=["dec8d"], fence=True)
    k.v(lambda eng: eng.tensor_tensor_scan(dectl[:, :], aux8[:, :], aux8[:, :], 0.0, OP.mult, OP.add),
        reads=["aux8"], writes=["dectl"], fence=True)
    k.act(dectl[:, :], dectl[:, :], AF.Exp, scale=dec8[0:8, 1:2], reads=["dectl", "dec8b"], writes=["dectl"])
    k.v(lambda eng: eng.scalar_tensor_tensor(scanw[:, :], dectl[:, :], dec8[0:8, 3:4], scanw[:, :],
                                             OP.mult, OP.add),
        reads=["dectl", "dec8d", "scanw"], writes=["scanw"], fence=True)
    k.v(lambda eng: eng.tensor_copy(hbf[:, :], scanw[:, :]), reads=["scanw"], writes=["hbf"], fence=True)

    if NBLOCKS == 2.5:
        return finish_tap()

    k.barrier()
    k.free("carr", "eall", "aux8", "dectl", "scanw")
    ych3 = k.alloc("ychunkB", [128, DT, 512], F32)
    hf2 = k.alloc("hf2", [32, T], BF16)
    wsm = k.alloc("wsm", [32, 2, D], BF16)
    wdn = k.alloc("wdn", [128, 16, 32], BF16)
    k.dma("gpsimd", wsm[0:8, 0, :], W_out2, writes=["wsm0"])
    k.dma("gpsimd", wsm[0:32, 1, :], W_up, writes=["wsm1"])
    for ki in range(16):
        k.dma("gpsimd", wdn[:, ki, :], W_down[ki * 128:(ki + 1) * 128, :], writes=["wdn"])

    for tc in range(TCN):
        emit_xn(tc)
        for dj in range(DT):
            k.mm(pA[:, :], [(wsm[0:8, 0, dj * 128:(dj + 1) * 128], hbf[:, tc * 512:(tc + 1) * 512])],
                 reads=["wsm0", "hbf"], writes=["pA"])
            k.v(lambda eng, dj=dj: eng.tensor_scalar(tmpb[:, :], pA[:, :],
                                                     cols[:, COL["b_out2"], dj:dj + 1], None, OP.add),
                reads=["pA", "col_b_out2"], writes=["tmpb"])
            k.v(lambda eng, dj=dj: eng.scalar_tensor_tensor(
                tmpb[:, :], xnc[:, dj, :], cols[:, COL["D_skip"], dj:dj + 1], tmpb[:, :], OP.mult, OP.add),
                reads=[f"xn{dj}", "col_D_skip", "tmpb"], writes=["tmpb"])
            k.v(lambda eng, dj=dj, tc=tc: eng.tensor_tensor(
                ych3[:, dj, :], tmpb[:, :], xT[:, dj, tc * 512:(tc + 1) * 512], OP.add),
                reads=["tmpb", f"xT{dj}"], writes=[f"y{dj}"])
        if NBLOCKS == 2.6:
            return finish_tap()
        bc, bkey = ln_stats(lambda dj: ych3[:, dj, :], lambda dj: [f"y{dj}"])
        for dj in range(DT):
            k.v(lambda eng, dj=dj, bc=bc: eng.tensor_tensor(tmpa[:, :], ych3[:, dj, :], bc[:, 0:512], OP.subtract),
                reads=[f"y{dj}", bkey], writes=["tmpa"])
            k.v(lambda eng, dj=dj, bc=bc: eng.scalar_tensor_tensor(
                tmpa[:, :], tmpa[:, :], cols[:, COL["ln_g2"], dj:dj + 1], bc[:, 512:1024], OP.mult, OP.mult),
                reads=["tmpa", bkey, "col_ln_g2"], writes=["tmpa"])
            k.v(lambda eng, dj=dj: eng.tensor_scalar(
                xnc[:, dj, :], tmpa[:, :], cols[:, COL["ln_b2"], dj:dj + 1], None, OP.add),
                reads=["tmpa", "col_ln_b2"], writes=[f"xn{dj}"])
        if NBLOCKS == 2.65:
            return finish_tap()
        k.mm(pB[0:32, :],
             [(wdn[:, ki, :], xnc[:, ki, :]) for ki in range(16)],
             reads=["wdn"] + [f"xn{ki}" for ki in range(16)], writes=["pB"])
        k.act(hf2[:, tc * 512:(tc + 1) * 512], pB[0:32, :], AF.Relu, bias=smalls[0:32, 1:2],
              reads=["pB", "smalls1"], writes=["hf2"])
        for dj in range(DT):
            k.mm(pC[:, :], [(wsm[0:32, 1, dj * 128:(dj + 1) * 128], hf2[:, tc * 512:(tc + 1) * 512])],
                 reads=["wsm1", "hf2"], writes=["pC"])
            k.v(lambda eng, dj=dj: eng.tensor_scalar(tmpb[:, :], pC[:, :],
                                                     cols[:, COL["b_up"], dj:dj + 1], None, OP.add),
                reads=["pC", "col_b_up"], writes=["tmpb"])
            k.v(lambda eng, dj=dj: eng.tensor_tensor(tmpb[:, :], tmpb[:, :], ych3[:, dj, :], OP.add),
                reads=["tmpb", f"y{dj}"], writes=["tmpb"])
            k.v(lambda eng, dj=dj, tc=tc: eng.tensor_tensor(
                ych3[:, dj, :], tmpb[:, :], xT[:, dj, tc * 512:(tc + 1) * 512], OP.add),
                reads=["tmpb", f"xT{dj}"], writes=[f"y{dj}"])
        if NBLOCKS == 2.7:
            return finish_tap()
        bc2, bk2 = ln_stats(lambda dj: ych3[:, dj, :], lambda dj: [f"y{dj}"])
        for dj in range(DT):
            k.v(lambda eng, dj=dj, bc2=bc2: eng.tensor_tensor(tmpa[:, :], ych3[:, dj, :], bc2[:, 0:512], OP.subtract),
                reads=[f"y{dj}", bk2], writes=["tmpa"])
            k.v(lambda eng, dj=dj, tc=tc, bc2=bc2: eng.tensor_tensor(
                xT[:, dj, tc * 512:(tc + 1) * 512], tmpa[:, :], bc2[:, 512:1024], OP.mult),
                reads=["tmpa", bk2], writes=[f"xT{dj}"])
        if NBLOCKS == 2.75:
            return finish_tap()

    if NBLOCKS == 3:
        return finish_tap()

    k.barrier()
    k.free("wdn", "wsm", "hf2", "ychunkB", "hbf", "dec8", "win8", "u2", "xnc")

    if NBLOCKS != 4:
        setup_p4()
        while drain_p4():
            pass
        if P4TAP:
            return P4TAP[0]
        k.barrier()
        k.free(*P4_FREE)

    hch = k.alloc("hch", [128, 22, 512], BF16)
    wbufA = k.alloc("wbufA_f", [128, 2, 16, 128], BF16)
    wbufB = k.alloc("wbufB_f", [128, 2, 16, 128], BF16)
    w3st = k.alloc("w3st", [128, 2, 22, 128], BF16)
    ych4 = k.alloc("ychunkC", [128, DT, 512], F32)

    def w12_stripe(wi, fi, ws, wbuf, wkey):
        f0g = fi * 128
        segs = []
        f = f0g
        while f < f0g + 128:
            cc2 = f // FS
            take = min((cc2 + 1) * FS - f, f0g + 128 - f)
            segs.append((cc2, f - cc2 * FS, f - f0g, take))
            f += take
        for (cc2, floc, foff, take) in segs:
            base = cc2 * AGS + wi * SH12
            k.dma("sync" if wi == 0 else "gpsimd", wbuf[:, ws, :, foff:foff + take],
                  ag_out[base:base + SH12].rearrange("(ki p f) -> p ki f", p=128, f=FS)
                  [:, :, floc:floc + take],
                  reads=["ag_out_d"], writes=[wkey])

    for tc in range(TCN):
        for half in range(2):
            for fo in range(22):
                fi = half * 22 + fo
                ws = fi % 2
                w12_stripe(0, fi, ws, wbufA, f"wSa{ws}")
                w12_stripe(1, fi, ws, wbufB, f"wSb{ws}")
                k.mm(pA[:, :],
                     [(wbufA[:, ws, ki, :], xT[:, ki, tc * 512:(tc + 1) * 512]) for ki in range(16)],
                     reads=[f"wSa{ws}"] + [f"xT{ki}" for ki in range(16)], writes=["pA"])
                k.mm(pB[:, :],
                     [(wbufB[:, ws, ki, :], xT[:, ki, tc * 512:(tc + 1) * 512]) for ki in range(16)],
                     reads=[f"wSb{ws}"] + [f"xT{ki}" for ki in range(16)], writes=["pB"])
                k.act(tmpa[:, :], pA[:, :], AF.Silu, reads=["pA"], writes=["tmpa"])
                k.v(lambda eng, fo=fo: eng.tensor_tensor(hch[:, fo, :], tmpa[:, :], pB[:, :], OP.mult),
                    reads=["tmpa", "pB"], writes=[f"h{fo}"])
            for dj in range(DT):
                cc3 = dj // 2
                dloc = (dj % 2) * 128
                ws = dj % 2
                base = cc3 * AGS + 2 * SH12
                k.dma("sync" if ws == 0 else "gpsimd", w3st[:, ws, :, :],
                      ag_out[base:base + SH3].rearrange("(fi p d) -> p fi d", p=128, d=DS3)
                      [:, half * 22:half * 22 + 22, dloc:dloc + 128],
                      reads=["ag_out_d"], writes=[f"w3st{ws}"])
                k.mm(pC[:, :],
                     [(w3st[:, ws, fo, :], hch[:, fo, :]) for fo in range(22)],
                     reads=[f"w3st{ws}"] + [f"h{fo}" for fo in range(22)], writes=["pC"])
                if half == 0:
                    k.v(lambda eng, dj=dj: eng.tensor_copy(ych4[:, dj, :], pC[:, :]),
                        reads=["pC"], writes=[f"y{dj}"])
                else:
                    k.v(lambda eng, dj=dj, tc=tc: eng.scalar_tensor_tensor(
                        tmpb[:, :], ych4[:, dj, :], 1.0, pC[:, :], OP.mult, OP.add),
                        reads=["pC", f"y{dj}"], writes=["tmpb"])
                    k.v(lambda eng, dj=dj, tc=tc: eng.tensor_tensor(
                        ych4[:, dj, :], tmpb[:, :], xT[:, dj, tc * 512:(tc + 1) * 512], OP.add),
                        reads=["tmpb", f"xT{dj}"], writes=[f"y{dj}"])
        bc, bkey = ln_stats(lambda dj: ych4[:, dj, :], lambda dj: [f"y{dj}"])
        for dj in range(DT):
            k.v(lambda eng, dj=dj, bc=bc: eng.tensor_tensor(tmpa[:, :], ych4[:, dj, :], bc[:, 0:512], OP.subtract),
                reads=[f"y{dj}", bkey], writes=["tmpa"])
            k.v(lambda eng, dj=dj, bc=bc: eng.tensor_tensor(tmpc[:, :], tmpa[:, :], bc[:, 512:1024], OP.mult),
                reads=["tmpa", bkey], writes=["tmpc"])
            k.dma("sync", outT[dj * 128:(dj + 1) * 128, tc * 512:(tc + 1) * 512], tmpc[:, :],
                  reads=["tmpc"], writes=["outd"])
    k.rec("sync", lambda eng: eng.nop(), reads=["outd"])
    k.finalize()
    return nc


_CACHE = {}


def kernel(L0_attn_x, L0_attn_Wqk, L0_attn_bqk, L0_attn_Wv, L0_attn_bv,
           L0_attn_Wu, L0_attn_bu, L0_attn_Wo, L0_attn_bo, L0_attn_ln_g,
           L0_attn_ln_b, L0_ffn_conv_w, L0_ffn_Wq, L0_ffn_Wk, L0_ffn_Wv,
           L0_ffn_Wg, L0_ffn_router_w, L0_ffn_Wout, L1_attn_ln_gamma,
           L1_attn_ln_beta, L1_attn_W_in, L1_attn_b_in, L1_attn_decay_logit,
           L1_attn_W_out, L1_attn_b_out, L1_attn_D_skip, L1_attn_W_down,
           L1_attn_b_down, L1_attn_W_up, L1_attn_b_up, L1_ffn_weight1,
           L1_ffn_weight2, L1_ffn_weight3):
    bf = ml_dtypes.bfloat16
    f32 = np.float32
    if "nc" not in _CACHE:
        _CACHE["nc"] = build_program()
    nc = _CACHE["nc"]

    x = np.asarray(L0_attn_x, dtype=f32).reshape(16384, 2048)
    w1 = np.asarray(L1_ffn_weight1).astype(bf, copy=False)
    w2 = np.asarray(L1_ffn_weight2).astype(bf, copy=False)
    w3 = np.asarray(L1_ffn_weight3).astype(bf, copy=False)

    shared = {
        "Wqk": np.asarray(L0_attn_Wqk, f32), "bqk": np.asarray(L0_attn_bqk, f32),
        "Wv0": np.asarray(L0_attn_Wv, f32), "bv0": np.asarray(L0_attn_bv, f32),
        "Wu0": np.asarray(L0_attn_Wu, f32), "bu0": np.asarray(L0_attn_bu, f32),
        "Wo0": np.asarray(L0_attn_Wo, f32), "bo0": np.asarray(L0_attn_bo, f32),
        "ln_g0": np.asarray(L0_attn_ln_g, f32), "ln_b0": np.asarray(L0_attn_ln_b, f32),
        "convw": np.asarray(L0_ffn_conv_w, f32),
        "Wq1": np.asarray(L0_ffn_Wq, f32), "Wk1": np.asarray(L0_ffn_Wk, f32),
        "Wv1": np.asarray(L0_ffn_Wv, f32), "Wg1": np.asarray(L0_ffn_Wg, f32),
        "routw": np.asarray(L0_ffn_router_w, f32),
        "Wout1": np.asarray(L0_ffn_Wout, f32),
        "ln_g2": np.asarray(L1_attn_ln_gamma, f32), "ln_b2": np.asarray(L1_attn_ln_beta, f32),
        "W_in": np.asarray(L1_attn_W_in, f32), "b_in": np.asarray(L1_attn_b_in, f32),
        "dec_l": np.asarray(L1_attn_decay_logit, f32),
        "W_out2": np.asarray(L1_attn_W_out, f32), "b_out2": np.asarray(L1_attn_b_out, f32),
        "D_skip": np.asarray(L1_attn_D_skip, f32),
        "W_down": np.asarray(L1_attn_W_down, f32), "b_down": np.asarray(L1_attn_b_down, f32),
        "W_up": np.asarray(L1_attn_W_up, f32), "b_up": np.asarray(L1_attn_b_up, f32),
        "ident_bf": np.eye(128, dtype=bf),
        "ident_f": np.eye(128, dtype=f32),
        "ones_in": np.ones(128, f32),
        "ones_in_bf": np.ones(128, bf),
        "g2pat": np.concatenate([np.repeat(np.float32([1, 0])[None], 64, 0),
                                 np.repeat(np.float32([0, 1])[None], 64, 0)], 0).astype(bf),
        "hselin": np.broadcast_to(np.eye(16, dtype=bf)[None], (128, 16, 16)).copy(),
    }
    in_maps = []
    for c in range(NCORES):
        m = dict(shared)
        m["xT_in"] = np.ascontiguousarray(x[c * 2048:(c + 1) * 2048, :].T)
        m["w1s"] = np.ascontiguousarray(w1[c * FS:(c + 1) * FS, :])
        m["w2s"] = np.ascontiguousarray(w2[c * FS:(c + 1) * FS, :])
        m["w3s"] = np.ascontiguousarray(w3[c * DS3:(c + 1) * DS3, :])
        cm = np.full(8, 1e9, f32)
        for cc in range(c):
            cm[cc] = float(c - 1 - cc)
        m["carry_m"] = cm
        in_maps.append(m)
    res = run_bass_kernel_spmd(nc, in_maps, core_ids=list(range(NCORES)))
    out = np.empty((16384, 2048), f32)
    for c in range(NCORES):
        out[c * 2048:(c + 1) * 2048, :] = res.results[c]["outT"].T
    return out.reshape(1, 16384, 2048)


# version 13
# speedup vs baseline: 1.0731x; 1.0271x over previous
import numpy as np
import ml_dtypes
import concourse.bass as bass
import concourse.mybir as mybir
from concourse.bass_utils import run_bass_kernel_spmd

F32 = mybir.dt.float32
BF16 = mybir.dt.bfloat16
AF = mybir.ActivationFunctionType
OP = mybir.AluOpType
AX = mybir.AxisListType

NCORES = 8
T = 2048
D = 2048
DT = 16
TTN = 16
TCN = 4
FS = 704
DS3 = 256
SH12 = 2048 * FS
SH3 = 5632 * DS3
AGS = 2 * SH12 + SH3
NSC = float(5632 * 2048 // 64)

NBLOCKS = 4

_LV = np.array([-1.0, -0.6962, -0.5251, -0.3949, -0.2844, -0.1848, -0.0911,
                0.0, 0.0796, 0.1609, 0.2461, 0.3379, 0.4407, 0.5626, 0.723,
                1.0], dtype=np.float32).astype(ml_dtypes.bfloat16).astype(np.float64)
NF4_MIDS = [float(x) for x in (_LV[:-1] + _LV[1:]) / 2.0]
NF4_DELTAS = [float(x) for x in np.diff(_LV)]
NF4_L0 = float(_LV[0])

ENGS = ("sync", "scalar", "vector", "tensor", "gpsimd")

ARENA_OFF = {}


NDCH = 6


class _Op:
    __slots__ = ("eng", "fn", "is_async", "step", "marked", "syncval", "waits",
                 "ridx", "sem", "chval", "pre_waits")

    def __init__(self, eng, fn, is_async, step):
        self.eng = eng
        self.fn = fn
        self.is_async = is_async
        self.step = step
        self.marked = is_async
        self.syncval = None
        self.waits = []
        self.ridx = 0
        self.sem = None
        self.chval = 0
        self.pre_waits = []


class K:
    def __init__(self, nc):
        self.nc = nc
        self.ops = {e: [] for e in ENGS}
        self.lastw = {}
        self.readers = {}
        self._cms = {}
        self._sem_cms = []
        self.sems = {}
        for e in ENGS:
            cm = nc.semaphore("sem_" + e)
            self.sems[e] = cm.__enter__()
            self._sem_cms.append(cm)
        self.dsem = {}
        self.dcnt = {}
        for q in ("sync", "scalar", "gpsimd"):
            self.dsem[q] = []
            self.dcnt[q] = []
            for i in range(NDCH):
                cm = nc.semaphore(f"dma_{q}_{i}")
                self.dsem[q].append(cm.__enter__())
                self._sem_cms.append(cm)
                self.dcnt[q].append(0)
        cm = nc.semaphore("sem_cc")
        self.ccsem = cm.__enter__()
        self._sem_cms.append(cm)
        self.cccnt = 0
        self._barn = 0
        self._ridx = 0
        self._arena = None
        self._astack = []
        self._atop = 0
        self._dmaidx = {q: 0 for q in ("sync", "scalar", "gpsimd")}

    ARENA_B = 92 * 1024

    def alloc(self, name, shape, dtype, psum=False, perm=False):
        if psum or perm:
            cm = (self.nc.psum_tensor if psum else self.nc.sbuf_tensor)(name, list(shape), dtype)
            t = cm.__enter__()
            self._cms[name] = cm
            return t
        if self._arena is None:
            cm = self.nc.sbuf_tensor("ARENA", [128, self.ARENA_B // 2], BF16)
            self._arena = cm.__enter__()
            self._cms["__ARENA"] = cm
        esz = 4 if dtype == F32 else 2
        nel = 1
        for s in shape[1:]:
            nel *= s
        nbytes = nel * esz
        nbytes_al = (nbytes + 3) // 4 * 4
        off = self._atop
        self._atop = off + nbytes_al
        assert self._atop <= self.ARENA_B, (name, self._atop)
        ARENA_OFF[name] = (off, tuple(shape), dtype)
        view = self._arena[0:shape[0], off // 2:(off + nbytes) // 2]
        if dtype != BF16:
            view = view.bitcast(dtype)
        if len(shape) == 3:
            view = view.rearrange("p (a b) -> p a b", a=shape[1])
        elif len(shape) == 4:
            view = view.rearrange("p (a b c) -> p a b c", a=shape[1], b=shape[2])
        self._astack.append((name, off))
        return view

    def free(self, *names):
        for n in names:
            nm, off = self._astack.pop()
            assert nm == n, (nm, n)
            self._atop = off

    def _dep(self, a, b, force=False):
        if a is None or a is b:
            return
        if (not force) and (a.eng == b.eng) and not a.is_async:
            return
        a.marked = True
        b.waits.append(a)

    def rec(self, eng, fn, reads=(), writes=(), is_async=False, step=1, fence=False):
        op = _Op(eng, fn, is_async, step)
        op.ridx = self._ridx
        self._ridx += 1
        for key in reads:
            self._dep(self.lastw.get(key), op, force=fence)
        for key in writes:
            self._dep(self.lastw.get(key), op)
            for r in self.readers.get(key, ()):
                self._dep(r, op)
        for key in reads:
            self.readers.setdefault(key, []).append(op)
        for key in writes:
            self.lastw[key] = op
            self.readers[key] = []
        self.ops[eng].append(op)
        return op

    def _rec_dma(self, queue, fn, reads, writes):
        ch = self._dmaidx[queue] % NDCH
        self._dmaidx[queue] += 1
        op = self.rec(queue, fn, reads, writes, is_async=True, step=16)
        op.sem = self.dsem[queue][ch]
        prev = self.dcnt[queue][ch]
        if prev:
            op.pre_waits.append((op.sem, prev))
        self.dcnt[queue][ch] = prev + 16
        op.chval = prev + 16
        return op

    def barrier(self, wait_cc=True):
        n = self._barn
        self._barn += 1
        for e in ENGS:
            op = self.rec(e, lambda eng: eng.drain(), writes=[f"__bar{n}_{e}"])
            op.marked = True
            if e in self.dsem:
                for i in range(NDCH):
                    if self.dcnt[e][i]:
                        op.pre_waits.append((self.dsem[e][i], self.dcnt[e][i]))
            if e == "gpsimd" and self.cccnt and wait_cc:
                op.pre_waits.append((self.ccsem, self.cccnt))
        for e in ENGS:
            self.rec(e, lambda eng: eng.nop(),
                     reads=[f"__bar{n}_{f}" for f in ENGS if f != e])

    def dma(self, queue, out, in_, reads=(), writes=(), slow=False):
        def fn(eng):
            return eng.dma_start(out=out, in_=in_, allow_slow_non_contiguous=slow)
        return self._rec_dma(queue, fn, reads, writes)

    def dmat(self, queue, out, in_, reads=(), writes=()):
        def fn(eng):
            return eng.dma_start_transpose(out, in_)
        return self._rec_dma(queue, fn, reads, writes)

    def cc(self, fn, reads, writes):
        op = self.rec("gpsimd", fn, reads, writes, is_async=True, step=1)
        op.sem = self.ccsem
        self.cccnt += 1
        op.chval = self.cccnt
        return op

    def mm(self, out_ap, pairs, reads, writes, start0=True, stop=True):
        def fn(eng):
            last = None
            n = len(pairs)
            for i, (lhsT, rhs) in enumerate(pairs):
                last = eng.matmul(out_ap, lhsT, rhs,
                                  start=(start0 and i == 0),
                                  stop=(stop and i == n - 1))
            return last
        return self.rec("tensor", fn, reads, writes)

    def tp(self, out_ap, in_ap, ident, reads, writes):
        def fn(eng):
            return eng.transpose(out_ap, in_ap, ident)
        return self.rec("tensor", fn, reads, writes)

    def act(self, out_ap, in_ap, func, reads, writes, bias=0.0, scale=1.0, fence=False):
        def fn(eng):
            return eng.activation(out_ap, in_ap, func, bias=bias, scale=scale)
        return self.rec("scalar", fn, reads, writes, fence=fence)

    def v(self, fn, reads, writes, fence=False):
        return self.rec("vector", fn, reads, writes, fence=fence)

    def finalize(self):
        for e in ENGS:
            c = 0
            for op in self.ops[e]:
                if op.marked and not op.is_async:
                    c += op.step
                    op.syncval = c
                    op.sem = self.sems[e]
                    op.chval = c

        def mk(e):
            def body(eng):
                hi = {}
                for op in self.ops[e]:
                    tgt = {}
                    for a in op.waits:
                        key = a.sem.name
                        if a.chval > tgt.get(key, (None, 0))[1]:
                            tgt[key] = (a.sem, a.chval)
                    for key, (s, v) in tgt.items():
                        if hi.get(key, 0) < v:
                            hi[key] = v
                            eng.wait_ge(s, v)
                    for (s, v) in op.pre_waits:
                        if hi.get(s.name, 0) < v:
                            hi[s.name] = v
                            eng.wait_ge(s, v)
                    inst = op.fn(eng)
                    if op.marked:
                        inst.then_inc(op.sem, op.step)
            return body
        with self.nc.Block() as block:
            block.sync(mk("sync"))
            block.scalar(mk("scalar"))
            block.vector(mk("vector"))
            block.tensor(mk("tensor"))
            block.gpsimd(mk("gpsimd"))
        for name in list(self._cms.keys())[::-1]:
            self._cms.pop(name).__exit__(None, None, None)
        for cm in self._sem_cms[::-1]:
            cm.__exit__(None, None, None)


def build_program():
    nc = bass.Bass("TRN2", target_bir_lowering=False, debug=False)
    nc.detect_race_conditions = False
    k = K(nc)

    def din(name, shape, dtype=F32):
        return nc.dram_tensor(name, list(shape), dtype, kind="ExternalInput").ap()

    xin = din("xT_in", (D, T))
    Wqk = din("Wqk", (D, D)); bqk = din("bqk", (D,))
    Wv0 = din("Wv0", (D, D)); bv0 = din("bv0", (D,))
    Wu0 = din("Wu0", (D, D)); bu0 = din("bu0", (D,))
    Wo0 = din("Wo0", (D, D)); bo0 = din("bo0", (D,))
    ln_g0 = din("ln_g0", (D,)); ln_b0 = din("ln_b0", (D,))
    convw = din("convw", (D, 2))
    Wq1 = din("Wq1", (D, 1024)); Wk1 = din("Wk1", (D, 1024))
    Wv1 = din("Wv1", (D, D)); Wg1 = din("Wg1", (D, D))
    routw = din("routw", (16, 128, 2))
    Wout1 = din("Wout1", (D, D))
    ln_g2 = din("ln_g2", (D,)); ln_b2 = din("ln_b2", (D,))
    W_in = din("W_in", (D, 8)); b_in = din("b_in", (8,))
    dec_l = din("dec_l", (8,))
    W_out2 = din("W_out2", (8, D)); b_out2 = din("b_out2", (D,))
    D_skip = din("D_skip", (D,))
    W_down = din("W_down", (D, 32)); b_down = din("b_down", (32,))
    W_up = din("W_up", (32, D)); b_up = din("b_up", (D,))
    w1s = din("w1s", (FS, 2048), BF16)
    w2s = din("w2s", (FS, 2048), BF16)
    w3s = din("w3s", (DS3, 5632), BF16)
    ident_bf = din("ident_bf", (128, 128), BF16)
    ident_f = din("ident_f", (128, 128))
    ones_in = din("ones_in", (128,))
    ones_in_bf = din("ones_in_bf", (128,), BF16)
    g2pat = din("g2pat", (128, 2), BF16)
    hselin = din("hselin", (128, 16, 16), BF16)
    carry_m = din("carry_m", (8,))

    outT = nc.dram_tensor("outT", [D, T], F32, kind="ExternalOutput").ap()

    wqk_scr = nc.dram_tensor("wscr_qk", [16, 128, 16, 128], BF16).ap()
    wu0_scr = nc.dram_tensor("wscr_u0", [16, 128, 16, 128], BF16).ap()
    wo0_scr = nc.dram_tensor("wscr_o0", [16, 128, 16, 128], BF16).ap()
    ffn_scr = {nm: nc.dram_tensor(f"wscr_f{nm}", [nt, 128, 16, 128], BF16).ap()
               for nm, nt in [("q", 8), ("k", 8), ("v", 16), ("g", 16), ("o", 16)]}
    kv_in = nc.dram_tensor("kv_in", [128, 2048], F32).ap()
    kv_out = nc.dram_tensor("kv_out", [128, 2048], F32).ap()
    mean_in = nc.dram_tensor("mean_in", [8], F32).ap()
    mean_out = nc.dram_tensor("mean_out", [8], F32).ap()
    E_in = nc.dram_tensor("E_in", [8], F32).ap()
    E_out = nc.dram_tensor("E_out", [64], F32).ap()
    rowb = nc.dram_tensor("rowb", [2, 1024], F32).ap()
    hrowb = nc.dram_tensor("hrowb", [16, 1024], F32).ap()
    s1fl = nc.dram_tensor("s1fl", [3, FS * 32], F32).ap()
    accd = nc.dram_tensor("accd", [3, 128], F32).ap()
    deqfl = nc.dram_tensor("deqfl", [3, FS * 32], BF16).ap()
    ag_in = nc.dram_tensor("ag_in", [AGS], BF16).ap()
    ag_out = nc.dram_tensor("ag_out", [8 * AGS], BF16).ap()

    xT = k.alloc("xT", [128, DT, T], BF16, perm=True)
    cols = k.alloc("cols", [128, 16, 16], F32, perm=True)
    COL = {}
    for i, n in enumerate(["bqk", "bv0", "bu0", "bo0", "ln_g0", "ln_b0", "cw1",
                           "ln_g2", "ln_b2", "b_out2", "D_skip", "b_up"]):
        COL[n] = i
    smalls = k.alloc("smalls", [32, 4], F32, perm=True)
    identb = k.alloc("identb", [128, 128], BF16, perm=True)
    identf = k.alloc("identf", [128, 128], F32, perm=True)
    onesc = k.alloc("onesc", [128, 1], F32, perm=True)
    onescb = k.alloc("onescb", [128, 1], BF16, perm=True)
    g2p = k.alloc("g2p", [128, 2], BF16, perm=True)
    hsel = k.alloc("hsel", [128, 16, 16], BF16, perm=True)
    tmpa = k.alloc("tmpa", [128, 512], F32, perm=True)
    tmpb = k.alloc("tmpb", [128, 512], F32, perm=True)
    tmpc = k.alloc("tmpc", [128, 512], F32, perm=True)
    sq2 = k.alloc("sq2", [128, 2, 512], F32, perm=True)
    rowbuf = k.alloc("rowbuf", [1, 2, 1024], F32, perm=True)
    bcast = k.alloc("bcast", [128, 2, 1024], F32, perm=True)
    tbf = k.alloc("tbf", [128, 4, 512], BF16, perm=True)

    pKV = k.alloc("pKV", [128, 2048], F32, psum=True)
    pA = k.alloc("pA", [128, 512], F32, psum=True)
    pB = k.alloc("pB", [128, 512], F32, psum=True)
    pC = k.alloc("pC", [128, 512], F32, psum=True)
    pD = k.alloc("pD", [128, 512], F32, psum=True)

    k.dma("sync", identb[:, :], ident_bf, writes=["identb"])
    k.dma("sync", identf[:, :], ident_f, writes=["identf"])
    k.dma("sync", onesc[:, :], ones_in.rearrange("(p one) -> p one", one=1), writes=["onesc"])
    k.dma("sync", onescb[:, :], ones_in_bf.rearrange("(p one) -> p one", one=1), writes=["onescb"])
    k.dma("sync", g2p[:, :], g2pat, writes=["g2p"])
    k.dma("sync", hsel[:, :, :], hselin, writes=["hsel"])
    for n, ap_ in [("bqk", bqk), ("bv0", bv0), ("bu0", bu0), ("bo0", bo0),
                   ("ln_g0", ln_g0), ("ln_b0", ln_b0), ("ln_g2", ln_g2),
                   ("ln_b2", ln_b2), ("b_out2", b_out2), ("D_skip", D_skip),
                   ("b_up", b_up)]:
        k.dma("sync", cols[:, COL[n], :], ap_.rearrange("(j p) -> p j", p=128),
              writes=[f"col_{n}"], slow=True)
    k.dma("sync", cols[:, COL["cw1"], :],
          convw[:, 1:2].rearrange("(j p) one -> p (j one)", p=128), writes=["col_cw1"],
          slow=True)
    k.dma("sync", smalls[0:8, 0:1], b_in.rearrange("(r one) -> r one", one=1), writes=["smalls0"])
    k.dma("sync", smalls[0:32, 1:2], b_down.rearrange("(r one) -> r one", one=1), writes=["smalls1"])
    for j in range(DT):
        k.dma("gpsimd", xT[:, j, :], xin[j * 128:(j + 1) * 128, :], writes=[f"xT{j}"])


    def wb4_view():
        off, _, _ = ARENA_OFF["wbufA"]
        return k._arena[0:128, off // 2:(off + 16384) // 2].rearrange(
            "p (s a b) -> p s a b", s=4, a=16)

    rowslot = [0]

    def ln_stats(src_fn, src_keys_fn, n_d=DT, denom=2048.0, bf=False):
        ones_ap = onescb if bf else onesc
        ones_key = "onescb" if bf else "onesc"
        slot = rowslot[0] % 2
        rowslot[0] += 1
        rb = rowbuf[0:1, slot, :]
        rk = f"rowbuf{slot}"
        for dj in range(n_d):
            src = src_fn(dj)
            sqs = dj % 2
            k.v(lambda eng, src=src, sqs=sqs: eng.tensor_tensor(sq2[:, sqs, :], src, src, OP.mult),
                reads=src_keys_fn(dj), writes=[f"sq{sqs}"])
            k.mm(pA[0:1, :], [(ones_ap[:, :], src)],
                 reads=[ones_key] + src_keys_fn(dj), writes=["pA"],
                 start0=(dj == 0), stop=(dj == n_d - 1))
            k.mm(pB[0:1, :], [(onesc[:, :], sq2[:, sqs, :])],
                 reads=["onesc", f"sq{sqs}"], writes=["pB"],
                 start0=(dj == 0), stop=(dj == n_d - 1))
        k.v(lambda eng: eng.tensor_scalar(rb[:, 0:512], pA[0:1, :], 1.0 / denom, None, OP.mult),
            reads=["pA"], writes=[rk])
        k.v(lambda eng: eng.tensor_scalar(tmpa[0:1, 0:512], pB[0:1, :], 1.0 / denom, None, OP.mult),
            reads=["pB"], writes=["tmpa"])
        k.v(lambda eng: eng.tensor_tensor(tmpb[0:1, 0:512], rb[:, 0:512], rb[:, 0:512], OP.mult),
            reads=[rk], writes=["tmpb"])
        k.v(lambda eng: eng.scalar_tensor_tensor(tmpc[0:1, 0:512], tmpb[0:1, 0:512], -1.0,
                                                 tmpa[0:1, 0:512], OP.mult, OP.add),
            reads=["tmpa", "tmpb"], writes=["tmpc"])
        k.v(lambda eng: eng.tensor_scalar(tmpc[0:1, 0:512], tmpc[0:1, 0:512], 1e-5, None, OP.add),
            reads=["tmpc"], writes=["tmpc"])
        k.v(lambda eng: eng.reciprocal(tmpa[0:1, 0:512], tmpc[0:1, 0:512]),
            reads=["tmpc"], writes=["tmpa"])
        k.act(rb[:, 512:1024], tmpa[0:1, 0:512], AF.Sqrt, reads=["tmpa"], writes=[rk])
        k.v(lambda eng: eng.tensor_tensor(tmpa[0:1, 0:512], rb[:, 512:1024], rb[:, 512:1024], OP.mult),
            reads=[rk], writes=["tmpa"])
        k.v(lambda eng: eng.tensor_tensor(tmpa[0:1, 0:512], tmpa[0:1, 0:512], tmpc[0:1, 0:512], OP.mult),
            reads=["tmpa", "tmpc"], writes=["tmpa"])
        k.v(lambda eng: eng.tensor_scalar(tmpa[0:1, 0:512], tmpa[0:1, 0:512], -0.5, 1.5, OP.mult, OP.add),
            reads=["tmpa"], writes=["tmpa"])
        k.v(lambda eng: eng.tensor_tensor(rb[:, 512:1024], rb[:, 512:1024], tmpa[0:1, 0:512], OP.mult),
            reads=["tmpa", rk], writes=[rk])
        k.dma("sync", rowb[slot, :], rb[:, :], reads=[rk], writes=[f"rbd{slot}"])
        bc = bcast[:, slot, :]
        bkey = f"bcast{slot}"
        k.dma("sync", bc, rowb[slot:slot + 1, :].partition_broadcast(128),
              reads=[f"rbd{slot}"], writes=[bkey])
        return bc, bkey

    def early_tap():
        k.barrier()
        for dj in range(DT):
            k.dma("gpsimd", outT[dj * 128:(dj + 1) * 128, :], xT[:, dj, :],
                  reads=[f"xT{dj}"], writes=["outd"])
        k.rec("gpsimd", lambda eng: eng.nop(), reads=["outd"])
        k.finalize()
        return nc


    P4TAP = []
    _p4 = {"it": None}
    meand = nc.dram_tensor("meand", [8], F32).ap()

    def setup_p4():
        wnat = k.alloc("wnat", [128, 5632], BF16)
        s1w = k.alloc("s1w", [128, 6, 32], F32)
        s3w = k.alloc("s3w", [128, 2, 88], F32)
        meansb = k.alloc("meansb", [1, 8], F32)
        acc = k.alloc("acc", [128, 4], F32)
        sblk = k.alloc("sblk", [88, 256], F32)
        scbf = k.alloc("scbf", [88, 2, 256], BF16)
        scq = k.alloc("scq", [88, 256], F32)
        sccol = k.alloc("sccol", [88, 4], F32)
        meanc = k.alloc("meanc", [88, 1], F32)
        recn = k.alloc("recn", [128, 6, 32], F32)
        deqn = k.alloc("deqn", [128, 6, 32], BF16)
        rec3 = k.alloc("rec3", [128, 2, 88], F32)
        deq3 = k.alloc("deq3", [128, 2, 88], BF16)
        scb2 = k.alloc("scb2", [128, 2048], BF16)
        stair = k.alloc("stair", [128, 2048], BF16)
        wqT = k.alloc("wqT", [128, 2, 128], BF16)
        scb2f = k.alloc("scb2f", [128, 128], F32)
        p4a = k.alloc("p4a", [128, 512], F32)
        p4s = k.alloc("p4s", [128, 2048], BF16)

        def staircase(src_ap, rows, cols_, rkey):
            first = True
            for bmid, dlt in zip(NF4_MIDS, NF4_DELTAS):
                k.v(lambda eng, bmid=bmid, dlt=dlt: eng.tensor_scalar(
                    p4s[0:rows, 0:cols_], src_ap, bmid, dlt, OP.is_gt, OP.mult),
                    reads=[rkey], writes=["p4s"])
                if first:
                    k.v(lambda eng, rows=rows, cols_=cols_: eng.tensor_scalar(
                        stair[0:rows, 0:cols_], p4s[0:rows, 0:cols_], NF4_L0, None, OP.add),
                        reads=["p4s"], writes=["stair"])
                    first = False
                else:
                    k.v(lambda eng, rows=rows, cols_=cols_: eng.tensor_tensor(
                        stair[0:rows, 0:cols_], stair[0:rows, 0:cols_],
                        p4s[0:rows, 0:cols_], OP.add),
                        reads=["stair", "p4s"], writes=["stair"])

        def gen():
            for wi, wap in [(0, w1s), (1, w2s)]:
                for tj in range(6):
                    rows = min(128, FS - tj * 128)
                    k.dma("sync", wnat[0:rows, 0:2048], wap[tj * 128:tj * 128 + rows, :], writes=["wnat"])
                    k.v(lambda eng, tj=tj, rows=rows: eng.tensor_reduce(
                        s1w[0:rows, tj, :], wnat[0:rows, 0:2048].rearrange("p (g e) -> p g e", e=64),
                        AX.X, OP.max, apply_absolute_value=True),
                        reads=["wnat"], writes=[f"s1w{tj}"])
                    k.v(lambda eng, tj=tj, rows=rows: eng.tensor_reduce(
                        p4a[0:rows, 0:1], s1w[0:rows, tj, :], AX.X, OP.add),
                        reads=[f"s1w{tj}"], writes=["p4a"], fence=True)
                    if rows < 128:
                        k.v(lambda eng, rows=rows: eng.tensor_scalar(
                            p4a[rows:128, 0:1], onesc[rows:128, :], 0.0, None, OP.mult),
                            reads=["onesc"], writes=["p4a"])
                    if tj == 0:
                        k.v(lambda eng, wi=wi: eng.tensor_copy(acc[:, wi:wi + 1], p4a[:, 0:1]),
                            reads=["p4a"], writes=[f"acc{wi}"], fence=True)
                    else:
                        k.v(lambda eng, wi=wi: eng.tensor_tensor(acc[:, wi:wi + 1], acc[:, wi:wi + 1],
                                                                 p4a[:, 0:1], OP.add),
                            reads=[f"acc{wi}", "p4a"], writes=[f"acc{wi}"], fence=True)
                    k.dma("sync", s1fl[wi, tj * 128 * 32:(tj * 128 + rows) * 32].rearrange("(p g) -> p g", g=32),
                          s1w[0:rows, tj, :], reads=[f"s1w{tj}"], writes=[f"s1fl{wi}"])
                    yield
            for tj in range(2):
                k.dma("sync", wnat[:, :], w3s[tj * 128:(tj + 1) * 128, :], writes=["wnat"])
                k.v(lambda eng, tj=tj: eng.tensor_reduce(
                    s3w[:, tj, :], wnat[:, :].rearrange("p (g e) -> p g e", e=64),
                    AX.X, OP.max, apply_absolute_value=True),
                    reads=["wnat"], writes=[f"s3w{tj}"])
                k.v(lambda eng, tj=tj: eng.tensor_reduce(p4a[:, 0:1], s3w[:, tj, :], AX.X, OP.add),
                    reads=[f"s3w{tj}"], writes=["p4a"], fence=True)
                if tj == 0:
                    k.v(lambda eng: eng.tensor_copy(acc[:, 2:3], p4a[:, 0:1]),
                        reads=["p4a"], writes=["acc2"], fence=True)
                else:
                    k.v(lambda eng: eng.tensor_tensor(acc[:, 2:3], acc[:, 2:3], p4a[:, 0:1], OP.add),
                        reads=["acc2", "p4a"], writes=["acc2"], fence=True)
                k.dma("sync", s1fl[2, tj * 128 * 88:(tj + 1) * 128 * 88].rearrange("(p g) -> p g", g=88),
                      s3w[:, tj, :], reads=[f"s3w{tj}"], writes=["s1fl2"])
                yield
            for w in range(3):
                k.dma("sync", accd[w, :], acc[:, w:w + 1], reads=[f"acc{w}"], writes=["accd"], slow=True)
            k.dma("sync", p4a[0:1, 0:384], accd.rearrange("w p -> (w p)").rearrange("(one n) -> one n", one=1),
                  reads=["accd"], writes=["p4a"])
            k.v(lambda eng: eng.tensor_reduce(meansb[0:1, 0:3],
                                              p4a[0:1, 0:384].rearrange("p (w n) -> p w n", w=3),
                                              AX.X, OP.add),
                reads=["p4a"], writes=["meansb"])
            k.v(lambda eng: eng.tensor_scalar(meansb[0:1, 3:8], p4a[0:1, 0:5], 0.0, None, OP.mult),
                reads=["p4a"], writes=["meansb"])
            k.dma("sync", mean_in, meansb[0:1, :], reads=["meansb"], writes=["mean_in_d"])
            k.cc(lambda eng: eng.collective_compute(
                "AllReduce", OP.add, replica_groups=[list(range(NCORES))],
                ins=[mean_in.opt()], outs=[mean_out.opt()]),
                reads=["mean_in_d"], writes=["mean_out_d"])
            k.dma("sync", meansb[0:1, :], mean_out, reads=["mean_out_d"], writes=["meansb"])
            k.v(lambda eng: eng.tensor_scalar(p4s[0:1, 0:8], meansb[0:1, :], 1.0 / NSC, None, OP.mult),
                reads=["meansb"], writes=["p4s"])
            k.v(lambda eng: eng.tensor_copy(meansb[0:1, :], p4s[0:1, 0:8]), reads=["p4s"], writes=["meansb"],
                fence=True)
            k.dma("sync", meand, meansb[0:1, :], reads=["meansb"], writes=["meand"])
            yield
            for wi in range(3):
                k.dma("sync", sblk[:, :], s1fl[wi, 0:22528].rearrange("(b j) -> b j", j=256),
                      reads=[f"s1fl{wi}"], writes=["sblk"])
                k.dma("sync", meanc[:, :], meand[wi:wi + 1].unsqueeze(0).partition_broadcast(88),
                      reads=["meand"], writes=["meanc"])
                k.v(lambda eng: eng.tensor_scalar(scbf[:, 0, :], sblk[:, :], meanc[:, 0:1], None, OP.subtract),
                    reads=["sblk", "meanc"], writes=["scbf"])
                k.v(lambda eng: eng.tensor_reduce(sccol[:, 0:1], scbf[:, 0, :], AX.X, OP.max,
                                                  apply_absolute_value=True),
                    reads=["scbf"], writes=["sccol"], fence=True)
                k.v(lambda eng: eng.reciprocal(sccol[:, 3:4], sccol[:, 0:1]), reads=["sccol"], writes=["sccol"],
                    fence=True)
                k.v(lambda eng: eng.tensor_scalar(scbf[:, 1, 0:1], sccol[:, 3:4], 128.0, None, OP.mult),
                    reads=["sccol"], writes=["scbf"], fence=True)
                k.v(lambda eng: eng.tensor_copy(sccol[:, 1:2], scbf[:, 1, 0:1]), reads=["scbf"], writes=["sccol"],
                    fence=True)
                k.v(lambda eng: eng.reciprocal(sccol[:, 2:3], sccol[:, 1:2]), reads=["sccol"], writes=["sccol"],
                    fence=True)
                k.v(lambda eng: eng.tensor_scalar(scbf[:, 0, :], scbf[:, 0, :], sccol[:, 1:2], None, OP.mult),
                    reads=["scbf", "sccol"], writes=["scbf"], fence=True)
                k.v(lambda eng: eng.tensor_copy(scq[:, :], scbf[:, 0, :]), reads=["scbf"], writes=["scq"],
                    fence=True)
                k.v(lambda eng: eng.tensor_scalar(scq[:, :], scq[:, :], 1.5 * 2.0 ** 23, 1.5 * 2.0 ** 23,
                                                  OP.add, OP.subtract),
                    reads=["scq"], writes=["scq"], fence=True)
                k.v(lambda eng: eng.tensor_scalar(scq[:, :], scq[:, :], 127.0, -128.0, OP.min, OP.max),
                    reads=["scq"], writes=["scq"], fence=True)
                k.v(lambda eng: eng.tensor_scalar(scbf[:, 1, :], scq[:, :], sccol[:, 2:3], None, OP.mult),
                    reads=["scq", "sccol"], writes=["scbf"], fence=True)
                k.v(lambda eng: eng.tensor_scalar(scbf[:, 1, :], scbf[:, 1, :], meanc[:, 0:1], None, OP.add),
                    reads=["scbf", "meanc"], writes=["scbf"], fence=True)
                k.dma("sync", deqfl[wi, 0:22528].rearrange("(b j) -> b j", j=256), scbf[:, 1, :],
                      reads=["scbf"], writes=[f"deqfl{wi}"])
                if NBLOCKS == 3.15 and wi == 0:
                    k.barrier()
                    k.dma("sync", outT[100:188, 0:256], sblk[:, :], reads=["sblk"], writes=["outd"])
                    k.dma("gpsimd", outT[200:288, 0:256], scbf[:, 0, :], reads=["scbf"], writes=["outd"])
                    k.dma("sync", outT[300:388, 0:256], scq[:, :], reads=["scq"], writes=["outd"])
                    k.dma("sync", outT[400:488, 0:4], sccol[:, :], reads=["sccol"], writes=["outd"], slow=True)
                    k.dma("sync", outT[500:588, 0:1], meanc[:, :], reads=["meanc"], writes=["outd"], slow=True)
                    k.dma("gpsimd", outT[600:688, 0:256], scbf[:, 1, :], reads=["scbf"], writes=["outd"])
                    k.rec("sync", lambda eng: eng.nop(), reads=["outd"])
                    k.finalize()
                    P4TAP.append(nc)
                    return
                yield
            if NBLOCKS == 3.2:
                k.barrier()
                for w in range(3):
                    k.dma("sync", outT[11 * w:11 * w + 11, :].rearrange("a b -> (a b)")[0:22528],
                          s1fl[w, 0:22528], reads=[f"s1fl{w}"], writes=["outd"])
                    k.dma("gpsimd", outT[33 + 11 * w:44 + 11 * w, :].rearrange("a b -> (a b)")[0:22528],
                          deqfl[w, 0:22528], reads=[f"deqfl{w}"], writes=["outd"])
                k.rec("sync", lambda eng: eng.nop(), reads=["outd"])
                k.finalize()
                P4TAP.append(nc)
                return
            for wi, wap in [(0, w1s), (1, w2s)]:
                for tj in range(6):
                    rows = min(128, FS - tj * 128)
                    k.dma("sync", s1w[0:rows, tj, :],
                          s1fl[wi, tj * 128 * 32:(tj * 128 + rows) * 32].rearrange("(p g) -> p g", g=32),
                          reads=[f"s1fl{wi}"], writes=[f"s1w{tj}"])
                k.v(lambda eng: eng.tensor_scalar(s1w[64:128, 5, :], identb[64:128, 0:32], 0.0, 1.0,
                                                  OP.mult, OP.add),
                    reads=["identb"], writes=["s1w5"])
                k.v(lambda eng: eng.reciprocal(recn[:, :, :], s1w[:, :, :]),
                    reads=[f"s1w{t}" for t in range(6)], writes=["recn"], fence=True)
                for tj in range(6):
                    rows = min(128, FS - tj * 128)
                    k.dma("sync", deqn[0:rows, tj, :],
                          deqfl[wi, tj * 128 * 32:(tj * 128 + rows) * 32].rearrange("(p g) -> p g", g=32),
                          reads=[f"deqfl{wi}"], writes=["deqn"])
                yield
                for tj in range(6):
                    rows = min(128, FS - tj * 128)
                    k.dma("sync", wnat[0:rows, 0:2048], wap[tj * 128:tj * 128 + rows, :], writes=["wnat"])
                    k.v(lambda eng, tj=tj, rows=rows: eng.tensor_tensor(
                        scb2[0:rows, :].rearrange("p (g e) -> p g e", e=64),
                        wnat[0:rows, 0:2048].rearrange("p (g e) -> p g e", e=64),
                        recn[0:rows, tj, :].unsqueeze(2).broadcast_to((rows, 32, 64)), OP.mult),
                        reads=["wnat", "recn"], writes=["scb2"], fence=True)
                    staircase(scb2[0:rows, 0:2048], rows, 2048, "scb2")
                    k.v(lambda eng, tj=tj, rows=rows: eng.tensor_tensor(
                        scb2[0:rows, :].rearrange("p (g e) -> p g e", e=64),
                        stair[0:rows, 0:2048].rearrange("p (g e) -> p g e", e=64),
                        deqn[0:rows, tj, :].unsqueeze(2).broadcast_to((rows, 32, 64)), OP.mult),
                        reads=["stair", "deqn"], writes=["scb2"])
                    yield
                    for dj in range(16):
                        ws = dj % 2
                        k.v(lambda eng, dj=dj, rows=rows: eng.tensor_copy(
                            scb2f[0:rows, :], scb2[0:rows, dj * 128:(dj + 1) * 128]),
                            reads=["scb2"], writes=["scb2f"])
                        k.tp(pC[:, 0:rows], scb2f[0:rows, :],
                             identf[0:rows, 0:rows], reads=["scb2f", "identf"], writes=["pC"])
                        k.v(lambda eng, ws=ws, rows=rows: eng.tensor_copy(wqT[:, ws, 0:rows], pC[:, 0:rows]),
                            reads=["pC"], writes=[f"wqT{ws}"])
                        k.dma("sync",
                              ag_in[wi * SH12:(wi + 1) * SH12].rearrange("(d f) -> d f", f=FS)
                              [dj * 128:(dj + 1) * 128, tj * 128:tj * 128 + rows],
                              wqT[:, ws, 0:rows], reads=[f"wqT{ws}"], writes=["ag_in_d"])
                    yield
            k.v(lambda eng: eng.reciprocal(rec3[:, :, :], s3w[:, :, :]),
                reads=["s3w0", "s3w1"], writes=["rec3"], fence=True)
            for tj in range(2):
                k.dma("sync", deq3[:, tj, :],
                      deqfl[2, tj * 128 * 88:(tj + 1) * 128 * 88].rearrange("(p g) -> p g", g=88),
                      reads=["deqfl2"], writes=["deq3"])
            yield
            for tj in range(2):
                k.dma("sync", wnat[:, :], w3s[tj * 128:(tj + 1) * 128, :], writes=["wnat"])
                for cc3i in range(3):
                    c0 = cc3i * 2048
                    cw = min(2048, 5632 - c0)
                    g0 = c0 // 64
                    gw = cw // 64
                    k.v(lambda eng, tj=tj, c0=c0, cw=cw, g0=g0, gw=gw: eng.tensor_tensor(
                        scb2[:, 0:cw].rearrange("p (g e) -> p g e", e=64),
                        wnat[:, c0:c0 + cw].rearrange("p (g e) -> p g e", e=64),
                        rec3[:, tj, g0:g0 + gw].unsqueeze(2).broadcast_to((128, gw, 64)), OP.mult),
                        reads=["wnat", "rec3"], writes=["scb2"], fence=True)
                    staircase(scb2[0:128, 0:cw], 128, cw, "scb2")
                    k.v(lambda eng, tj=tj, c0=c0, cw=cw, g0=g0, gw=gw: eng.tensor_tensor(
                        scb2[:, 0:cw].rearrange("p (g e) -> p g e", e=64),
                        stair[:, 0:cw].rearrange("p (g e) -> p g e", e=64),
                        deq3[:, tj, g0:g0 + gw].unsqueeze(2).broadcast_to((128, gw, 64)), OP.mult),
                        reads=["stair", "deq3"], writes=["scb2"])
                    yield
                    for fj in range(cw // 128):
                        fg = (c0 // 128) + fj
                        ws = fg % 2
                        k.v(lambda eng, fj=fj: eng.tensor_copy(scb2f[:, :], scb2[:, fj * 128:(fj + 1) * 128]),
                            reads=["scb2"], writes=["scb2f"])
                        k.tp(pC[:, 0:128], scb2f[:, :],
                             identf[:, :], reads=["scb2f", "identf"], writes=["pC"])
                        k.v(lambda eng, ws=ws: eng.tensor_copy(wqT[:, ws, :], pC[:, 0:128]),
                            reads=["pC"], writes=[f"wqT{ws}"])
                        k.dma("sync",
                              ag_in[2 * SH12:].rearrange("(f d) -> f d", d=DS3)
                              [fg * 128:(fg + 1) * 128, tj * 128:(tj + 1) * 128],
                              wqT[:, ws, :], reads=[f"wqT{ws}"], writes=["ag_in_d"])
                    yield
            if NBLOCKS == 3.5:
                k.barrier()
                k.dma("gpsimd", outT[:, 0:704], ag_in[0:SH12].rearrange("(d f) -> d f", f=FS),
                      reads=["ag_in_d"], writes=["outd"])
                k.dma("gpsimd", outT[:, 704:1408], ag_in[SH12:2 * SH12].rearrange("(d f) -> d f", f=FS),
                      reads=["ag_in_d"], writes=["outd"])
                k.dma("gpsimd", outT[:, 1408:2048],
                      ag_in[2 * SH12:2 * SH12 + 2048 * 640].rearrange("(a b) -> a b", b=640),
                      reads=["ag_in_d"], writes=["outd"])
                k.rec("sync", lambda eng: eng.nop(), reads=["outd"])
                k.finalize()
                P4TAP.append(nc)
                return
            k.cc(lambda eng: eng.collective_compute(
                "AllGather", OP.bypass, replica_groups=[list(range(NCORES))],
                ins=[ag_in.opt()], outs=[ag_out.opt()]),
                reads=["ag_in_d"], writes=["ag_out_d"])
        _p4["it"] = gen()

    def drain_p4():
        it = _p4["it"]
        if it is None:
            return False
        try:
            next(it)
            return True
        except StopIteration:
            _p4["it"] = None
            return False

    P4_FREE = ("p4s", "p4a", "scb2f", "wqT", "stair", "scb2", "deq3", "rec3",
               "deqn", "recn", "meanc", "sccol", "scq", "scbf", "sblk", "acc",
               "meansb", "s3w", "s1w", "wnat")
    if NBLOCKS == 4:
        setup_p4()

    if NBLOCKS == 0:
        return early_tap()

    wbufA = k.alloc("wbufA", [128, 2, 16, 256], BF16)
    kvbf = k.alloc("kvbf", [128, 2048], BF16)
    wbufB = k.alloc("wbufB", [128, 2, 16, 256], BF16)
    bias_bc = k.alloc("bias_bc", [128, 2, 512], F32)
    kvsb = k.alloc("kvsb", [128, 2048], F32)

    WqkR = Wqk.rearrange("(ki p) c -> p ki c", p=128)
    Wv0R = Wv0.rearrange("(ki p) c -> p ki c", p=128)
    Wu0R = Wu0.rearrange("(ki p) c -> p ki c", p=128)
    Wo0R = Wo0.rearrange("(ki p) c -> p ki c", p=128)
    for hc in range(8):
        ws = hc % 2
        if NBLOCKS == 4:
            drain_p4()
        k.dma("gpsimd", wbufA[:, ws, :, :], WqkR[:, :, hc * 256:(hc + 1) * 256],
              writes=[f"wA{ws}"])
        k.dma("sync", wqk_scr[2 * hc, :, :, :], wbufA[:, ws, :, 0:128],
              reads=[f"wA{ws}"], writes=["wqk_scr"])
        k.dma("sync", wqk_scr[2 * hc + 1, :, :, :], wbufA[:, ws, :, 128:256],
              reads=[f"wA{ws}"], writes=["wqk_scr"])
        k.dma("gpsimd", wbufB[:, ws, :, :], Wv0R[:, :, hc * 256:(hc + 1) * 256],
              writes=[f"wB{ws}"])
        k.dma("sync", bias_bc[:, ws, 0:256],
              bqk[hc * 256:(hc + 1) * 256].unsqueeze(0).partition_broadcast(128), writes=[f"bbc{ws}"])
        k.dma("sync", bias_bc[:, ws, 256:512],
              bv0[hc * 256:(hc + 1) * 256].unsqueeze(0).partition_broadcast(128), writes=[f"bbc{ws}"])
        for tt in range(TTN):
            k.mm(pA[:, 0:256],
                 [(xT[:, ki, tt * 128:(tt + 1) * 128], wbufA[:, ws, ki, :]) for ki in range(16)],
                 reads=[f"xT{ki}" for ki in range(16)] + [f"wA{ws}"], writes=["pA"])
            k.mm(pB[:, 0:256],
                 [(xT[:, ki, tt * 128:(tt + 1) * 128], wbufB[:, ws, ki, :]) for ki in range(16)],
                 reads=[f"xT{ki}" for ki in range(16)] + [f"wB{ws}"], writes=["pB"])
            k.v(lambda eng, ws=ws: eng.tensor_tensor(tmpa[:, 0:256], pA[:, 0:256], bias_bc[:, ws, 0:256], OP.add),
                reads=["pA", f"bbc{ws}"], writes=["tmpa"])
            k.v(lambda eng: eng.tensor_scalar(tmpb[:, 0:256], tmpa[:, 0:256], 0.0, None, OP.min),
                reads=["tmpa"], writes=["tmpb"])
            k.act(tmpc[:, 0:256], tmpb[:, 0:256], AF.Exp, reads=["tmpb"], writes=["tmpc"])
            k.v(lambda eng: eng.tensor_scalar(tmpa[:, 0:256], tmpa[:, 0:256], 0.0, None, OP.max),
                reads=["tmpa"], writes=["tmpa"])
            k.v(lambda eng: eng.scalar_tensor_tensor(tbf[:, 0, 0:256], tmpc[:, 0:256], -1.0,
                                                     tmpa[:, 0:256], OP.add, OP.add),
                reads=["tmpa", "tmpc"], writes=["tbf0"])
            k.v(lambda eng, ws=ws: eng.tensor_tensor(tmpb[:, 0:256], pB[:, 0:256], bias_bc[:, ws, 256:512], OP.add),
                reads=["pB", f"bbc{ws}"], writes=["tmpb"])
            k.act(tbf[:, 1, 0:256], tmpb[:, 0:256], AF.Silu, reads=["tmpb"], writes=["tbf1"])
            for hh in range(2):
                h = hc * 2 + hh
                col = (h % 4) * 512 + (h // 4) * 128
                k.mm(pKV[:, col:col + 128],
                     [(tbf[:, 0, hh * 128:(hh + 1) * 128], tbf[:, 1, hh * 128:(hh + 1) * 128])],
                     reads=["tbf0", "tbf1"], writes=[f"pKV{h % 4}"],
                     start0=(tt == 0), stop=(tt == TTN - 1))
    for b4 in range(4):
        k.v(lambda eng, b4=b4: eng.tensor_copy(kvsb[:, b4 * 512:(b4 + 1) * 512], pKV[:, b4 * 512:(b4 + 1) * 512]),
            reads=[f"pKV{b4}"], writes=["kvsb"])
    k.dma("sync", kv_in, kvsb[:, :], reads=["kvsb"], writes=["kv_in_d"])
    if NBLOCKS == 0.5:
        return early_tap()
    k.cc(lambda eng: eng.collective_compute(
        "AllReduce", OP.add, replica_groups=[list(range(NCORES))],
        ins=[kv_in.opt()], outs=[kv_out.opt()]),
        reads=["kv_in_d"], writes=["kv_out_d"])
    k.dma("gpsimd", kvbf[:, :], kv_out, reads=["kv_out_d"], writes=["kvbf"])
    if NBLOCKS == 0.75:
        return early_tap()

    k.barrier()
    k.free("kvsb", "bias_bc", "wbufB")
    wb4 = wb4_view()
    ws4c = [0]

    def nws():
        ws4c[0] = (ws4c[0] + 1) % 4
        return ws4c[0]
    ych1 = k.alloc("ych1", [128, DT, 512], BF16)
    zbuf = k.alloc("zbuf", [128, 16, 512], BF16)
    ubuf = k.alloc("ubuf", [128, 512], BF16)

    for tc in range(TCN):
        for h in range(16):
            ws = nws()
            col = (h % 4) * 512 + (h // 4) * 128
            if NBLOCKS == 4:
                drain_p4()
            k.dma("sync" if ws % 2 == 0 else "gpsimd", wb4[:, ws, :, :],
                  wqk_scr[h, :, :, :], reads=["wqk_scr"], writes=[f"w4_{ws}"])
            k.mm(pA[:, :],
                 [(wb4[:, ws, ki, :], xT[:, ki, tc * 512:(tc + 1) * 512]) for ki in range(16)],
                 reads=[f"w4_{ws}"] + [f"xT{ki}" for ki in range(16)], writes=["pA"])
            k.v(lambda eng, h=h: eng.tensor_scalar(tmpa[:, :], pA[:, :],
                                                   cols[:, COL["bqk"], h:h + 1], None, OP.add),
                reads=["pA", "col_bqk"], writes=["tmpa"])
            k.v(lambda eng: eng.tensor_scalar(tmpb[:, :], tmpa[:, :], 0.0, None, OP.min),
                reads=["tmpa"], writes=["tmpb"])
            k.act(tmpc[:, :], tmpb[:, :], AF.Exp, reads=["tmpb"], writes=["tmpc"])
            k.v(lambda eng: eng.tensor_scalar(tmpa[:, :], tmpa[:, :], 0.0, None, OP.max),
                reads=["tmpa"], writes=["tmpa"])
            k.v(lambda eng: eng.scalar_tensor_tensor(tbf[:, 0, :], tmpc[:, :], -1.0,
                                                     tmpa[:, :], OP.add, OP.add),
                reads=["tmpa", "tmpc"], writes=["tbf0"])
            if NBLOCKS == 0.78:
                continue
            k.mm(pD[:, :], [(kvbf[:, col:col + 128], tbf[:, 0, :])],
                 reads=["kvbf", "tbf0"], writes=["pD"])
            if NBLOCKS == 0.79:
                continue
            k.v(lambda eng, h=h: eng.tensor_copy(ych1[:, h, :], pD[:, :]),
                reads=["pD"], writes=[f"y{h}"])
        if NBLOCKS in (0.78, 0.79, 0.8):
            return early_tap()
        bc, bkey = ln_stats(lambda dj: ych1[:, dj, :], lambda dj: [f"y{dj}"], bf=True)
        if NBLOCKS == 0.85:
            return early_tap()
        for dj in range(DT):
            ws = nws()
            if NBLOCKS == 4:
                drain_p4()
            if tc == 0:
                k.dma("gpsimd", wb4[:, ws, :, :],
                      Wu0R[:, :, dj * 128:(dj + 1) * 128], writes=[f"w4_{ws}"])
                k.dma("sync", wu0_scr[dj, :, :, :], wb4[:, ws, :, :],
                      reads=[f"w4_{ws}"], writes=["wu0_scr"])
            else:
                k.dma("sync" if ws % 2 == 0 else "gpsimd", wb4[:, ws, :, :],
                      wu0_scr[dj, :, :, :], reads=["wu0_scr"], writes=[f"w4_{ws}"])
            k.mm(pA[:, :],
                 [(wb4[:, ws, ki, :], xT[:, ki, tc * 512:(tc + 1) * 512]) for ki in range(16)],
                 reads=[f"w4_{ws}"] + [f"xT{ki}" for ki in range(16)], writes=["pA"])
            k.act(ubuf[:, :], pA[:, :], AF.Silu, bias=cols[:, COL["bu0"], dj:dj + 1],
                  reads=["pA", "col_bu0"], writes=["ubuf"])
            k.v(lambda eng, dj=dj, bc=bc: eng.tensor_tensor(tmpa[:, :], ych1[:, dj, :], bc[:, 0:512], OP.subtract),
                reads=[f"y{dj}", bkey], writes=["tmpa"])
            k.v(lambda eng, dj=dj, bc=bc: eng.scalar_tensor_tensor(
                tmpa[:, :], tmpa[:, :], cols[:, COL["ln_g0"], dj:dj + 1], bc[:, 512:1024], OP.mult, OP.mult),
                reads=["tmpa", bkey, "col_ln_g0"], writes=["tmpa"])
            k.v(lambda eng, dj=dj: eng.scalar_tensor_tensor(
                zbuf[:, dj, :], tmpa[:, :], cols[:, COL["ln_b0"], dj:dj + 1], ubuf[:, :], OP.add, OP.mult),
                reads=["tmpa", "ubuf", "col_ln_b0"], writes=[f"z{dj}"])
        for dj in range(DT):
            ws = nws()
            if NBLOCKS == 4:
                drain_p4()
            if tc == 0:
                k.dma("gpsimd", wb4[:, ws, :, :],
                      Wo0R[:, :, dj * 128:(dj + 1) * 128], writes=[f"w4_{ws}"])
                k.dma("sync", wo0_scr[dj, :, :, :], wb4[:, ws, :, :],
                      reads=[f"w4_{ws}"], writes=["wo0_scr"])
            else:
                k.dma("sync" if ws % 2 == 0 else "gpsimd", wb4[:, ws, :, :],
                      wo0_scr[dj, :, :, :], reads=["wo0_scr"], writes=[f"w4_{ws}"])
            k.mm(pB[:, :],
                 [(wb4[:, ws, ki, :], zbuf[:, ki, :]) for ki in range(16)],
                 reads=[f"w4_{ws}"] + [f"z{ki}" for ki in range(16)], writes=["pB"])
            k.v(lambda eng, dj=dj: eng.tensor_scalar(tmpb[:, :], pB[:, :],
                                                     cols[:, COL["bo0"], dj:dj + 1], None, OP.add),
                reads=["pB", "col_bo0"], writes=["tmpb"])
            k.v(lambda eng, dj=dj, tc=tc: eng.tensor_tensor(
                ych1[:, dj, :], tmpb[:, :], xT[:, dj, tc * 512:(tc + 1) * 512], OP.add),
                reads=["tmpb", f"xT{dj}"], writes=[f"y{dj}"])
        bc2, bk2 = ln_stats(lambda dj: ych1[:, dj, :], lambda dj: [f"y{dj}"], bf=True)
        for dj in range(DT):
            k.v(lambda eng, dj=dj, bc2=bc2: eng.tensor_tensor(tmpa[:, :], ych1[:, dj, :], bc2[:, 0:512], OP.subtract),
                reads=[f"y{dj}", bk2], writes=["tmpa"])
            k.v(lambda eng, dj=dj, tc=tc, bc2=bc2: eng.tensor_tensor(
                xT[:, dj, tc * 512:(tc + 1) * 512], tmpa[:, :], bc2[:, 512:1024], OP.mult),
                reads=["tmpa", bk2], writes=[f"xT{dj}"])

    def finish_tap():
        k.barrier()
        for dj in range(DT):
            k.dma("gpsimd", outT[dj * 128:(dj + 1) * 128, :], xT[:, dj, :],
                  reads=[f"xT{dj}"], writes=["outd"])
        k.rec("gpsimd", lambda eng: eng.nop(), reads=["outd"])
        k.finalize()
        return nc

    if NBLOCKS == 1:
        return finish_tap()

    if NBLOCKS == 4:
        while drain_p4():
            pass
    k.barrier(wait_cc=False)
    k.free("ubuf", "zbuf", "ych1", "kvbf")
    if NBLOCKS == 4:
        k.free("wbufA")
        k.free(*P4_FREE)
        wbufA = k.alloc("wbufA", [128, 2, 16, 256], BF16)
        wb4 = wb4_view()

    xcb = k.alloc("xcb", [128, 16, 512], BF16)
    qkz = k.alloc("qkz", [128, 16, 512], BF16)
    vb4 = k.alloc("vb4", [128, 16, 512], BF16)
    gb4 = k.alloc("gb4", [128, 16, 512], BF16)
    rwsb = k.alloc("rwsb", [128, 16, 2], BF16)
    dotsb = k.alloc("dotsb", [128, 4, 16], F32)
    logsb = k.alloc("logsb", [128, 4, 32], F32)
    csb = k.alloc("csb", [128, 4, 16], F32)
    cTsb = k.alloc("cTsb", [16, 512], F32)
    statro = k.alloc("statro", [16, 1024], F32)
    hbc = k.alloc("hbc", [128, 1024], F32)
    for h in range(16):
        k.dma("gpsimd", rwsb[:, h, :], routw[h, :, :], writes=["rwsb"])

    for tc in range(TCN):
        for j in range(DT):
            k.act(xcb[:, j, :], xT[:, j, tc * 512:(tc + 1) * 512], AF.Silu,
                  scale=cols[:, COL["cw1"], j:j + 1], reads=[f"xT{j}", "col_cw1"], writes=[f"xcb{j}"])
        for Wap, ntile, nm in [(Wq1, 8, "q"), (Wk1, 8, "k"), (Wv1, 16, "v"), (Wg1, 16, "g")]:
            WapR = Wap.rearrange("(ki p) c -> p ki c", p=128)
            for dj in range(ntile):
                ws = nws()
                if tc == 0:
                    k.dma("gpsimd", wb4[:, ws, :, :],
                          WapR[:, :, dj * 128:(dj + 1) * 128], writes=[f"w4_{ws}"])
                    k.dma("sync", ffn_scr[nm][dj, :, :, :], wb4[:, ws, :, :],
                          reads=[f"w4_{ws}"], writes=[f"scr_{nm}"])
                else:
                    k.dma("sync" if ws % 2 == 0 else "gpsimd", wb4[:, ws, :, :],
                          ffn_scr[nm][dj, :, :, :], reads=[f"scr_{nm}"], writes=[f"w4_{ws}"])
                k.mm(pA[:, :],
                     [(wb4[:, ws, ki, :], xcb[:, ki, :]) for ki in range(16)],
                     reads=[f"w4_{ws}"] + [f"xcb{ki}" for ki in range(16)], writes=["pA"])
                if nm == "q":
                    k.act(qkz[:, dj, :], pA[:, :], AF.Silu, reads=["pA"], writes=[f"z{dj}"])
                elif nm == "k":
                    k.v(lambda eng, dj=dj: eng.tensor_copy(qkz[:, 8 + dj, :], pA[:, :]),
                        reads=["pA"], writes=[f"z{8 + dj}"])
                elif nm == "v":
                    k.v(lambda eng, dj=dj: eng.tensor_copy(vb4[:, dj, :], pA[:, :]),
                        reads=["pA"], writes=[f"fv{dj}"])
                else:
                    k.act(gb4[:, dj, :], pA[:, :], AF.Silu, reads=["pA"], writes=[f"fg{dj}"])
        pCv = pC[:, :].rearrange("p (s c) -> p s c", s=4)
        pDv = pD[:, :].rearrange("p (s c) -> p s c", s=4)
        for j in range(8):
            k.v(lambda eng, j=j: eng.tensor_tensor(tbf[:, 0, :], qkz[:, j, :], qkz[:, 8 + j, :], OP.mult),
                reads=[f"z{j}", f"z{8 + j}"], writes=["tbf0"])
            for s4 in range(4):
                k.mm(pCv[:, s4, 2 * j:2 * j + 2],
                     [(tbf[:, 0, s4 * 128:(s4 + 1) * 128], g2p[:, :])],
                     reads=["tbf0", "g2p"], writes=["pC"])
        k.v(lambda eng: eng.tensor_copy(dotsb[:, :, :], pCv[:, :, 0:16]), reads=["pC"], writes=["dotsb"])
        for h in range(16):
            for s4 in range(4):
                k.mm(pDv[:, s4, 2 * h:2 * h + 2],
                     [(vb4[:, h, s4 * 128:(s4 + 1) * 128], rwsb[:, h, :])],
                     reads=[f"fv{h}", "rwsb"], writes=["pD"])
        k.v(lambda eng: eng.tensor_copy(logsb[:, :, :], pDv[:, :, 0:32]), reads=["pD"], writes=["logsb"])
        lgv = logsb[:, :, :].rearrange("p s (h two) -> p s h two", two=2)
        k.v(lambda eng: eng.tensor_tensor(csb[:, :, :], lgv[:, :, :, 1], lgv[:, :, :, 0], OP.subtract),
            reads=["logsb"], writes=["csb"])
        k.v(lambda eng: eng.tensor_scalar(tmpa[:, 0:64].rearrange("p (s h) -> p s h", s=4),
                                          csb[:, :, :], -1.0, None, OP.mult),
            reads=["csb"], writes=["tmpa"])
        k.v(lambda eng: eng.tensor_tensor(csb[:, :, :], csb[:, :, :],
                                          tmpa[:, 0:64].rearrange("p (s h) -> p s h", s=4), OP.max),
            reads=["csb", "tmpa"], writes=["csb"])
        k.act(csb[:, :, :], csb[:, :, :], AF.Sigmoid, reads=["csb"], writes=["csb"])
        k.v(lambda eng: eng.scalar_tensor_tensor(csb[:, :, :], csb[:, :, :], 0.125,
                                                 dotsb[:, :, :], OP.mult, OP.mult),
            reads=["csb", "dotsb"], writes=["csb"])
        for s4 in range(4):
            k.tp(pC[0:16, 0:128], csb[:, s4, :], identf[:, :], reads=["csb", "identf"], writes=["pC"])
            k.v(lambda eng, s4=s4: eng.tensor_copy(cTsb[:, s4 * 128:(s4 + 1) * 128], pC[0:16, 0:128]),
                reads=["pC"], writes=["cTsb"])
        for h in range(16):
            k.mm(pA[0:16, :], [(hsel[:, h, :], vb4[:, h, :])],
                 reads=["hsel", f"fv{h}"], writes=["pA"],
                 start0=(h == 0), stop=(h == 15))
            k.v(lambda eng, h=h: eng.tensor_tensor(tbf[:, 0, :], vb4[:, h, :], vb4[:, h, :], OP.mult),
                reads=[f"fv{h}"], writes=["tbf0"])
            k.mm(pB[0:16, :], [(hsel[:, h, :], tbf[:, 0, :])],
                 reads=["hsel", "tbf0"], writes=["pB"],
                 start0=(h == 0), stop=(h == 15))
        k.v(lambda eng: eng.tensor_scalar(statro[:, 0:512], pA[0:16, :], 1.0 / 128.0, None, OP.mult),
            reads=["pA"], writes=["statro"])
        k.v(lambda eng: eng.tensor_scalar(statro[:, 512:1024], pB[0:16, :], 1.0 / 128.0, None, OP.mult),
            reads=["pB"], writes=["statro"])
        k.v(lambda eng: eng.tensor_tensor(sq2[0:16, 0, :], statro[:, 0:512], statro[:, 0:512], OP.mult),
            reads=["statro"], writes=["sq0"])
        k.v(lambda eng: eng.scalar_tensor_tensor(sq2[0:16, 0, :], sq2[0:16, 0, :], -1.0,
                                                 statro[:, 512:1024], OP.mult, OP.add),
            reads=["sq0", "statro"], writes=["sq0"])
        k.v(lambda eng: eng.tensor_tensor(tmpa[0:16, :], cTsb[:, :], cTsb[:, :], OP.mult),
            reads=["cTsb"], writes=["tmpa"])
        k.v(lambda eng: eng.tensor_tensor(tmpa[0:16, :], tmpa[0:16, :], sq2[0:16, 0, :], OP.mult),
            reads=["tmpa", "sq0"], writes=["tmpa"])
        k.v(lambda eng: eng.tensor_scalar(tmpa[0:16, :], tmpa[0:16, :], 1e-5, None, OP.add),
            reads=["tmpa"], writes=["tmpa"])
        k.v(lambda eng: eng.reciprocal(tmpb[0:16, :], tmpa[0:16, :]), reads=["tmpa"], writes=["tmpb"])
        k.act(tmpc[0:16, :], tmpb[0:16, :], AF.Sqrt, reads=["tmpb"], writes=["tmpc"])
        k.v(lambda eng: eng.tensor_tensor(tmpb[0:16, :], tmpc[0:16, :], tmpc[0:16, :], OP.mult),
            reads=["tmpc"], writes=["tmpb"])
        k.v(lambda eng: eng.tensor_tensor(tmpb[0:16, :], tmpb[0:16, :], tmpa[0:16, :], OP.mult),
            reads=["tmpb", "tmpa"], writes=["tmpb"])
        k.v(lambda eng: eng.tensor_scalar(tmpb[0:16, :], tmpb[0:16, :], -0.5, 1.5, OP.mult, OP.add),
            reads=["tmpb"], writes=["tmpb"])
        k.v(lambda eng: eng.tensor_tensor(tmpc[0:16, :], tmpc[0:16, :], tmpb[0:16, :], OP.mult),
            reads=["tmpb", "tmpc"], writes=["tmpc"])
        k.v(lambda eng: eng.tensor_tensor(statro[:, 512:1024], cTsb[:, :], tmpc[0:16, :], OP.mult),
            reads=["cTsb", "tmpc"], writes=["statro"])
        k.dma("sync", hrowb, statro[:, :], reads=["statro"], writes=["hrb"])
        for h in range(16):
            k.dma("sync", hbc[:, :], hrowb[h:h + 1, :].partition_broadcast(128),
                  reads=["hrb"], writes=["hbc"])
            k.v(lambda eng, h=h: eng.tensor_tensor(tmpa[:, :], vb4[:, h, :], hbc[:, 0:512], OP.subtract),
                reads=[f"fv{h}", "hbc"], writes=["tmpa"])
            k.v(lambda eng, h=h: eng.tensor_tensor(tmpa[:, :], tmpa[:, :], hbc[:, 512:1024], OP.mult),
                reads=["tmpa", "hbc"], writes=["tmpa"])
            k.v(lambda eng, h=h: eng.tensor_tensor(qkz[:, h, :], tmpa[:, :], gb4[:, h, :], OP.mult),
                reads=["tmpa", f"fg{h}"], writes=[f"z{h}"])
        for dj in range(DT):
            ws = nws()
            if tc == 0:
                k.dma("gpsimd", wb4[:, ws, :, :],
                      Wout1.rearrange("(ki p) c -> p ki c", p=128)[:, :, dj * 128:(dj + 1) * 128],
                      writes=[f"w4_{ws}"])
                k.dma("sync", ffn_scr["o"][dj, :, :, :], wb4[:, ws, :, :],
                      reads=[f"w4_{ws}"], writes=["scr_o"])
            else:
                k.dma("sync" if ws % 2 == 0 else "gpsimd", wb4[:, ws, :, :],
                      ffn_scr["o"][dj, :, :, :], reads=["scr_o"], writes=[f"w4_{ws}"])
            k.mm(pB[:, :],
                 [(wb4[:, ws, ki, :], qkz[:, ki, :]) for ki in range(16)],
                 reads=[f"w4_{ws}"] + [f"z{ki}" for ki in range(16)], writes=["pB"])
            k.v(lambda eng, dj=dj, tc=tc: eng.tensor_tensor(
                vb4[:, dj, :], pB[:, :], xT[:, dj, tc * 512:(tc + 1) * 512], OP.add),
                reads=["pB", f"xT{dj}"], writes=[f"fv{dj}"])
        bc3, bk3 = ln_stats(lambda dj: vb4[:, dj, :], lambda dj: [f"fv{dj}"], bf=True)
        for dj in range(DT):
            k.v(lambda eng, dj=dj, bc3=bc3: eng.tensor_tensor(tmpa[:, :], vb4[:, dj, :], bc3[:, 0:512], OP.subtract),
                reads=[f"fv{dj}", bk3], writes=["tmpa"])
            k.v(lambda eng, dj=dj, tc=tc, bc3=bc3: eng.tensor_tensor(
                xT[:, dj, tc * 512:(tc + 1) * 512], tmpa[:, :], bc3[:, 512:1024], OP.mult),
                reads=["tmpa", bk3], writes=[f"xT{dj}"])

    if NBLOCKS == 2:
        return finish_tap()

    k.barrier()
    k.free("hbc", "statro", "cTsb", "csb", "logsb", "dotsb", "rwsb",
           "gb4", "vb4", "qkz", "xcb", "wbufA")

    xnc = k.alloc("xnc", [128, 16, 512], BF16)
    u2 = k.alloc("u2", [8, T], F32)
    win8 = k.alloc("win8", [128, 16, 8], BF16)
    dec8 = k.alloc("dec8", [8, 6], F32)
    k.dma("sync", dec8[0:8, 5:6], dec_l.rearrange("(r one) -> r one", one=1), writes=["dec_lg"])
    for ki in range(16):
        k.dma("gpsimd", win8[:, ki, :], W_in[ki * 128:(ki + 1) * 128, :], writes=["win8"])
    k.act(dec8[0:8, 0:1], dec8[0:8, 5:6], AF.Sigmoid, reads=["dec_lg"], writes=["dec8a"])
    k.act(dec8[0:8, 1:2], dec8[0:8, 0:1], AF.Ln, reads=["dec8a"], writes=["dec8b"], fence=True)
    k.v(lambda eng: eng.tensor_scalar(dec8[0:8, 2:3], dec8[0:8, 1:2], 2048.0, None, OP.mult),
        reads=["dec8b"], writes=["dec8c"])

    def emit_xn(tc):
        bc, bkey = ln_stats(lambda dj, tc=tc: xT[:, dj, tc * 512:(tc + 1) * 512],
                            lambda dj: [f"xT{dj}"], bf=True)
        for dj in range(DT):
            k.v(lambda eng, dj=dj, tc=tc, bc=bc: eng.tensor_tensor(
                tmpa[:, :], xT[:, dj, tc * 512:(tc + 1) * 512], bc[:, 0:512], OP.subtract),
                reads=[f"xT{dj}", bkey], writes=["tmpa"])
            k.v(lambda eng, dj=dj, bc=bc: eng.scalar_tensor_tensor(
                tmpa[:, :], tmpa[:, :], cols[:, COL["ln_g2"], dj:dj + 1], bc[:, 512:1024], OP.mult, OP.mult),
                reads=["tmpa", bkey, "col_ln_g2"], writes=["tmpa"])
            k.v(lambda eng, dj=dj: eng.tensor_scalar(
                xnc[:, dj, :], tmpa[:, :], cols[:, COL["ln_b2"], dj:dj + 1], None, OP.add),
                reads=["tmpa", "col_ln_b2"], writes=[f"xn{dj}"])

    for tc in range(TCN):
        emit_xn(tc)
        k.mm(pA[0:8, :],
             [(win8[:, ki, :], xnc[:, ki, :]) for ki in range(16)],
             reads=["win8"] + [f"xn{ki}" for ki in range(16)], writes=["pA"])
        k.v(lambda eng, tc=tc: eng.tensor_scalar(u2[:, tc * 512:(tc + 1) * 512], pA[0:8, :],
                                                 smalls[0:8, 0:1], None, OP.add),
            reads=["pA", "smalls0"], writes=["u2"])
    if NBLOCKS == 2.3:
        return finish_tap()
    hbf = k.alloc("hbf", [8, T], BF16)
    scanw = k.alloc("scanw", [8, T], F32)
    dectl = k.alloc("dectl", [8, T], F32)
    aux8 = k.alloc("aux8", [8, T], F32)
    eall = k.alloc("eall", [8, 8], F32)
    carr = k.alloc("carr", [8, 8], F32)
    k.dma("sync", carr[:, :], carry_m.unsqueeze(0).partition_broadcast(8), writes=["carr"])
    k.v(lambda eng: eng.tensor_scalar(aux8[:, :], u2[:, :], 0.0, 1.0, OP.mult, OP.add),
        reads=["u2"], writes=["aux8"], fence=True)
    k.v(lambda eng: eng.tensor_scalar(dectl[:, :], aux8[:, :], dec8[0:8, 0:1], None, OP.mult),
        reads=["aux8", "dec8a"], writes=["dectl"], fence=True)
    k.v(lambda eng: eng.tensor_tensor_scan(scanw[:, :], dectl[:, :], u2[:, :], 0.0, OP.mult, OP.add),
        reads=["dectl", "u2"], writes=["scanw"], fence=True)
    k.dma("sync", E_in, scanw[:, 2047:2048], reads=["scanw"], writes=["E_in_d"])
    k.cc(lambda eng: eng.collective_compute(
        "AllGather", OP.bypass, replica_groups=[list(range(NCORES))],
        ins=[E_in.opt()], outs=[E_out.opt()]),
        reads=["E_in_d"], writes=["E_out_d"])
    k.dma("sync", eall[:, :], E_out.rearrange("(c r) -> r c", r=8), reads=["E_out_d"],
          writes=["eall"], slow=True)
    k.act(carr[:, :], carr[:, :], AF.Exp, scale=dec8[0:8, 2:3], reads=["carr", "dec8c"], writes=["carr"])
    k.v(lambda eng: eng.tensor_tensor(carr[:, :], carr[:, :], eall[:, :], OP.mult),
        reads=["carr", "eall"], writes=["carr"])
    k.v(lambda eng: eng.reduce_sum(dec8[0:8, 3:4], carr[:, :], axis=AX.X),
        reads=["carr"], writes=["dec8d"], fence=True)
    k.v(lambda eng: eng.tensor_tensor_scan(dectl[:, :], aux8[:, :], aux8[:, :], 0.0, OP.mult, OP.add),
        reads=["aux8"], writes=["dectl"], fence=True)
    k.act(dectl[:, :], dectl[:, :], AF.Exp, scale=dec8[0:8, 1:2], reads=["dectl", "dec8b"], writes=["dectl"])
    k.v(lambda eng: eng.scalar_tensor_tensor(scanw[:, :], dectl[:, :], dec8[0:8, 3:4], scanw[:, :],
                                             OP.mult, OP.add),
        reads=["dectl", "dec8d", "scanw"], writes=["scanw"], fence=True)
    k.v(lambda eng: eng.tensor_copy(hbf[:, :], scanw[:, :]), reads=["scanw"], writes=["hbf"], fence=True)

    if NBLOCKS == 2.5:
        return finish_tap()

    k.barrier()
    k.free("carr", "eall", "aux8", "dectl", "scanw")
    ych3 = k.alloc("ychunkB", [128, DT, 512], F32)
    hf2 = k.alloc("hf2", [32, T], BF16)
    wsm = k.alloc("wsm", [32, 2, D], BF16)
    wdn = k.alloc("wdn", [128, 16, 32], BF16)
    k.dma("gpsimd", wsm[0:8, 0, :], W_out2, writes=["wsm0"])
    k.dma("gpsimd", wsm[0:32, 1, :], W_up, writes=["wsm1"])
    for ki in range(16):
        k.dma("gpsimd", wdn[:, ki, :], W_down[ki * 128:(ki + 1) * 128, :], writes=["wdn"])

    for tc in range(TCN):
        emit_xn(tc)
        for dj in range(DT):
            k.mm(pA[:, :], [(wsm[0:8, 0, dj * 128:(dj + 1) * 128], hbf[:, tc * 512:(tc + 1) * 512])],
                 reads=["wsm0", "hbf"], writes=["pA"])
            k.v(lambda eng, dj=dj: eng.tensor_scalar(tmpb[:, :], pA[:, :],
                                                     cols[:, COL["b_out2"], dj:dj + 1], None, OP.add),
                reads=["pA", "col_b_out2"], writes=["tmpb"])
            k.v(lambda eng, dj=dj: eng.scalar_tensor_tensor(
                tmpb[:, :], xnc[:, dj, :], cols[:, COL["D_skip"], dj:dj + 1], tmpb[:, :], OP.mult, OP.add),
                reads=[f"xn{dj}", "col_D_skip", "tmpb"], writes=["tmpb"])
            k.v(lambda eng, dj=dj, tc=tc: eng.tensor_tensor(
                ych3[:, dj, :], tmpb[:, :], xT[:, dj, tc * 512:(tc + 1) * 512], OP.add),
                reads=["tmpb", f"xT{dj}"], writes=[f"y{dj}"])
        if NBLOCKS == 2.6:
            return finish_tap()
        bc, bkey = ln_stats(lambda dj: ych3[:, dj, :], lambda dj: [f"y{dj}"])
        for dj in range(DT):
            k.v(lambda eng, dj=dj, bc=bc: eng.tensor_tensor(tmpa[:, :], ych3[:, dj, :], bc[:, 0:512], OP.subtract),
                reads=[f"y{dj}", bkey], writes=["tmpa"])
            k.v(lambda eng, dj=dj, bc=bc: eng.scalar_tensor_tensor(
                tmpa[:, :], tmpa[:, :], cols[:, COL["ln_g2"], dj:dj + 1], bc[:, 512:1024], OP.mult, OP.mult),
                reads=["tmpa", bkey, "col_ln_g2"], writes=["tmpa"])
            k.v(lambda eng, dj=dj: eng.tensor_scalar(
                xnc[:, dj, :], tmpa[:, :], cols[:, COL["ln_b2"], dj:dj + 1], None, OP.add),
                reads=["tmpa", "col_ln_b2"], writes=[f"xn{dj}"])
        if NBLOCKS == 2.65:
            return finish_tap()
        k.mm(pB[0:32, :],
             [(wdn[:, ki, :], xnc[:, ki, :]) for ki in range(16)],
             reads=["wdn"] + [f"xn{ki}" for ki in range(16)], writes=["pB"])
        k.act(hf2[:, tc * 512:(tc + 1) * 512], pB[0:32, :], AF.Relu, bias=smalls[0:32, 1:2],
              reads=["pB", "smalls1"], writes=["hf2"])
        for dj in range(DT):
            k.mm(pC[:, :], [(wsm[0:32, 1, dj * 128:(dj + 1) * 128], hf2[:, tc * 512:(tc + 1) * 512])],
                 reads=["wsm1", "hf2"], writes=["pC"])
            k.v(lambda eng, dj=dj: eng.tensor_scalar(tmpb[:, :], pC[:, :],
                                                     cols[:, COL["b_up"], dj:dj + 1], None, OP.add),
                reads=["pC", "col_b_up"], writes=["tmpb"])
            k.v(lambda eng, dj=dj: eng.tensor_tensor(tmpb[:, :], tmpb[:, :], ych3[:, dj, :], OP.add),
                reads=["tmpb", f"y{dj}"], writes=["tmpb"])
            k.v(lambda eng, dj=dj, tc=tc: eng.tensor_tensor(
                ych3[:, dj, :], tmpb[:, :], xT[:, dj, tc * 512:(tc + 1) * 512], OP.add),
                reads=["tmpb", f"xT{dj}"], writes=[f"y{dj}"])
        if NBLOCKS == 2.7:
            return finish_tap()
        bc2, bk2 = ln_stats(lambda dj: ych3[:, dj, :], lambda dj: [f"y{dj}"])
        for dj in range(DT):
            k.v(lambda eng, dj=dj, bc2=bc2: eng.tensor_tensor(tmpa[:, :], ych3[:, dj, :], bc2[:, 0:512], OP.subtract),
                reads=[f"y{dj}", bk2], writes=["tmpa"])
            k.v(lambda eng, dj=dj, tc=tc, bc2=bc2: eng.tensor_tensor(
                xT[:, dj, tc * 512:(tc + 1) * 512], tmpa[:, :], bc2[:, 512:1024], OP.mult),
                reads=["tmpa", bk2], writes=[f"xT{dj}"])
        if NBLOCKS == 2.75:
            return finish_tap()

    if NBLOCKS == 3:
        return finish_tap()

    k.barrier()
    k.free("wdn", "wsm", "hf2", "ychunkB", "hbf", "dec8", "win8", "u2", "xnc")

    if NBLOCKS != 4:
        setup_p4()
        while drain_p4():
            pass
        if P4TAP:
            return P4TAP[0]
        k.barrier()
        k.free(*P4_FREE)

    hch = k.alloc("hch", [128, 22, 512], BF16)
    wbufA = k.alloc("wbufA_f", [128, 2, 16, 128], BF16)
    wbufB = k.alloc("wbufB_f", [128, 2, 16, 128], BF16)
    w3st = k.alloc("w3st", [128, 2, 22, 128], BF16)
    ych4 = k.alloc("ychunkC", [128, DT, 512], F32)

    def w12_stripe(wi, fi, ws, wbuf, wkey):
        f0g = fi * 128
        segs = []
        f = f0g
        while f < f0g + 128:
            cc2 = f // FS
            take = min((cc2 + 1) * FS - f, f0g + 128 - f)
            segs.append((cc2, f - cc2 * FS, f - f0g, take))
            f += take
        for (cc2, floc, foff, take) in segs:
            base = cc2 * AGS + wi * SH12
            k.dma("sync" if wi == 0 else "gpsimd", wbuf[:, ws, :, foff:foff + take],
                  ag_out[base:base + SH12].rearrange("(ki p f) -> p ki f", p=128, f=FS)
                  [:, :, floc:floc + take],
                  reads=["ag_out_d"], writes=[wkey])

    for tc in range(TCN):
        for half in range(2):
            for fo in range(22):
                fi = half * 22 + fo
                ws = fi % 2
                w12_stripe(0, fi, ws, wbufA, f"wSa{ws}")
                w12_stripe(1, fi, ws, wbufB, f"wSb{ws}")
                k.mm(pA[:, :],
                     [(wbufA[:, ws, ki, :], xT[:, ki, tc * 512:(tc + 1) * 512]) for ki in range(16)],
                     reads=[f"wSa{ws}"] + [f"xT{ki}" for ki in range(16)], writes=["pA"])
                k.mm(pB[:, :],
                     [(wbufB[:, ws, ki, :], xT[:, ki, tc * 512:(tc + 1) * 512]) for ki in range(16)],
                     reads=[f"wSb{ws}"] + [f"xT{ki}" for ki in range(16)], writes=["pB"])
                k.act(tmpa[:, :], pA[:, :], AF.Silu, reads=["pA"], writes=["tmpa"])
                k.v(lambda eng, fo=fo: eng.tensor_tensor(hch[:, fo, :], tmpa[:, :], pB[:, :], OP.mult),
                    reads=["tmpa", "pB"], writes=[f"h{fo}"])
            for dj in range(DT):
                cc3 = dj // 2
                dloc = (dj % 2) * 128
                ws = dj % 2
                base = cc3 * AGS + 2 * SH12
                k.dma("sync" if ws == 0 else "gpsimd", w3st[:, ws, :, :],
                      ag_out[base:base + SH3].rearrange("(fi p d) -> p fi d", p=128, d=DS3)
                      [:, half * 22:half * 22 + 22, dloc:dloc + 128],
                      reads=["ag_out_d"], writes=[f"w3st{ws}"])
                k.mm(pC[:, :],
                     [(w3st[:, ws, fo, :], hch[:, fo, :]) for fo in range(22)],
                     reads=[f"w3st{ws}"] + [f"h{fo}" for fo in range(22)], writes=["pC"])
                if half == 0:
                    k.v(lambda eng, dj=dj: eng.tensor_copy(ych4[:, dj, :], pC[:, :]),
                        reads=["pC"], writes=[f"y{dj}"])
                else:
                    k.v(lambda eng, dj=dj, tc=tc: eng.scalar_tensor_tensor(
                        tmpb[:, :], ych4[:, dj, :], 1.0, pC[:, :], OP.mult, OP.add),
                        reads=["pC", f"y{dj}"], writes=["tmpb"])
                    k.v(lambda eng, dj=dj, tc=tc: eng.tensor_tensor(
                        ych4[:, dj, :], tmpb[:, :], xT[:, dj, tc * 512:(tc + 1) * 512], OP.add),
                        reads=["tmpb", f"xT{dj}"], writes=[f"y{dj}"])
        bc, bkey = ln_stats(lambda dj: ych4[:, dj, :], lambda dj: [f"y{dj}"])
        for dj in range(DT):
            k.v(lambda eng, dj=dj, bc=bc: eng.tensor_tensor(tmpa[:, :], ych4[:, dj, :], bc[:, 0:512], OP.subtract),
                reads=[f"y{dj}", bkey], writes=["tmpa"])
            k.v(lambda eng, dj=dj, bc=bc: eng.tensor_tensor(tmpc[:, :], tmpa[:, :], bc[:, 512:1024], OP.mult),
                reads=["tmpa", bkey], writes=["tmpc"])
            k.dma("sync", outT[dj * 128:(dj + 1) * 128, tc * 512:(tc + 1) * 512], tmpc[:, :],
                  reads=["tmpc"], writes=["outd"])
    k.rec("sync", lambda eng: eng.nop(), reads=["outd"])
    k.finalize()
    return nc


_CACHE = {}


def kernel(L0_attn_x, L0_attn_Wqk, L0_attn_bqk, L0_attn_Wv, L0_attn_bv,
           L0_attn_Wu, L0_attn_bu, L0_attn_Wo, L0_attn_bo, L0_attn_ln_g,
           L0_attn_ln_b, L0_ffn_conv_w, L0_ffn_Wq, L0_ffn_Wk, L0_ffn_Wv,
           L0_ffn_Wg, L0_ffn_router_w, L0_ffn_Wout, L1_attn_ln_gamma,
           L1_attn_ln_beta, L1_attn_W_in, L1_attn_b_in, L1_attn_decay_logit,
           L1_attn_W_out, L1_attn_b_out, L1_attn_D_skip, L1_attn_W_down,
           L1_attn_b_down, L1_attn_W_up, L1_attn_b_up, L1_ffn_weight1,
           L1_ffn_weight2, L1_ffn_weight3):
    bf = ml_dtypes.bfloat16
    f32 = np.float32
    if "nc" not in _CACHE:
        _CACHE["nc"] = build_program()
    nc = _CACHE["nc"]

    x = np.asarray(L0_attn_x, dtype=f32).reshape(16384, 2048)
    w1 = np.asarray(L1_ffn_weight1).astype(bf, copy=False)
    w2 = np.asarray(L1_ffn_weight2).astype(bf, copy=False)
    w3 = np.asarray(L1_ffn_weight3).astype(bf, copy=False)

    shared = {
        "Wqk": np.asarray(L0_attn_Wqk, f32), "bqk": np.asarray(L0_attn_bqk, f32),
        "Wv0": np.asarray(L0_attn_Wv, f32), "bv0": np.asarray(L0_attn_bv, f32),
        "Wu0": np.asarray(L0_attn_Wu, f32), "bu0": np.asarray(L0_attn_bu, f32),
        "Wo0": np.asarray(L0_attn_Wo, f32), "bo0": np.asarray(L0_attn_bo, f32),
        "ln_g0": np.asarray(L0_attn_ln_g, f32), "ln_b0": np.asarray(L0_attn_ln_b, f32),
        "convw": np.asarray(L0_ffn_conv_w, f32),
        "Wq1": np.asarray(L0_ffn_Wq, f32), "Wk1": np.asarray(L0_ffn_Wk, f32),
        "Wv1": np.asarray(L0_ffn_Wv, f32), "Wg1": np.asarray(L0_ffn_Wg, f32),
        "routw": np.asarray(L0_ffn_router_w, f32),
        "Wout1": np.asarray(L0_ffn_Wout, f32),
        "ln_g2": np.asarray(L1_attn_ln_gamma, f32), "ln_b2": np.asarray(L1_attn_ln_beta, f32),
        "W_in": np.asarray(L1_attn_W_in, f32), "b_in": np.asarray(L1_attn_b_in, f32),
        "dec_l": np.asarray(L1_attn_decay_logit, f32),
        "W_out2": np.asarray(L1_attn_W_out, f32), "b_out2": np.asarray(L1_attn_b_out, f32),
        "D_skip": np.asarray(L1_attn_D_skip, f32),
        "W_down": np.asarray(L1_attn_W_down, f32), "b_down": np.asarray(L1_attn_b_down, f32),
        "W_up": np.asarray(L1_attn_W_up, f32), "b_up": np.asarray(L1_attn_b_up, f32),
        "ident_bf": np.eye(128, dtype=bf),
        "ident_f": np.eye(128, dtype=f32),
        "ones_in": np.ones(128, f32),
        "ones_in_bf": np.ones(128, bf),
        "g2pat": np.concatenate([np.repeat(np.float32([1, 0])[None], 64, 0),
                                 np.repeat(np.float32([0, 1])[None], 64, 0)], 0).astype(bf),
        "hselin": np.broadcast_to(np.eye(16, dtype=bf)[None], (128, 16, 16)).copy(),
    }
    in_maps = []
    for c in range(NCORES):
        m = dict(shared)
        m["xT_in"] = np.ascontiguousarray(x[c * 2048:(c + 1) * 2048, :].T)
        m["w1s"] = np.ascontiguousarray(w1[c * FS:(c + 1) * FS, :])
        m["w2s"] = np.ascontiguousarray(w2[c * FS:(c + 1) * FS, :])
        m["w3s"] = np.ascontiguousarray(w3[c * DS3:(c + 1) * DS3, :])
        cm = np.full(8, 1e9, f32)
        for cc in range(c):
            cm[cc] = float(c - 1 - cc)
        m["carry_m"] = cm
        in_maps.append(m)
    res = run_bass_kernel_spmd(nc, in_maps, core_ids=list(range(NCORES)))
    out = np.empty((16384, 2048), f32)
    for c in range(NCORES):
        out[c * 2048:(c + 1) * 2048, :] = res.results[c]["outT"].T
    return out.reshape(1, 16384, 2048)
